# version 15
# baseline (speedup 1.0000x reference)
import math
from contextlib import ExitStack

import numpy as np
import ml_dtypes

import concourse.bass as bass
import concourse.mybir as mybir
from concourse.bass import ds
from concourse.bass_utils import run_bass_kernel_spmd
from concourse.bacc import Bacc

F32 = mybir.dt.float32
BF16 = mybir.dt.bfloat16
I32 = mybir.dt.int32
AF = mybir.ActivationFunctionType
ALU = mybir.AluOpType
AX = mybir.AxisListType

B, T, D = 4, 4096, 1024
TK = 2048
HALO = 256
TKH = TK + HALO
CA = 10.0
CB = 80.0
EPS = 1e-5
WIN = 256

DEBUG_TAPS = []
PHASES = 99
SUBPH = 99
LAST_RESULTS = None

BF = ml_dtypes.bfloat16


def _bucket(rel):
    n = -np.asarray(rel)
    nb = 16
    ret = (n < 0).astype(np.int64) * nb
    n = np.abs(n)
    mx = nb // 2
    small = n < mx
    with np.errstate(divide="ignore"):
        large = mx + (np.log(np.maximum(n, 1).astype(np.float64) / mx)
                      / math.log(128.0 / mx) * (nb - mx)).astype(np.int64)
    return ret + np.where(small, n, np.minimum(large, nb - 1))


_CCACHE = {}


def _host_consts(half):
    if half in _CCACHE:
        return _CCACHE[half]
    t0 = half * TK
    o = {}
    inv_freq = 1.0 / 10000.0 ** (np.arange(0, 64, 2, dtype=np.float64) / 64)
    tpos = np.arange(t0 - HALO, t0 + TK, dtype=np.float64)
    fr = np.outer(inv_freq, tpos)
    c64 = np.tile(np.cos(fr), (8, 1))
    s64 = np.tile(np.sin(fr), (8, 1))
    o["cosT"] = c64.astype(np.float32)
    o["cosTl"] = (c64 - o["cosT"].astype(np.float64)).astype(np.float32)
    o["sinT"] = s64.astype(np.float32)
    o["sinTl"] = (s64 - o["sinT"].astype(np.float64)).astype(np.float32)
    masks = np.zeros((8, 128, 512), np.float32)
    for j in range(6):
        dd = (HALO - j * 128) + np.arange(512)[None, :] - np.arange(128)[:, None]
        masks[j] = ((dd >= 0) & (dd <= WIN - 1)).astype(np.float32)
    for sl, j in ((6, 0), (7, 1)):
        kg = t0 - HALO + j * 128 + np.arange(128)[:, None]
        dd = t0 + np.arange(512)[None, :] - kg
        masks[sl] = ((dd >= 0) & (dd <= WIN - 1) & (kg >= 0)).astype(np.float32)
    o["amask"] = masks.reshape(8 * 128, 512).astype(BF)
    wbase = (t0 - 92) // 128
    indw = np.zeros((32, 20), np.float32)
    for w in range(19):
        gch = wbase + w
        if 0 <= gch < 32:
            indw[:gch, w] = 1.0
    indw[:, 19] = 1.0
    o["indw"] = indw.astype(BF)
    o["indwB"] = np.ascontiguousarray(
        np.concatenate([np.repeat(indw[:, w:w + 1], 128, 1) for w in range(19)], 1)
    ).astype(np.float32)
    ohsel = np.zeros((128, 32 * 32), np.float32)
    for gg in range(32):
        ohsel[:, gg * 32 + gg] = 1.0
    o["ohsel"] = ohsel.astype(BF)
    o["wbase"] = wbase
    e2 = np.zeros((32, TK), np.float32)
    dd = t0 + np.arange(TK) - 4096
    bk = _bucket(dd)
    for n in np.nonzero(dd > -91)[0]:
        e2[bk[n], n] += 1.0
        e2[15, n] -= 1.0
    o["e2"] = e2.astype(BF)
    edg = np.zeros((32, 185), np.float32)
    for di in range(185):
        d = di - 92
        edg[_bucket(d), di] += 1.0
        edg[_bucket(d - 1), di] -= 1.0
    o["edg"] = edg.astype(BF)
    o["trirev"] = (np.arange(128)[:, None] <= (127 - np.arange(128))[None, :]).astype(BF)
    e2n = np.zeros((32, 512), np.float32)
    dn = np.arange(512) - 512
    bkn = _bucket(dn)
    far = int(_bucket(np.asarray([-3000]))[0])
    for n in range(512):
        if bkn[n] != far:
            e2n[bkn[n], n] += 1.0
            e2n[far, n] -= 1.0
    o["e2n"] = e2n.astype(BF)
    fi = np.zeros((32, 1), np.float32)
    fi[far, 0] = 1.0
    o["farind"] = fi.astype(BF)
    iw = np.zeros((32, 32 * 128), np.float32)
    for cc in range(32):
        iw[:cc, cc * 128:(cc + 1) * 128] = 1.0
    o["indwb"] = iw.astype(BF)
    cf = np.zeros((512, 8 * 128), np.float32)
    for pr in range(8):
        for m in range(128):
            cf[15 * 16 + 2 * pr + (m >= 64), pr * 128 + m] = 1.0
    o["cfsel"] = cf.astype(BF)
    _CCACHE[half] = o
    return o


def _perm_wqkv(wqkv):
    cols = []
    for part in (0, 1):
        base = part * 512
        lo, hi = [], []
        for h in range(8):
            lo.extend(range(base + h * 64, base + h * 64 + 32))
            hi.extend(range(base + h * 64 + 32, base + h * 64 + 64))
        cols.extend(lo + hi)
    cols.extend(range(1024, 1536))
    return np.ascontiguousarray(wqkv[:, cols])


class P:
    def __init__(self, nc, ctx, blk):
        self.nc, self.ctx, self.blk = nc, ctx, blk
        self.sem = {}
        self.cnt = {}
        for s in ("SD", "SG", "SA", "ST", "SV", "SS", "CC", "GC"):
            self.sem[s] = ctx.enter_context(nc.semaphore(s))
            self.cnt[s] = 0
        self.taps = []

    def sb(self, name, shape, dt=F32, ctx=None):
        return (ctx or self.ctx).enter_context(self.nc.sbuf_tensor(name, shape, dt))

    def psum(self, name, shape, ctx=None):
        return (ctx or self.ctx).enter_context(self.nc.psum_tensor(name, shape, F32))

    def step(self, engine, waits=(), after=()):
        sec = getattr(self.blk, engine)
        pp = self

        def deco(body):
            wv = [(s, pp.cnt[s]) for s in waits] + list(after)

            def run(eng):
                for s, v in wv:
                    if v > 0:
                        eng.wait_ge(pp.sem[s], v)
                body(eng)
            sec(run)
        return deco

    def fin(self, inst, sig, n=1):
        inst.then_inc(self.sem[sig], n)
        self.cnt[sig] += n

    def dma(self, eng, out, in_, sig, **kw):
        self.fin(eng.dma_start(out=out, in_=in_, **kw), sig, 16)

    def tap(self, name, tensor, shape, dt):
        if name in DEBUG_TAPS:
            if tensor is None:
                self.taps.append((name, tensor, shape, dt))
                return
            h = self.nc.dram_tensor("tap_" + name, list(shape), dt,
                                    kind="ExternalOutput")
            pp = self

            @self.step("sync", waits=("SV", "SS", "ST", "GC"))
            def _(s):
                pp.dma(s, h.ap(), tensor[0:shape[0], 0:shape[1]], "SD")

    def barrier(self):
        allsems = ("SD", "SG", "SA", "ST", "SV", "SS", "GC")
        for e in ("sync", "gpsimd", "scalar", "tensor", "vector"):
            @self.step(e, waits=allsems)
            def _(eng):
                pass


def build_nc():
    nc = Bacc("TRN2", target_bir_lowering=False, detect_race_conditions=False)
    g = {}

    def din(name, shape, dt=F32):
        g[name] = nc.dram_tensor(name, shape, dt, kind="ExternalInput")

    din("xf", [D, TKH])
    din("wqkv", [D, 1536])
    din("wout", [512, D])
    for n in ("wqT", "wkT", "wvT", "woT", "w1T", "w2T", "frhs"):
        din(n, [D, D])
    din("ball", [128, 48])
    din("relb", [32, 16])
    din("cosT", [256, TKH])
    din("cosTl", [256, TKH])
    din("sinT", [256, TKH])
    din("sinTl", [256, TKH])
    din("amask", [8 * 128, 512], BF16)
    din("edg", [32, 185], BF16)
    din("trirev", [128, 128], BF16)
    din("trirevf", [128, 128])
    din("ident", [128, 128])
    din("indwb", [32, 32 * 128], BF16)
    din("ohsel", [128, 32 * 32], BF16)
    din("e2n", [32, 512], BF16)
    din("farind", [32, 1], BF16)
    din("relbh", [32, 8])
    din("bvrow", [1, D])
    g["y"] = nc.dram_tensor("y", [D, TK], F32, kind="ExternalOutput")
    g["strmh"] = nc.dram_tensor("strmh", [D, TK], F32)
    g["h3h"] = nc.dram_tensor("h3h", [D, TK], BF16)
    g["qth"] = nc.dram_tensor("qth", [D, TK], BF16)
    g["atth"] = nc.dram_tensor("atth", [D, TK], BF16)
    g["growh"] = nc.dram_tensor("growh", [8, 1280], BF16)
    g["kth"] = nc.dram_tensor("kth", [D, TK], BF16)
    g["vth"] = nc.dram_tensor("vth", [TK, D], BF16)
    g["vtf"] = nc.dram_tensor("vtf", [TK, D], F32)
    g["whrh"] = nc.dram_tensor("whrh", [8, 512], BF16)
    g["rowh"] = nc.dram_tensor("rowh", [16, TK], F32)
    g["s1h"] = nc.dram_tensor("s1h", [D, TK], F32)

    with ExitStack() as ctx:
        blk = ctx.enter_context(nc.Block())
        p = P(nc, ctx, blk)
        _emit(p, g, ctx)
    return nc


def _emit(p, g, ctx):
    G = {}
    G["bsb"] = p.sb("bsb", [128, 48])
    G["onesb"] = p.sb("onesb", [128, 1], BF16)
    G["onesf"] = p.sb("onesf", [1, 128])
    G["onesm"] = p.sb("onesm", [128, 64])
    G["csh"] = p.sb("csh", [128, 4])
    G["relbb"] = p.sb("relbb", [32, 16], BF16)
    G["relbhb"] = p.sb("relbhb", [32, 8], BF16)
    G["stg"] = p.sb("stg", [128, 2 * TKH])
    G["xtm"] = p.sb("xtm", [128, 8 * D], BF16)
    G["stt"] = p.sb("stt", [128, 128])
    G["stt2"] = p.sb("stt2", [128, 40])
    G["rowb"] = p.sb("rowb", [128, 128])
    G["rowf"] = p.sb("rowf", [1, TKH])
    G["lssav"] = p.sb("lssav", [18, 128])
    G["ident"] = p.sb("identsb", [128, 128])
    G["reps"] = p.sb("reps", [128, 2 * TKH])
    G["scr"] = p.sb("scr", [128, TKH])
    G["xh"] = p.sb("xh", [128, 8 * TKH], BF16)
    G["bps"] = p.psum("bps", [128, 512])

    @p.step("gpsimd")
    def _(gp):
        gp.memset(G["onesb"][:], 1.0)
        gp.memset(G["onesf"][:], 1.0)
        gp.memset(G["onesm"][:], 1.0)
        gp.memset(G["csh"][:, 0:1], -CA)
        gp.memset(G["csh"][:, 2:3], EPS)
        gp.memset(G["csh"][:, 3:4], 0.0)
        gp.memset(G["stt"][:], 0.0)
        gp.memset(G["stt2"][:], 0.0)
        gp.memset(G["rowb"][:], 0.0)
        gp.memset(G["rowf"][:], 0.0)
        p.fin(gp.memset(G["csh"][:, 1:2], -CB), "GC")
        p.dma(gp, G["relbb"][:], g["relb"].ap(), "SG")
        p.dma(gp, G["relbhb"][:], g["relbh"].ap(), "SG")

    @p.step("sync")
    def _(s):
        p.dma(s, G["bsb"][:], g["ball"].ap(), "SD")
        p.dma(s, G["ident"][:], g["ident"].ap(), "SD")

    _norm(p, g, G, src=g["xf"], ncols=TKH, precast=False)
    p.tap("h1", G["xh"], (128, 8 * TKH), BF16)
    p.tap("dbg_stt", G["stt"], (128, 128), F32)
    p.tap("dbg_stt2", G["stt2"], (128, 40), F32)
    p.tap("dbg_xtm", G["xtm"], (128, 4 * D), BF16)
    p.tap("dbg_reps", G["reps"], (128, 2 * TKH), F32)
    if PHASES < 2:
        return _fin(p, g)

    with ExitStack() as c_aa:
        atta = p.sb("atta", [128, 4 * TK], BF16, ctx=c_aa)
        qlo = p.sb("qlo", [128, 2 * TKH], BF16, ctx=c_aa)
        qhi = p.sb("qhi", [128, 2 * TKH], BF16, ctx=c_aa)
        klo = p.sb("klo", [128, 2 * TKH], BF16, ctx=c_aa)
        khi = p.sb("khi", [128, 2 * TKH], BF16, ctx=c_aa)
        vta = p.sb("vta", [128, 18 * 512], BF16, ctx=c_aa)

        with ExitStack() as c_qkv:
            p.barrier()
            wqm = p.sb("wqm", [128, 2 * 1024], BF16, ctx=c_qkv)
            wqv = p.sb("wqv", [128, 8 * 512], BF16, ctx=c_qkv)
            HTK = TKH // 2
            csb = p.sb("csb", [128, HTK], ctx=c_qkv)
            snb = p.sb("snb", [128, HTK], ctx=c_qkv)
            rs1 = G["scr"]
            rs2 = G["stg"]
            pA = p.psum("pA", [128, 512], ctx=c_qkv)
            pB = p.psum("pB", [128, 512], ctx=c_qkv)

            @p.step("gpsimd")
            def _(gp):
                for kc in range(8):
                    p.dma(gp, wqv[:, kc * 512:(kc + 1) * 512],
                          g["wqkv"].ap()[kc * 128:(kc + 1) * 128, 1024:1536], "SG")

            NT5 = [(0, 512), (512, 512), (1024, 512), (1536, 512), (2048, 256)]
            sv0 = p.cnt["SV"]
            st_after = {}
            si = 0
            for m in range(8):
                dst = (qlo, qhi, klo, khi)[m // 2]

                @p.step("gpsimd", after=(("ST", st_after.get(m - 2, 0)),))
                def _(gp, m=m):
                    for kc in range(8):
                        p.dma(gp, wqm[:, (m % 2) * 1024 + kc * 128: (m % 2) * 1024 + kc * 128 + 128],
                              g["wqkv"].ap()[kc * 128:(kc + 1) * 128, m * 128:(m + 1) * 128],
                              "SG")

                sg_m = p.cnt["SG"]
                for (tc0, tn) in NT5:
                    ps = pA if si % 2 == 0 else pB

                    @p.step("tensor", after=(("SV", max(sv0, sv0 + si - 1)), ("SG", sg_m)))
                    def _(t, m=m, tc0=tc0, tn=tn, ps=ps):
                        for kc in range(8):
                            inst = t.matmul(ps[:, 0:tn],
                                            wqm[:, (m % 2) * 1024 + kc * 128: (m % 2) * 1024 + kc * 128 + 128],
                                            G["xh"][:, kc * TKH + tc0: kc * TKH + tc0 + tn],
                                            start=(kc == 0), stop=(kc == 7))
                        p.fin(inst, "ST")

                    @p.step("vector", waits=("ST",))
                    def _(v, dst=dst, m=m, tc0=tc0, tn=tn, ps=ps):
                        inst = v.tensor_copy(dst[:, (m % 2) * TKH + tc0: (m % 2) * TKH + tc0 + tn],
                                             ps[:, 0:tn])
                        p.fin(inst, "SV")
                    si += 1
                st_after[m] = p.cnt["ST"]

            sg_v = p.cnt["SG"]
            for tc in range(18):
                ps = pA if si % 2 == 0 else pB

                @p.step("tensor", after=(("SV", sv0 + si - 1), ("SG", sg_v)))
                def _(t, tc=tc, ps=ps):
                    for kc in range(8):
                        inst = t.matmul(ps[:, 0:512],
                                        G["xh"][:, kc * TKH + tc * 128: kc * TKH + tc * 128 + 128],
                                        wqv[:, kc * 512:(kc + 1) * 512],
                                        start=(kc == 0), stop=(kc == 7))
                    p.fin(inst, "ST")

                @p.step("vector", waits=("ST",))
                def _(v, tc=tc, ps=ps):
                    inst = v.tensor_copy(vta[:, tc * 512:(tc + 1) * 512], ps[:, 0:512])
                    p.fin(inst, "SV")
                si += 1

            for c in range(2):
                for hf in range(2):
                    @p.step("sync", waits=("SV",))
                    def _(s, c=c, hf=hf):
                        h0 = hf * HTK
                        p.dma(s, csb[:],
                              g["cosT"].ap()[c * 128:(c + 1) * 128, h0:h0 + HTK], "SD")
                        p.dma(s, snb[:],
                              g["sinT"].ap()[c * 128:(c + 1) * 128, h0:h0 + HTK], "SD")
                        p.dma(s, rs1[:, 0:HTK],
                              g["cosTl"].ap()[c * 128:(c + 1) * 128, h0:h0 + HTK], "SD")
                        p.dma(s, rs2[:, 0:HTK],
                              g["sinTl"].ap()[c * 128:(c + 1) * 128, h0:h0 + HTK], "SD")

                    @p.step("vector", waits=("SD",))
                    def _(v, c=c, hf=hf):
                        v.tensor_add(snb[:], snb[:], rs2[:, 0:HTK])
                        v.tensor_add(csb[:], csb[:], rs1[:, 0:HTK])
                        sl = slice(c * TKH + hf * HTK, c * TKH + (hf + 1) * HTK)
                        for lo, hi in ((qlo, qhi), (klo, khi)):
                            v.tensor_mul(rs1[:, 0:HTK], lo[:, sl], snb[:])
                            v.tensor_mul(rs2[:, 0:HTK], hi[:, sl], snb[:])
                            v.tensor_mul(lo[:, sl], lo[:, sl], csb[:])
                            v.tensor_mul(hi[:, sl], hi[:, sl], csb[:])
                            v.tensor_sub(lo[:, sl], lo[:, sl], rs2[:, 0:HTK])
                            inst = v.tensor_add(hi[:, sl], hi[:, sl], rs1[:, 0:HTK])
                        p.fin(inst, "SV")

        p.tap("qlo", qlo, (128, 2 * TKH), BF16)
        p.tap("khi", khi, (128, 2 * TKH), BF16)
        p.tap("vta", vta, (128, 18 * 512), BF16)
        if PHASES < 3:
            return _fin(p, g)

        with ExitStack() as c_att:
            p.barrier()
            msk = p.sb("msk", [128, 8 * 512], BF16, ctx=c_att)
            ptl = p.sb("ptl", [128, 4 * 512], BF16, ctx=c_att)
            dsb = p.sb("dsb", [128, 2 * 512], ctx=c_att)
            saps = [p.psum(f"saps{i}", [128, 512], ctx=c_att) for i in range(4)]
            avps = [p.psum(f"avps{i}", [128, 512], ctx=c_att) for i in range(2)]
            dnps = p.psum("dnps", [128, 512], ctx=c_att)
            rpps = G["bps"]

            @p.step("sync")
            def _(s):
                for sl in range(8):
                    p.dma(s, msk[:, sl * 512:(sl + 1) * 512],
                          g["amask"].ap()[sl * 128:(sl + 1) * 128, :], "SD")

            @p.step("gpsimd")
            def _(gp):
                p.fin(gp.memset(dsb[:], 1.0), "GC")

            for hg in range(2):
                for qt in range(4):
                    _attn_a_tile(p, G, hg, qt, qlo, qhi, klo, khi, vta, msk,
                                 ptl, saps, avps, dnps, rpps, dsb, atta)

        p.tap("atta", atta, (128, 4 * TK), BF16)
        if PHASES < 4:
            return _fin(p, g)

        with ExitStack() as c_wo:
            p.barrier()
            wos = p.sb("wos", [128, 4 * D], BF16, ctx=c_wo)
            ppa = p.psum("ppra", [128, 512], ctx=c_wo)
            ppb = p.psum("pprb", [128, 512], ctx=c_wo)

            @p.step("gpsimd")
            def _(gp):
                for kc in range(4):
                    p.dma(gp, wos[:, kc * D:(kc + 1) * D],
                          g["wout"].ap()[kc * 128:(kc + 1) * 128, :], "SG")

            def wout_mm(t, m, n0, nn, ps):
                for kc in range(4):
                    inst = t.matmul(ps[:, 0:nn],
                                    wos[:, kc * D + m * 128: kc * D + m * 128 + 128],
                                    atta[:, kc * TK + n0: kc * TK + n0 + nn],
                                    start=(kc == 0), stop=(kc == 3))
                return inst

            _proj_residual(p, g, G, mm=wout_mm, src_res=g["xf"], src_off=HALO,
                           ppa=ppa, ppb=ppb, fscale=None, wait_w="SG")

    if PHASES < 5:
        return _fin(p, g)

    G["qrhs"] = p.sb("qrhs", [128, 8 * D], BF16)
    G["rsv"] = p.sb("rsv", [128, 8])
    with ExitStack() as c_f1:
        p.barrier()
        h2q = p.sb("h2q", [128, 8 * TK], BF16, ctx=c_f1)
        qrhs, rsv = G["qrhs"], G["rsv"]
        G["scrq"] = p.sb("scrq_f1", [128, 512], ctx=c_f1)
        G["scri"] = p.sb("scri_f1", [128, TK], I32, ctx=c_f1)
        G["scri2"] = p.sb("scri2_f1", [128, TK], I32, ctx=c_f1)

        _norm(p, g, G, src=g["strmh"], ncols=TK, precast=True, quant_out=h2q)
        p.tap("h2q", h2q, (128, 8 * TK), BF16)
        _quant_rhs(p, g, G, c_f1)
        p.tap("qrhs", qrhs, (128, 8 * D), BF16)
        p.tap("rsv", rsv, (128, 8), F32)
        if PHASES >= 6:
            with ExitStack() as c_mm:
                p.barrier()
                ppa = p.psum("pfa", [128, 512], ctx=c_mm)
                ppb = p.psum("pfb", [128, 512], ctx=c_mm)

                def ffn_mm(t, m, n0, nn, ps):
                    for kc in range(8):
                        inst = t.matmul(ps[:, 0:nn],
                                        qrhs[:, kc * D + m * 128: kc * D + m * 128 + 128],
                                        h2q[:, kc * TK + n0: kc * TK + n0 + nn],
                                        start=(kc == 0), stop=(kc == 7))
                    return inst

                _proj_residual(p, g, G, mm=ffn_mm, src_res=g["strmh"], src_off=0,
                               ppa=ppa, ppb=ppb, fscale="ffn", wait_w=None)
    if PHASES < 7:
        return _fin(p, g)

    _attn_b(p, g, G)
    if PHASES < 8:
        return _fin(p, g)
    with ExitStack() as c_f2:
        p.barrier()
        h4q = p.sb("h4q", [128, 8 * TK], BF16, ctx=c_f2)
        G["scrq"] = p.sb("scrq_f2", [128, 512], ctx=c_f2)
        G["scri"] = p.sb("scri_f2", [128, TK], I32, ctx=c_f2)
        G["scri2"] = p.sb("scri2_f2", [128, TK], I32, ctx=c_f2)
        _norm(p, g, G, src=g["strmh"], ncols=TK, precast=True, quant_out=h4q)
        ppa = p.psum("pf2a", [128, 512], ctx=c_f2)
        ppb = p.psum("pf2b", [128, 512], ctx=c_f2)

        def ffn2_mm(t, m, n0, nn, ps):
            for kc in range(8):
                inst = t.matmul(ps[:, 0:nn],
                                G["qrhs"][:, kc * D + m * 128: kc * D + m * 128 + 128],
                                h4q[:, kc * TK + n0: kc * TK + n0 + nn],
                                start=(kc == 0), stop=(kc == 7))
            return inst

        _proj_residual(p, g, G, mm=ffn2_mm, src_res=g["strmh"], src_off=0,
                       ppa=ppa, ppb=ppb, fscale="ffn", wait_w=None, dst=g["y"])
    return _fin(p, g)


def _norm(p, g, G, src, ncols, precast, quant_out=None, eps=EPS, pcsrc=None):
    stg, xtm, stt = G["stg"], G["xtm"], G["stt"]
    reps, scr = G["reps"], G["scr"]
    xh = G["xh"]
    pc = pcsrc if pcsrc is not None else xh
    pcw = TK if pcsrc is not None else TKH
    nt = (ncols + 127) // 128

    if not precast:
        for c in range(8):
            @p.step("sync", waits=("SV",))
            def _(s, c=c):
                p.dma(s, stg[:, (c % 2) * TKH:(c % 2) * TKH + ncols],
                      src.ap()[c * 128:(c + 1) * 128, 0:ncols], "SD")

            @p.step("vector", waits=("SD",))
            def _(v, c=c):
                inst = v.tensor_copy(xh[:, c * TKH:c * TKH + ncols],
                                     stg[:, (c % 2) * TKH:(c % 2) * TKH + ncols])
                p.fin(inst, "SV")

    ngr = (nt + 3) // 4
    svq = {}
    for tg in range(ngr):
        span = min(512, ncols - tg * 512)
        nti = (span + 127) // 128
        xoff = (tg % 2) * 4 * D
        svgate = svq[tg - 2] if tg >= 2 else p.cnt["SV"]

        @p.step("scalar", after=(("SV", svgate),))
        def _(sc, tg=tg, span=span, xoff=xoff):
            nb = span // 128
            for c in range(8):
                p.dma(sc, bass.AP(xtm, xoff + c * 512, [[8 * D, 128], [128, nb], [1, 128]]),
                      bass.AP(pc, c * pcw + tg * 512, [[8 * pcw, 128], [128, nb], [1, 128]]),
                      "SA", transpose=True)

        if quant_out is not None:
            qo = quant_out
            for c in range(8):
                @p.step("sync", waits=("SV", "SA"))
                def _(s, c=c, tg=tg, span=span):
                    p.dma(s, stg[:, (c % 2) * TKH:(c % 2) * TKH + span],
                          src.ap()[c * 128:(c + 1) * 128, tg * 512:tg * 512 + span],
                          "SD")

                @p.step("vector", waits=("SD",))
                def _(v, c=c, tg=tg, span=span):
                    inst = v.tensor_sub(qo[:, 4 * D + c * 512:4 * D + c * 512 + span],
                                        stg[:, (c % 2) * TKH:(c % 2) * TKH + span],
                                        pc[:, c * pcw + tg * 512:c * pcw + tg * 512 + span])
                    p.fin(inst, "SV")

            @p.step("scalar", waits=("SV",))
            def _(sc, tg=tg, span=span):
                nb = span // 128
                for c in range(8):
                    p.dma(sc, bass.AP(qo, c * 512, [[8 * TK, 128], [128, nb], [1, 128]]),
                          bass.AP(qo, 4 * D + c * 512, [[8 * TK, 128], [128, nb], [1, 128]]),
                          "SA", transpose=True)

            @p.step("vector", waits=("SA",))
            def _(v, tg=tg, nti=nti, qo=qo, xoff=xoff):
                stt2 = G["stt2"]
                for pp in range(nti // 2):
                    ta = tg * 4 + 2 * pp
                    tb = ta + 1
                    hiA = bass.AP(xtm, xoff + (2 * pp) * 128,
                                  [[8 * D, 128], [512, 8], [1, 128]])
                    hiB = bass.AP(xtm, xoff + (2 * pp + 1) * 128,
                                  [[8 * D, 128], [512, 8], [1, 128]])
                    loA = bass.AP(qo, (2 * pp) * 128, [[8 * TK, 128], [512, 8], [1, 128]])
                    loB = bass.AP(qo, (2 * pp + 1) * 128, [[8 * TK, 128], [512, 8], [1, 128]])
                    tmpA = scr[:, 0:1024]
                    tmpB = scr[:, 1024:2048]
                    dmpA = qo[:, 4 * D:4 * D + 1024]
                    dmpB = qo[:, 4 * D + 1024:4 * D + 2048]
                    v.tensor_tensor(tmpA, hiA, loA, ALU.add)
                    v.tensor_tensor(tmpB, hiB, loB, ALU.add)
                    v.tensor_reduce(stt[:, ta:ta + 1], tmpA, AX.X, ALU.add)
                    v.tensor_reduce(stt[:, tb:tb + 1], tmpB, AX.X, ALU.add)
                    v.tensor_scalar(stt[:, 32 + ta:33 + ta], stt[:, ta:ta + 1],
                                    1.0 / D, None, ALU.mult)
                    v.tensor_scalar(stt[:, 32 + tb:33 + tb], stt[:, tb:tb + 1],
                                    1.0 / D, None, ALU.mult)
                    v.scalar_tensor_tensor(dmpA, tmpA, 1.0, tmpA, ALU.mult, ALU.mult,
                                           accum_out=stt2[:, ta:ta + 1])
                    v.scalar_tensor_tensor(dmpB, tmpB, 1.0, tmpB, ALU.mult, ALU.mult,
                                           accum_out=stt2[:, tb:tb + 1])
                    v.tensor_scalar(tmpA, tmpA, stt[:, 32 + ta:33 + ta], None,
                                    ALU.subtract)
                    v.tensor_scalar(tmpB, tmpB, stt[:, 32 + tb:33 + tb], None,
                                    ALU.subtract)
                    v.tensor_reduce(stt[:, 96 + ta:97 + ta], tmpA, AX.X,
                                    ALU.max, apply_absolute_value=True)
                    inst = v.tensor_reduce(stt[:, 96 + tb:97 + tb], tmpB, AX.X,
                                           ALU.max, apply_absolute_value=True)
                    p.fin(inst, "SV")
        else:
            @p.step("vector", waits=("SA",))
            def _(v, tg=tg, nti=nti, xoff=xoff):
                for tj in range(nti):
                    ti = tg * 4 + tj
                    tm = bass.AP(xtm, xoff + tj * 128, [[8 * D, 128], [512, 8], [1, 128]])
                    sc3 = bass.AP(scr, 0, [[TKH, 128], [128, 8], [1, 128]])
                    stt2 = G["stt2"]
                    v.tensor_reduce(stt[:, ti:ti + 1], tm, AX.XY, ALU.add)
                    inst = v.scalar_tensor_tensor(sc3, tm, 1.0, tm, ALU.mult,
                                                  ALU.mult, accum_out=stt2[:, ti:ti + 1])
                    p.fin(inst, "SV")
        svq[tg] = p.cnt["SV"]

    stt2 = G["stt2"]

    @p.step("vector", waits=("SV",))
    def _(v):
        inst = v.tensor_scalar(stt[:, 32:32 + nt], stt[:, 0:nt], 1.0 / D,
                               None, ALU.mult)
        p.fin(inst, "SV")

    @p.step("gpsimd", waits=("SV",))
    def _(gp):
        inst = gp.tensor_mul(scr[:, 0:nt], stt[:, 32:32 + nt], stt[:, 32:32 + nt])
        p.fin(inst, "GC")

    @p.step("vector", waits=("GC",))
    def _(v):
        inst = v.scalar_tensor_tensor(stt2[:, 18:18 + nt], stt2[:, 0:nt], 1.0 / D,
                                      scr[:, 0:nt], ALU.mult, ALU.subtract)
        p.fin(inst, "SV")

    @p.step("scalar", waits=("SV",))
    def _(sc):
        ecol = G["csh"][:, 2:3] if eps else G["csh"][:, 3:4]
        inst = sc.activation(stt[:, 64:64 + nt], stt2[:, 18:18 + nt], AF.Sqrt,
                             bias=ecol, scale=1.0)
        p.fin(inst, "SS")

    @p.step("vector", waits=("SS",))
    def _(v):
        h = (nt + 1) // 2
        v.reciprocal(stt[:, 64:64 + h], stt[:, 64:64 + h])
        inst = v.reciprocal(stt[:, 64 + h:64 + nt], stt[:, 64 + h:64 + nt])
        p.fin(inst, "SV")

    if quant_out is None:
        _rows_transpose(p, G)
        _bcast_rows(p, g, G, [(32, 0), (64, 1)], ncols)
        for j in range(4):
            c0, c1 = 2 * j, 2 * j + 1

            @p.step("sync", waits=("SV",))
            def _(s, c0=c0, c1=c1):
                p.dma(s, stg[:, 0:ncols],
                      src.ap()[c0 * 128:(c0 + 1) * 128, 0:ncols], "SD")
                p.dma(s, stg[:, TKH:TKH + ncols],
                      src.ap()[c1 * 128:(c1 + 1) * 128, 0:ncols], "SD")

            @p.step("vector", waits=("SD",))
            def _(v, c0=c0, c1=c1, j=j):
                if j == 0:
                    v.tensor_copy(scr[:, 0:1], G["onesb"][:, 0:1])
                v.tensor_sub(stg[:, 0:ncols], stg[:, 0:ncols], reps[:, 0:ncols])
                v.tensor_sub(stg[:, TKH:TKH + ncols], stg[:, TKH:TKH + ncols],
                             reps[:, 0:ncols])
                v.tensor_mul(xh[:, c0 * TKH:c0 * TKH + ncols], stg[:, 0:ncols],
                             reps[:, TKH:TKH + ncols])
                inst = v.tensor_mul(xh[:, c1 * TKH:c1 * TKH + ncols],
                                    stg[:, TKH:TKH + ncols],
                                    reps[:, TKH:TKH + ncols])
                p.fin(inst, "SV")
    else:
        @p.step("vector", waits=("SV",))
        def _(v):
            h = nt // 2
            a = slice(0, h)
            b = slice(h, nt)
            def S(base, sl):
                return stt[:, base + sl.start: base + sl.stop]
            v.tensor_mul(scr[:, 0:h], S(96, a), S(64, a))
            v.tensor_mul(scr[:, h:nt], S(96, b), S(64, b))
            v.tensor_scalar(scr[:, 0:h], scr[:, 0:h], 1e-12, 1.0 / 127.0,
                            ALU.max, ALU.mult)
            v.tensor_scalar(scr[:, h:nt], scr[:, h:nt], 1e-12, 1.0 / 127.0,
                            ALU.max, ALU.mult)
            v.reciprocal(scr[:, nt:nt + h], scr[:, 0:h])
            v.reciprocal(scr[:, nt + h:2 * nt], scr[:, h:nt])
            v.tensor_mul(S(96, a), S(64, a), scr[:, nt:nt + h])
            v.tensor_mul(S(96, b), S(64, b), scr[:, nt + h:2 * nt])
            v.scalar_tensor_tensor(S(64, a), S(32, a), -0.5, S(96, a),
                                   ALU.bypass, ALU.mult)
            v.scalar_tensor_tensor(S(64, b), S(32, b), -0.5, S(96, b),
                                   ALU.bypass, ALU.mult)
            v.tensor_scalar(S(64, a), S(64, a), -0.5, None, ALU.add)
            v.tensor_scalar(S(64, b), S(64, b), -0.5, None, ALU.add)
            v.tensor_copy(S(32, a), scr[:, 0:h])
            inst = v.tensor_copy(S(32, b), scr[:, h:nt])
            p.fin(inst, "SV")

        _rows_transpose(p, G)

        @p.step("vector", waits=("ST",))
        def _(v):
            inst = v.tensor_copy(G["lssav"][0:nt, :], G["bps"][32:32 + nt, 0:128])
            p.fin(inst, "SV")

        _bcast_rows(p, g, G, [(96, 0), (64, 1)], ncols)

        scri, scri2 = G["scri"], G["scri2"]
        for j in range(4):
            c0, c1 = 2 * j, 2 * j + 1

            @p.step("sync", waits=("SV",))
            def _(s, c0=c0, c1=c1):
                p.dma(s, stg[:, 0:ncols],
                      src.ap()[c0 * 128:(c0 + 1) * 128, 0:ncols], "SD")
                p.dma(s, stg[:, TKH:TKH + ncols],
                      src.ap()[c1 * 128:(c1 + 1) * 128, 0:ncols], "SD")

            @p.step("vector", waits=("SD",))
            def _(v, c0=c0, c1=c1, j=j):
                z0 = stg[:, 0:ncols]
                z1 = stg[:, TKH:TKH + ncols]
                q0 = quant_out[:, c0 * TK:c0 * TK + ncols]
                q1 = quant_out[:, c1 * TK:c1 * TK + ncols]
                if j == 0:
                    v.tensor_copy(scr[:, 0:1], G["onesb"][:, 0:1])
                v.tensor_mul(z0, z0, reps[:, 0:ncols])
                v.tensor_mul(z1, z1, reps[:, 0:ncols])
                v.tensor_sub(z0, z0, reps[:, TKH:TKH + ncols])
                v.tensor_sub(z1, z1, reps[:, TKH:TKH + ncols])
                v.tensor_copy(scri[:, 0:ncols], z0)
                v.tensor_copy(scri2[:, 0:ncols], z1)
                v.tensor_copy(q0, scri[:, 0:ncols])
                v.tensor_copy(q1, scri2[:, 0:ncols])
                v.tensor_tensor(scr[:, 0:ncols], z0, q0, ALU.is_lt)
                v.tensor_tensor(scri[:, 0:ncols], z1, q1, ALU.is_lt)
                v.tensor_sub(q0, q0, scr[:, 0:ncols])
                inst = v.tensor_sub(q1, q1, scri[:, 0:ncols])
                p.fin(inst, "SV")


def _rows_transpose(p, G, ncolT=128):
    @p.step("tensor", waits=("SV",))
    def _(t):
        inst = t.transpose(G["bps"][0:128, 0:128], G["stt"][:, 0:128], G["ident"][:])
        p.fin(inst, "ST")

    @p.step("vector", waits=("ST",))
    def _(v):
        inst = v.tensor_copy(G["rowb"][0:128, :], G["bps"][0:128, 0:128])
        p.fin(inst, "SV")


def _bcast_rows(p, g, G, pairs, ncols, srcbuf=None):
    bps, onesf, reps, rowf = G["bps"], G["onesf"], G["reps"], G["rowf"]
    rb = srcbuf if srcbuf is not None else G["rowb"]
    nt = (ncols + 127) // 128
    for rowbase, dstc in pairs:
        @p.step("scalar", waits=("SV",))
        def _(sc, rowbase=rowbase):
            p.dma(sc, rowf[0:1, 0:ncols], rb[rowbase:rowbase + nt, 0:128], "SA")

        for bi in range((ncols + 511) // 512):
            n0 = bi * 512
            nn = min(512, ncols - n0)

            @p.step("tensor", waits=("SA", "SV"))
            def _(t, n0=n0, nn=nn):
                inst = t.matmul(bps[:, 0:nn], onesf[:], rowf[0:1, n0:n0 + nn],
                                start=True, stop=True)
                p.fin(inst, "ST")

            @p.step("vector", waits=("ST",))
            def _(v, dstc=dstc, n0=n0, nn=nn):
                inst = v.tensor_copy(reps[:, dstc * TKH + n0: dstc * TKH + n0 + nn],
                                     bps[:, 0:nn])
                p.fin(inst, "SV")


def _attn_a_tile(p, G, hg, qt, qlo, qhi, klo, khi, vta, msk, ptl,
                 saps, avps, dnps, rpps, dsb, atta):
    heads = [4 * hg + i for i in range(4)]
    qof = hg * TKH + HALO + qt * 512

    for j in range(6):
        kc = qt * 4 + j
        msl = 6 + j if (qt == 0 and j < 2) else j

        @p.step("tensor", waits=("SS", "SV", "SD"))
        def _(t, kc=kc):
            for i, h in enumerate(heads):
                ro = (h % 4) * 32
                ksl = slice(hg * TKH + kc * 128, hg * TKH + kc * 128 + 128)
                qsl = slice(qof, qof + 512)
                t.matmul(saps[i][:, :], klo[ro:ro + 32, ksl], qlo[ro:ro + 32, qsl],
                         start=True, stop=False, tile_position=(ro, 0))
                inst = t.matmul(saps[i][:, :], khi[ro:ro + 32, ksl],
                                qhi[ro:ro + 32, qsl], start=False, stop=True,
                                tile_position=(ro, 0))
                p.fin(inst, "ST")

        @p.step("scalar", waits=("ST",))
        def _(sc):
            for i in range(4):
                inst = sc.activation(ptl[:, i * 512:(i + 1) * 512], saps[i][:, :],
                                     AF.Exp, bias=G["csh"][:, 0:1], scale=0.125)
                p.fin(inst, "SS")

        @p.step("vector", waits=("SS",))
        def _(v, msl=msl):
            for i in range(4):
                inst = v.tensor_mul(ptl[:, i * 512:(i + 1) * 512],
                                    ptl[:, i * 512:(i + 1) * 512],
                                    msk[:, msl * 512:(msl + 1) * 512])
            p.fin(inst, "SV")

        @p.step("tensor", waits=("SV",))
        def _(t, j=j, kc=kc):
            for pr in range(2):
                for ii in range(2):
                    i = pr * 2 + ii
                    h = heads[i]
                    vsl = slice(kc * 512 + h * 64, kc * 512 + h * 64 + 64)
                    t.matmul(avps[pr][ii * 64:(ii + 1) * 64, :], vta[:, vsl],
                             ptl[:, i * 512:(i + 1) * 512],
                             start=(j == 0), stop=(j == 5),
                             tile_position=(0, ii * 64), skip_group_check=True)
                    inst = t.matmul(dnps[i * 32:i * 32 + 1, :], G["onesb"][:],
                                    ptl[:, i * 512:(i + 1) * 512],
                                    start=(j == 0), stop=(j == 5),
                                    tile_position=(0, i * 32), skip_group_check=True)
                    p.fin(inst, "ST")

    @p.step("vector", waits=("ST", "GC"))
    def _(v):
        for i in range(4):
            v.tensor_copy(dsb[i * 32:i * 32 + 1, 0:512], dnps[i * 32:i * 32 + 1, :])
        for i in range(4):
            inst = v.reciprocal(dsb[i * 32:i * 32 + 1, 512:1024],
                                dsb[i * 32:i * 32 + 1, 0:512])
        p.fin(inst, "SV")

    for pr in range(2):
        @p.step("tensor", waits=("SV",))
        def _(t, pr=pr):
            for ii in range(2):
                i = pr * 2 + ii
                inst = t.matmul(rpps[ii * 64:(ii + 1) * 64, :],
                                G["onesm"][i * 32:i * 32 + 1, 0:64],
                                dsb[i * 32:i * 32 + 1, 512:1024],
                                start=True, stop=True,
                                tile_position=(i * 32, ii * 64),
                                skip_group_check=True)
                p.fin(inst, "ST")

        @p.step("vector", waits=("ST",))
        def _(v, pr=pr):
            pc = hg * 2 + pr
            ob = pc * TK + qt * 512
            v.tensor_copy(dsb[:, 0:256], rpps[:, 0:256])
            v.tensor_copy(dsb[:, 256:512], rpps[:, 256:512])
            v.tensor_mul(atta[:, ob:ob + 256], avps[pr][:, 0:256], dsb[:, 0:256])
            inst = v.tensor_mul(atta[:, ob + 256:ob + 512], avps[pr][:, 256:512],
                                dsb[:, 256:512])
            p.fin(inst, "SV")


def _proj_residual(p, g, G, mm, src_res, src_off, ppa, ppb, fscale, wait_w, dst=None):
    stg, xh, reps = G["stg"], G["xh"], G["reps"]
    if fscale == "ffn":
        _bcast_rows(p, g, G, [(0, 0)], TK, srcbuf=G["lssav"])
    items = [(m, n0) for m in range(8) for n0 in (0, 512, 1024, 1536)]
    sv0 = p.cnt["SV"]
    extra = (("SG", p.cnt["SG"]),) if wait_w else ()
    for si, (m, n0) in enumerate(items):
        ps = ppa if si % 2 == 0 else ppb

        @p.step("tensor", after=(("SV", max(sv0, sv0 + (si - 1) * 2)),) + extra)
        def _(t, m=m, n0=n0, ps=ps):
            inst = mm(t, m, n0, 512, ps)
            p.fin(inst, "ST")

        @p.step("sync", waits=("SV",))
        def _(s, m=m, n0=n0, si=si):
            p.dma(s, stg[:, (si % 2) * TKH:(si % 2) * TKH + 512],
                  src_res.ap()[m * 128:(m + 1) * 128, src_off + n0: src_off + n0 + 512],
                  "SD")

        @p.step("vector", waits=("ST", "SD"))
        def _(v, m=m, n0=n0, si=si, ps=ps):
            s0 = (si % 2) * TKH
            if fscale == "ffn":
                v.scalar_tensor_tensor(G["scrq"][:, 0:256], ps[:, 0:256],
                                       G["rsv"][:, m:m + 1], reps[:, n0:n0 + 256],
                                       ALU.mult, ALU.mult)
                v.scalar_tensor_tensor(G["scrq"][:, 256:512], ps[:, 256:512],
                                       G["rsv"][:, m:m + 1],
                                       reps[:, n0 + 256:n0 + 512],
                                       ALU.mult, ALU.mult)
                v.tensor_add(stg[:, s0:s0 + 256], stg[:, s0:s0 + 256],
                             G["scrq"][:, 0:256])
                v.tensor_add(stg[:, s0 + 256:s0 + 512], stg[:, s0 + 256:s0 + 512],
                             G["scrq"][:, 256:512])
            else:
                v.tensor_add(stg[:, s0:s0 + 256], stg[:, s0:s0 + 256], ps[:, 0:256])
                v.tensor_add(stg[:, s0 + 256:s0 + 512], stg[:, s0 + 256:s0 + 512],
                             ps[:, 256:512])
            v.tensor_copy(xh[:, m * TKH + n0: m * TKH + n0 + 256], stg[:, s0:s0 + 256])
            inst = v.tensor_copy(xh[:, m * TKH + n0 + 256: m * TKH + n0 + 512],
                                 stg[:, s0 + 256:s0 + 512])
            p.fin(inst, "SV", 2)

        @p.step("sync", waits=("SV",))
        def _(s, m=m, n0=n0, si=si):
            p.dma(s, (dst or g["strmh"]).ap()[m * 128:(m + 1) * 128, n0:n0 + 512],
                  stg[:, (si % 2) * TKH:(si % 2) * TKH + 512], "SD")


def _quant_rhs(p, g, G, cscope):
    stg, scr, scri = G["stg"], G["scr"], G["scri"]
    xtm, reps, stt = G["xtm"], G["reps"], G["stt"]
    qrhs, rsv = G["qrhs"], G["rsv"]
    hi8 = p.sb("hi8_qr", [128, 8 * D], BF16, ctx=cscope)
    lo8 = p.sb("lo8_qr", [128, 8 * D], BF16, ctx=cscope)

    for j in range(4):
        c0, c1 = 2 * j, 2 * j + 1

        @p.step("sync", waits=("SV",))
        def _(s, c0=c0, c1=c1):
            p.dma(s, stg[:, 0:D], g["frhs"].ap()[c0 * 128:(c0 + 1) * 128, :], "SD")
            p.dma(s, stg[:, TKH:TKH + D],
                  g["frhs"].ap()[c1 * 128:(c1 + 1) * 128, :], "SD")

        @p.step("vector", waits=("SD",))
        def _(v, c0=c0, c1=c1):
            v.tensor_copy(hi8[:, c0 * D:(c0 + 1) * D], stg[:, 0:D])
            v.tensor_copy(hi8[:, c1 * D:(c1 + 1) * D], stg[:, TKH:TKH + D])
            v.tensor_sub(lo8[:, c0 * D:(c0 + 1) * D], stg[:, 0:D],
                         hi8[:, c0 * D:(c0 + 1) * D])
            inst = v.tensor_sub(lo8[:, c1 * D:(c1 + 1) * D], stg[:, TKH:TKH + D],
                                hi8[:, c1 * D:(c1 + 1) * D])
            p.fin(inst, "SV")

    for ig in range(2):
        @p.step("scalar", waits=("SV",))
        def _(sc, ig=ig):
            for c in range(8):
                p.dma(sc, bass.AP(xtm, c * 512, [[8 * D, 128], [128, 4], [1, 128]]),
                      bass.AP(hi8, c * D + ig * 512, [[8 * D, 128], [128, 4], [1, 128]]),
                      "SA", transpose=True)
                p.dma(sc, bass.AP(G["xh"], c * 512, [[8 * TKH, 128], [128, 4], [1, 128]]),
                      bass.AP(lo8, c * D + ig * 512, [[8 * D, 128], [128, 4], [1, 128]]),
                      "SA", transpose=True)

        @p.step("vector", waits=("SA",))
        def _(v, ig=ig):
            for pp in range(2):
                ta = ig * 4 + 2 * pp
                tb = ta + 1
                hiA = bass.AP(xtm, (2 * pp) * 128, [[8 * D, 128], [512, 8], [1, 128]])
                hiB = bass.AP(xtm, (2 * pp + 1) * 128, [[8 * D, 128], [512, 8], [1, 128]])
                loA = bass.AP(G["xh"], (2 * pp) * 128, [[8 * TKH, 128], [512, 8], [1, 128]])
                loB = bass.AP(G["xh"], (2 * pp + 1) * 128, [[8 * TKH, 128], [512, 8], [1, 128]])
                scA = bass.AP(scr, 0, [[TKH, 128], [128, 8], [1, 128]])
                scB = bass.AP(scr, 1024, [[TKH, 128], [128, 8], [1, 128]])
                v.tensor_tensor(scA, hiA, loA, ALU.add)
                v.tensor_tensor(scB, hiB, loB, ALU.add)
                v.tensor_reduce(stt[:, 96 + ta:97 + ta], scA, AX.XY,
                                ALU.max, apply_absolute_value=True)
                inst = v.tensor_reduce(stt[:, 96 + tb:97 + tb], scB, AX.XY,
                                       ALU.max, apply_absolute_value=True)
                p.fin(inst, "SV")

    @p.step("vector", waits=("SV",))
    def _(v):
        v.tensor_scalar(rsv[:, 0:4], stt[:, 96:100], 1e-12, 1.0 / 127.0,
                        ALU.max, ALU.mult)
        v.tensor_scalar(rsv[:, 4:8], stt[:, 100:104], 1e-12, 1.0 / 127.0,
                        ALU.max, ALU.mult)
        v.reciprocal(stt[:, 96:100], rsv[:, 0:4])
        inst = v.reciprocal(stt[:, 100:104], rsv[:, 4:8])
        p.fin(inst, "SV")

    _rows_transpose(p, G)
    _bcast_rows(p, g, G, [(96, 1)], D)

    scri2 = G["scri2"]
    for j in range(4):
        c0, c1 = 2 * j, 2 * j + 1

        @p.step("sync", waits=("SV",))
        def _(s, c0=c0, c1=c1):
            p.dma(s, stg[:, 0:D], g["frhs"].ap()[c0 * 128:(c0 + 1) * 128, :], "SD")
            p.dma(s, stg[:, TKH:TKH + D],
                  g["frhs"].ap()[c1 * 128:(c1 + 1) * 128, :], "SD")

        @p.step("vector", waits=("SD",))
        def _(v, c0=c0, c1=c1, j=j):
            z0 = stg[:, 0:D]
            z1 = stg[:, TKH:TKH + D]
            q0 = qrhs[:, c0 * D:(c0 + 1) * D]
            q1 = qrhs[:, c1 * D:(c1 + 1) * D]
            if j == 0:
                v.tensor_copy(scr[:, 0:1], G["onesb"][:, 0:1])
            v.tensor_mul(z0, z0, reps[:, TKH:TKH + D])
            v.tensor_mul(z1, z1, reps[:, TKH:TKH + D])
            v.tensor_scalar(z0, z0, 0.5, None, ALU.add)
            v.tensor_scalar(z1, z1, 0.5, None, ALU.add)
            v.tensor_copy(scri[:, 0:D], z0)
            v.tensor_copy(scri2[:, 0:D], z1)
            v.tensor_copy(q0, scri[:, 0:D])
            v.tensor_copy(q1, scri2[:, 0:D])
            v.tensor_tensor(scr[:, 0:D], z0, q0, ALU.is_lt)
            v.tensor_tensor(scri[:, 0:D], z1, q1, ALU.is_lt)
            v.tensor_sub(q0, q0, scr[:, 0:D])
            inst = v.tensor_sub(q1, q1, scri[:, 0:D])
            p.fin(inst, "SV")


def _fin(p, g):
    nc = p.nc
    outs = []
    for name, tensor, shape, dt in p.taps:
        if tensor is None or isinstance(dt, str):
            continue
        h = nc.dram_tensor("tap_" + name, list(shape), dt, kind="ExternalOutput")
        outs.append((h, tensor, shape))

    @p.step("sync", waits=("SV", "SS", "ST", "SG", "SA", "GC"))
    def _(s):
        for h, tensor, shape in outs:
            p.dma(s, h.ap(), tensor[0:shape[0], 0:shape[1]], "SD")
        s.wait_ge(p.sem["SD"], p.cnt["SD"])


def kernel(**inputs):
    global LAST_RESULTS
    x = np.asarray(inputs["L0_attn_a_x"])
    wqkv_p = _perm_wqkv(np.asarray(inputs["L0_attn_a_Wqkv"], np.float32))
    wout = np.ascontiguousarray(np.asarray(inputs["L0_attn_a_Wout"], np.float32))
    wT = {n: np.ascontiguousarray(np.asarray(inputs[f"L0_attn_b_{m}"], np.float32).T)
          for n, m in (("wqT", "Wq"), ("wkT", "Wk"), ("wvT", "Wv"),
                       ("woT", "Wo"), ("w1T", "W1"), ("w2T", "W2"))}
    frhs = np.ascontiguousarray(np.asarray(inputs["L0_ffn_rhs"], np.float32))
    ball = np.zeros((128, 48), np.float32)
    for bi, nm in enumerate(("bq", "bk", "bv", "bo", "b1", "b2")):
        ball[:, bi * 8:(bi + 1) * 8] = np.asarray(
            inputs[f"L0_attn_b_{nm}"], np.float32).reshape(8, 128).T
    relb = np.asarray(inputs["L0_attn_b_rel_bias"], np.float32)

    in_maps = []
    for c in range(8):
        b, half = divmod(c, 2)
        t0 = half * TK
        xf = np.zeros((D, TKH), np.float32)
        if t0 >= HALO:
            xf[:, :] = x[b, t0 - HALO: t0 + TK, :].T
        else:
            xf[:, HALO:] = x[b, t0: t0 + TK, :].T
        cst = _host_consts(half)
        in_maps.append(dict(
            xf=np.ascontiguousarray(xf), wqkv=wqkv_p, wout=wout, frhs=frhs,
            ball=ball, relb=relb, cosT=cst["cosT"], sinT=cst["sinT"],
            cosTl=cst["cosTl"], sinTl=cst["sinTl"],
            amask=cst["amask"], edg=cst["edg"],
            trirev=cst["trirev"], trirevf=cst["trirev"].astype(np.float32),
            indwb=cst["indwb"], ohsel=cst["ohsel"],
            e2n=cst["e2n"], farind=cst["farind"],
            relbh=np.ascontiguousarray(relb[:, half * 8:(half + 1) * 8]),
            bvrow=np.asarray(inputs["L0_attn_b_bv"], np.float32).reshape(1, D),
            ident=np.eye(128, dtype=np.float32), **wT))

    nc = build_nc()
    nc.compile()
    res = run_bass_kernel_spmd(nc, in_maps, core_ids=list(range(8)))
    LAST_RESULTS = res
    out = np.empty((B, T, D), np.float32)
    for c in range(8):
        b, half = divmod(c, 2)
        out[b, half * TK:(half + 1) * TK, :] = np.asarray(res.results[c]["y"]).T
    return out


def _attn_b(p, g, G):
    stg, xh, reps, scr = G["stg"], G["xh"], G["reps"], G["scr"]

    _norm(p, g, G, src=g["strmh"], ncols=TK, precast=True)

    @p.step("sync", waits=("SV",))
    def _(s):
        for c in range(8):
            p.dma(s, g["h3h"].ap()[c * 128:(c + 1) * 128, :],
                  xh[:, c * TKH: c * TKH + TK], "SD")
    p.tap("h3", xh, (128, 8 * TKH), BF16)

    with ExitStack() as c_qkb:
        p.barrier()
        wqb = p.sb("wqb", [128, 2 * D], BF16, ctx=c_qkb)
        wvb = p.sb("wvb", [128, 8 * D], BF16, ctx=c_qkb)
        vcb = p.sb("vcb", [128, 2 * D], BF16, ctx=c_qkb)
        vcf = p.sb("vcf", [128, 2 * D], ctx=c_qkb)
        qkev = p.sb("qkev", [128, 2 * 512], BF16, ctx=c_qkb)
        bvr = p.sb("bvr", [1, D], ctx=c_qkb)
        pqa = p.psum("pqa", [128, 512], ctx=c_qkb)
        pqb = p.psum("pqb", [128, 512], ctx=c_qkb)

        @p.step("sync")
        def _(s):
            p.dma(s, bvr[:], g["bvrow"].ap(), "SD")

        st_after = {}
        for wi, (wname, bslot, dsth) in enumerate(
                (("wqT", 0, "qth"), ("wkT", 1, "kth"))):
            sv0q = p.cnt["SV"]
            si = 0
            for m in range(8):
                @p.step("gpsimd", after=(("ST", st_after.get(wi * 8 + m - 2, 0)),))
                def _(gp, m=m, wname=wname):
                    for kc in range(8):
                        p.dma(gp, wqb[:, (m % 2) * D + kc * 128:(m % 2) * D + kc * 128 + 128],
                              g[wname].ap()[kc * 128:(kc + 1) * 128, m * 128:(m + 1) * 128],
                              "SG")

                sg_m = p.cnt["SG"]
                for tt in range(4):
                    ps = pqa if si % 2 == 0 else pqb

                    @p.step("tensor", after=(("SV", max(sv0q, sv0q + si - 1)), ("SG", sg_m)))
                    def _(t, m=m, tt=tt, ps=ps):
                        for kc in range(8):
                            inst = t.matmul(
                                ps[:, 0:512],
                                wqb[:, (m % 2) * D + kc * 128:(m % 2) * D + kc * 128 + 128],
                                xh[:, kc * TKH + tt * 512: kc * TKH + tt * 512 + 512],
                                start=(kc == 0), stop=(kc == 7))
                        p.fin(inst, "ST")

                    @p.step("vector", waits=("ST", "SD"))
                    def _(v, m=m, ps=ps, bslot=bslot, si=si):
                        inst = v.tensor_scalar(qkev[:, (si % 2) * 512:(si % 2) * 512 + 512],
                                               ps[:, 0:512],
                                               G["bsb"][:, bslot * 8 + m: bslot * 8 + m + 1],
                                               None, ALU.add)
                        p.fin(inst, "SV")

                    @p.step("sync", waits=("SV",))
                    def _(s, m=m, tt=tt, dsth=dsth, si=si):
                        p.dma(s, g[dsth].ap()[m * 128:(m + 1) * 128, tt * 512:(tt + 1) * 512],
                              qkev[:, (si % 2) * 512:(si % 2) * 512 + 512], "SD")
                    si += 1
                st_after[wi * 8 + m] = p.cnt["ST"]

        @p.step("gpsimd", waits=("ST",))
        def _(gp):
            for kc in range(8):
                p.dma(gp, wvb[:, kc * D:(kc + 1) * D],
                      g["wvT"].ap()[kc * 128:(kc + 1) * 128, :], "SG")

        sv0v = p.cnt["SV"]
        sg_v = p.cnt["SG"]
        sd_v = p.cnt["SD"]
        for tc in range(16):
            for nsl in range(2):
                si = tc * 2 + nsl
                ps = pqa if si % 2 == 0 else pqb

                @p.step("tensor", after=(("SV", max(sv0v, sv0v + 2 * (si - 1))),
                                         ("SG", sg_v), ("SD", sd_v)))
                def _(t, tc=tc, nsl=nsl, ps=ps):
                    for kc in range(8):
                        t.matmul(ps[:, 0:512],
                                 xh[:, kc * TKH + tc * 128: kc * TKH + tc * 128 + 128],
                                 wvb[:, kc * D + nsl * 512: kc * D + nsl * 512 + 512],
                                 start=(kc == 0), stop=False)
                    inst = t.matmul(ps[:, 0:512], G["onesf"][0:1, 0:128],
                                    bvr[0:1, nsl * 512:(nsl + 1) * 512],
                                    start=False, stop=True)
                    p.fin(inst, "ST")

                @p.step("vector", waits=("ST",))
                def _(v, tc=tc, nsl=nsl, ps=ps):
                    v.tensor_copy(
                        vcf[:, (tc % 2) * D + nsl * 512:(tc % 2) * D + (nsl + 1) * 512],
                        ps[:, 0:512])
                    inst = v.tensor_copy(
                        vcb[:, (tc % 2) * D + nsl * 512:(tc % 2) * D + (nsl + 1) * 512],
                        ps[:, 0:512])
                    p.fin(inst, "SV", 2)

            @p.step("sync", waits=("SV",))
            def _(s, tc=tc):
                p.dma(s, g["vth"].ap()[tc * 128:(tc + 1) * 128, :],
                      vcb[:, (tc % 2) * D:(tc % 2) * D + D], "SD")
                p.dma(s, g["vtf"].ap()[tc * 128:(tc + 1) * 128, :],
                      vcf[:, (tc % 2) * D:(tc % 2) * D + D], "SD")

    if SUBPH < 2:
        return
    with ExitStack() as c_ab:
        p.barrier()
        trirev_s = p.sb("trirev_s", [128, 128], ctx=c_ab)
        indw_s = p.sb("indw_s", [32, 32 * 128], BF16, ctx=c_ab)
        csumsb = p.sb("csumsb", [32, 64], BF16, ctx=c_ab)
        ohsel_s = p.sb("ohsel_s", [128, 32 * 32], ctx=c_ab)
        e2s = p.sb("e2s", [32, 512], BF16, ctx=c_ab)
        edg_s = p.sb("edg_s", [32, 185], BF16, ctx=c_ab)
        farind_s = p.sb("farind_s", [32, 1], BF16, ctx=c_ab)
        grw = p.sb("grw", [8, 1280], BF16, ctx=c_ab)
        cfr = p.sb("cfr", [1, 8], ctx=c_ab)
        ones32 = p.sb("ones32", [32, 1], ctx=c_ab)
        wev = p.sb("wev", [8, 512], BF16, ctx=c_ab)
        qblk = p.sb("qblk", [128, 8 * 256], BF16, ctx=c_ab)
        kblk = p.sb("kblk", [128, 16 * 256], BF16, ctx=c_ab)
        vb65 = p.sb("vb65", [128, 32 * 65], BF16, ctx=c_ab)
        vload = p.sb("vload", [128, 32 * 64], ctx=c_ab)
        pwin = p.sb("pwin", [128, 32 * 64], BF16, ctx=c_ab)
        ttl = p.sb("ttl", [128, 6 * 512], BF16, ctx=c_ab)
        bsb64 = p.sb("bsb64", [64, 8 * 512], ctx=c_ab)
        csums = p.sb("csums", [32, 64], ctx=c_ab)
        ptrowS = p.sb("ptrowS", [1, 64], ctx=c_ab)
        ptrowB = p.sb("ptrowB", [1, 64], BF16, ctx=c_ab)
        whrs = p.sb("whrs", [1, 512], BF16, ctx=c_ab)
        cfpt = p.sb("cfpt", [64, 1], ctx=c_ab)
        ptl = p.sb("ptlb", [128, 2 * 512], BF16, ctx=c_ab)
        dsb = p.sb("dsb_ab", [1, 2 * 512], ctx=c_ab)
        mgf = p.sb("mgf", [64, 512], ctx=c_ab)
        pnbs = p.sb("pnbs", [64, 512], ctx=c_ab)
        mgb = p.sb("mgb", [64, 512], BF16, ctx=c_ab)
        spsE = p.psum("spsE", [128, 1024], ctx=c_ab)
        spsF = p.psum("spsF", [128, 1024], ctx=c_ab)
        avb = p.psum("avb", [128, 512], ctx=c_ab)
        avb2 = p.psum("avb2", [128, 512], ctx=c_ab)
        ppw = p.psum("ppw", [128, 512], ctx=c_ab)
        pps = ppw

        @p.step("sync")
        def _(s):
            p.dma(s, trirev_s[:], g["trirevf"].ap(), "SD")
            p.dma(s, indw_s[:], g["indwb"].ap(), "SD")
            p.dma(s, e2s[:], g["e2n"].ap(), "SD")
            p.dma(s, edg_s[:], g["edg"].ap(), "SD")
            p.dma(s, farind_s[:], g["farind"].ap(), "SD")

        @p.step("gpsimd")
        def _(gp):
            gp.memset(ohsel_s[:], 0.0)
            gp.memset(ones32[:], 1.0)
            gp.memset(bass.AP(ohsel_s, 0, [[32 * 32, 128], [33, 32]]), 1.0)
            p.fin(gp.memset(grw[:], 0.0), "GC")

        @p.step("tensor", waits=("SD", "SG", "GC"))
        def _(t):
            t.matmul(ppw[0:8, 0:185], G["relbhb"][:], edg_s[:],
                     start=True, stop=True, skip_group_check=True)
            inst = t.matmul(ppw[0:1, 192:200], farind_s[:], G["relbhb"][:],
                            start=True, stop=True, skip_group_check=True)
            p.fin(inst, "ST")

        @p.step("vector", waits=("ST",))
        def _(v):
            v.tensor_copy(grw[0:8, 548:733], ppw[0:8, 0:185])
            inst = v.tensor_copy(cfr[:], ppw[0:1, 192:200])
            p.fin(inst, "SV")

        @p.step("tensor", waits=("SV",))
        def _(t):
            inst = t.matmul(ppw[0:8, 0:512], G["relbhb"][:], e2s[:],
                            start=True, stop=True, skip_group_check=True)
            p.fin(inst, "ST")

        @p.step("vector", waits=("ST",))
        def _(v):
            inst = v.tensor_copy(wev[0:8, :], ppw[0:8, 0:512])
            p.fin(inst, "SV")

        @p.step("sync", waits=("SV",))
        def _(s):
            p.dma(s, g["growh"].ap(), grw[:], "SD")
            p.dma(s, g["whrh"].ap(), wev[:], "SD")

        MSV = {}
        for m in range(8 if SUBPH >= 3 else 0):
            @p.step("sync", waits=("SV", "SS"))
            def _(s, m=m):
                for c in range(8):
                    p.dma(s, qblk[:, c * 256:(c + 1) * 256],
                          g["qth"].ap()[c * 128:(c + 1) * 128, m * 256:(m + 1) * 256],
                          "SD")
                for jkf in range(16):
                    for hb in range(2):
                        p.dma(s, kblk[hb * 64:hb * 64 + 64, jkf * 256:(jkf + 1) * 256],
                              g["kth"].ap()[jkf * 64:(jkf + 1) * 64,
                                            m * 256:(m + 1) * 256], "SD")
                p.dma(s, bass.AP(vload, 0, [[32 * 64, 128], [64, 32], [1, 64]]),
                      bass.AP(g["vtf"], m * 256 * D, [[64, 128], [8192, 32], [1, 64]]),
                      "SD")
                p.dma(s, whrs[:], g["whrh"].ap()[m:m + 1, :], "SD")
                for rr in range(6):
                    p.dma(s, ttl[:, rr * 512:(rr + 1) * 512],
                          bass.AP(g["growh"], m * 1280 + 641 - rr * 128,
                                  [[1, 128], [1, 512]]), "SD")

            @p.step("gpsimd", waits=("SV", "SS"))
            def _(gp, m=m):
                p.fin(gp.memset(vb65[:], 1.0), "GC")

            @p.step("sync", waits=("GC",))
            def _(s, m=m):
                for t2 in range(2):
                    p.dma(s, bass.AP(vb65, t2 * 16 * 65, [[32 * 65, 128], [65, 16], [1, 64]]),
                          bass.AP(g["vth"], (m * 256 + t2 * 128) * D,
                                  [[D, 128], [64, 16], [1, 64]]), "SD")

            @p.step("tensor", waits=("SD", "SV"))
            def _(t, m=m):
                for cc in range(32):
                    inst = t.matmul(pps[0:32, 0:64], ohsel_s[:, cc * 32:(cc + 1) * 32],
                                    vload[:, cc * 64:(cc + 1) * 64],
                                    start=(cc == 0), stop=(cc == 31))
                p.fin(inst, "ST")

            @p.step("vector", waits=("ST",))
            def _(v):
                v.tensor_copy(csums[:], pps[0:32, 0:64])
                inst = v.tensor_copy(csumsb[:], pps[0:32, 0:64])
                p.fin(inst, "SV")

            @p.step("tensor", waits=("SV",))
            def _(t):
                inst = t.matmul(pps[0:1, 64:128], ones32[:], csums[:],
                                start=True, stop=True, skip_group_check=True)
                p.fin(inst, "ST")

            @p.step("vector", waits=("ST",))
            def _(v):
                v.tensor_copy(ptrowS[:], pps[0:1, 64:128])
                inst = v.tensor_copy(ptrowB[:], pps[0:1, 64:128])
                p.fin(inst, "SV")

            @p.step("tensor", waits=("SV",))
            def _(t, m=m):
                inst = t.matmul(pps[0:64, 128:129], ptrowS[:], cfr[0:1, m:m + 1],
                                start=True, stop=True, skip_group_check=True)
                p.fin(inst, "ST")

            @p.step("vector", waits=("ST",))
            def _(v):
                inst = v.tensor_copy(cfpt[:], pps[0:64, 128:129])
                p.fin(inst, "SV")

            for cg in range(4):
                @p.step("tensor", waits=("SV",))
                def _(t, cg=cg):
                    t.matmul(ppw[:, 0:512], trirev_s[:],
                             vload[:, cg * 512:(cg + 1) * 512],
                             start=True, stop=False, skip_group_check=True)
                    for i in range(8):
                        cc = cg * 8 + i
                        inst = t.matmul(ppw[:, i * 64:(i + 1) * 64],
                                        indw_s[:, cc * 128:(cc + 1) * 128],
                                        csumsb[:], start=False, stop=True,
                                        skip_group_check=True)
                    p.fin(inst, "ST")

                @p.step("vector", waits=("ST",))
                def _(v, cg=cg):
                    inst = v.tensor_copy(pwin[:, cg * 512:(cg + 1) * 512], ppw[:, 0:512])
                    p.fin(inst, "SV")

            for sub in range(8):
                rrs = [rr for rr in range(6) if 0 <= sub * 4 - 1 + rr <= 31]

                @p.step("tensor", waits=("SV",))
                def _(t, m=m, sub=sub, rrs=rrs):
                    for i, rr in enumerate(rrs):
                        cc = sub * 4 - 1 + rr
                        last = (i == len(rrs) - 1) and sub != 7
                        inst = t.matmul(G["bps"][0:64, 0:512],
                                        pwin[:, cc * 64:(cc + 1) * 64],
                                        ttl[:, rr * 512:(rr + 1) * 512],
                                        start=(i == 0), stop=last,
                                        skip_group_check=True)
                    if sub == 7:
                        inst = t.matmul(G["bps"][0:64, 0:512], ptrowB[:], whrs[:],
                                        start=False, stop=True,
                                        skip_group_check=True)
                    p.fin(inst, "ST")

                @p.step("vector", waits=("ST",))
                def _(v, sub=sub):
                    inst = v.tensor_copy(bsb64[:, sub * 512:(sub + 1) * 512],
                                         G["bps"][0:64, 0:512])
                    p.fin(inst, "SV")

            p.tap(f"pwin{m}", pwin, (128, 32 * 64), BF16)
            p.tap(f"bias{m}", bsb64, (64, 8 * 512), F32)

            blk_sv = p.cnt["SV"]
            for c in range(8 if SUBPH >= 4 else 0):
                ss0c = p.cnt["SS"]
                gp_idx = m * 8 + c
                sv_pass = max(blk_sv, MSV.get(gp_idx - 2, 0))
                avbc = avb if c % 2 == 0 else avb2
                def _emit_av(kc):
                    jkf, jkt2 = kc // 2, kc % 2

                    @p.step("tensor", after=(("SS", ss0c + kc + 1),)
                            + ((("SV", sv_pass),) if kc == 0 else ()))
                    def _(t, kc=kc, jkf=jkf, jkt2=jkt2):
                        inst = t.matmul(avbc[0:65, 0:512],
                                        vb65[:, (jkt2 * 16 + jkf) * 65:
                                             (jkt2 * 16 + jkf) * 65 + 65],
                                        ptl[:, (kc % 2) * 512:(kc % 2) * 512 + 512],
                                        start=(kc == 0), stop=(kc == 31),
                                        skip_group_check=True)
                        p.fin(inst, "ST")

                for kc in range(32):
                    jkf, jkt2 = kc // 2, kc % 2
                    sE = spsE if kc % 2 == 0 else spsF

                    @p.step("tensor", after=(("SS", max(ss0c, ss0c + kc - 1)),
                                             ("SV", sv_pass)))
                    def _(t, m=m, c=c, jkf=jkf, jkt2=jkt2, sE=sE):
                        col0 = jkf * 256 + jkt2 * 128
                        t.matmul(sE[:, 256:512], kblk[0:64, col0:col0 + 128],
                                 qblk[0:64, c * 256:(c + 1) * 256],
                                 start=True, stop=True, tile_position=(0, 0),
                                 skip_group_check=True)
                        inst = t.matmul(sE[:, 512:768], kblk[64:128, col0:col0 + 128],
                                        qblk[64:128, c * 256:(c + 1) * 256],
                                        start=True, stop=True, tile_position=(64, 0),
                                        skip_group_check=True)
                        p.fin(inst, "ST")

                    @p.step("scalar", waits=("ST",))
                    def _(sc, kc=kc, sE=sE):
                        inst = sc.activation(ptl[:, (kc % 2) * 512:(kc % 2) * 512 + 512],
                                             sE[:, 256:768], AF.Exp,
                                             bias=G["csh"][:, 1:2], scale=1.0)
                        p.fin(inst, "SS")

                    if SUBPH >= 5 and kc > 0:
                        _emit_av(kc - 1)
                if SUBPH >= 5:
                    _emit_av(31)

                if SUBPH < 6:
                    continue

                @p.step("vector", waits=("ST",))
                def _(v, avbc=avbc):
                    v.tensor_copy(dsb[0:1, 0:256], avbc[64:65, 0:256])
                    v.tensor_copy(dsb[0:1, 256:512], avbc[64:65, 256:512])
                    v.reciprocal(dsb[0:1, 512:768], dsb[0:1, 0:256])
                    inst = v.reciprocal(dsb[0:1, 768:1024], dsb[0:1, 256:512])
                    p.fin(inst, "SV")

                @p.step("tensor", waits=("SV",))
                def _(t):
                    inst = t.matmul(ppw[0:64, 0:512], G["onesf"][0:1, 0:64],
                                    dsb[0:1, 512:1024], start=True, stop=True,
                                    skip_group_check=True)
                    p.fin(inst, "ST")

                @p.step("vector", waits=("ST", "SD"))
                def _(v, m=m, c=c, avbc=avbc):
                    v.tensor_copy(pnbs[:, 0:256], ppw[0:64, 0:256])
                    v.tensor_copy(pnbs[:, 256:512], ppw[0:64, 256:512])
                    v.tensor_mul(mgf[:, 0:256], avbc[0:64, 0:256], pnbs[:, 0:256])
                    v.tensor_mul(mgf[:, 256:512], avbc[0:64, 256:512],
                                 pnbs[:, 256:512])
                    for h2 in range(2):
                        jqf = 2 * c + h2
                        inst = v.scalar_tensor_tensor(
                            mgb[:, h2 * 256:(h2 + 1) * 256],
                            mgf[:, h2 * 256:(h2 + 1) * 256],
                            cfpt[:],
                            bass.AP(bsb64, jqf, [[8 * 512, 64], [16, 256]]),
                            ALU.add, ALU.add)
                    p.fin(inst, "SV")

                @p.step("sync", waits=("SV",))
                def _(s, m=m, c=c):
                    for h2 in range(2):
                        jqf = 2 * c + h2
                        p.dma(s, g["atth"].ap()[jqf * 64:(jqf + 1) * 64,
                                                m * 256:(m + 1) * 256],
                              mgb[:, h2 * 256:(h2 + 1) * 256], "SD")
                MSV[m * 8 + c] = p.cnt["SV"]

    p.tap("attnraw", None, (D, TK), "hbm:atth")

    with ExitStack() as c_out:
        p.barrier()
        xcast = p.sb("xcast", [128, 8 * TK], BF16, ctx=c_out)
        xcast2 = p.sb("xcast2", [128, 8 * TK], BF16, ctx=c_out)
        wob = p.sb("wob", [128, 2 * D], BF16, ctx=c_out)
        h3st = p.sb("h3st", [128, 2 * 512], BF16, ctx=c_out)
        poa = p.psum("poa", [128, 512], ctx=c_out)
        pob = p.psum("pob", [128, 512], ctx=c_out)

        @p.step("sync")
        def _(s):
            for c in range(8):
                p.dma(s, xh[:, c * TKH: c * TKH + TK],
                      g["atth"].ap()[c * 128:(c + 1) * 128, :], "SD")

        def _proj_inner(wname, bslot, rhs_t, add_t, out_cast, out_hbm):
            st_after = {}
            sv0o = p.cnt["SV"]
            sd0o = p.cnt["SD"]
            si = 0
            for m in range(8):
                @p.step("gpsimd", after=(("ST", st_after.get(m - 2, 0)),))
                def _(gp, m=m, wname=wname):
                    for kc in range(8):
                        p.dma(gp, wob[:, (m % 2) * D + kc * 128:(m % 2) * D + kc * 128 + 128],
                              g[wname].ap()[kc * 128:(kc + 1) * 128, m * 128:(m + 1) * 128],
                              "SG")

                sg_m = p.cnt["SG"]
                for tt in range(4):
                    ps = poa if si % 2 == 0 else pob

                    @p.step("tensor", after=(("SV", max(sv0o, sv0o + 2 * (si - 1))),
                                             ("SG", sg_m), ("SD", sd0o)))
                    def _(t, m=m, tt=tt, ps=ps, rhs_t=rhs_t):
                        for kc in range(8):
                            inst = t.matmul(
                                ps[:, 0:512],
                                wob[:, (m % 2) * D + kc * 128:(m % 2) * D + kc * 128 + 128],
                                rhs_t[:, kc * (TKH if rhs_t is xh else TK) + tt * 512:
                                      kc * (TKH if rhs_t is xh else TK) + tt * 512 + 512],
                                start=(kc == 0), stop=(kc == 7))
                        p.fin(inst, "ST")

                    if add_t is not None and isinstance(add_t, str):
                        @p.step("sync", waits=("SV",))
                        def _(s, m=m, tt=tt, si=si, add_t=add_t):
                            p.dma(s, h3st[:, (si % 2) * 512:(si % 2) * 512 + 512],
                                  g[add_t].ap()[m * 128:(m + 1) * 128, tt * 512:(tt + 1) * 512],
                                  "SD")

                    @p.step("vector", waits=("ST", "SD"))
                    def _(v, m=m, tt=tt, si=si, ps=ps, bslot=bslot, add_t=add_t,
                          out_cast=out_cast):
                        bvec = G["bsb"][:, bslot * 8 + m: bslot * 8 + m + 1]
                        if add_t is None:
                            at, ab = None, 0
                        elif isinstance(add_t, str):
                            at, ab = h3st, (si % 2) * 512
                        elif add_t is G["xh"]:
                            at, ab = add_t, m * TKH + tt * 512
                        else:
                            at, ab = add_t, m * TK + tt * 512
                        for hh in range(2):
                            hs = slice(hh * 256, hh * 256 + 256)
                            if at is not None:
                                v.scalar_tensor_tensor(
                                    stg[:, hs], ps[:, hs], bvec,
                                    at[:, ab + hh * 256: ab + hh * 256 + 256],
                                    ALU.add, ALU.add)
                            else:
                                v.tensor_scalar(stg[:, hs], ps[:, hs],
                                                bvec, None, ALU.add)
                        ob = m * TK + tt * 512
                        v.tensor_copy(out_cast[:, ob:ob + 256], stg[:, 0:256])
                        inst = v.tensor_copy(out_cast[:, ob + 256:ob + 512],
                                             stg[:, 256:512])
                        p.fin(inst, "SV", 2)

                    @p.step("sync", waits=("SV",))
                    def _(s, m=m, tt=tt, out_hbm=out_hbm):
                        if out_hbm is not None:
                            p.dma(s, g[out_hbm].ap()[m * 128:(m + 1) * 128,
                                                     tt * 512:(tt + 1) * 512],
                                  stg[:, 0:512], "SD")
                    si += 1
                st_after[m] = p.cnt["ST"]

        _proj_inner("woT", 3, xh, "h3h", xcast, "s1h")
        if "s1" in DEBUG_TAPS:
            hs1 = p.nc.dram_tensor("tap_s1", [D, TK], F32, kind="ExternalOutput")

            @p.step("sync", waits=("SV", "SS", "ST", "GC"))
            def _(s):
                p.dma(s, hs1.ap(), g["s1h"].ap(), "SD")

        _norm(p, g, G, src=g["s1h"], ncols=TK, precast=True, eps=0.0,
              pcsrc=xcast)
        p.tap("attblk", xh, (128, 8 * TKH), BF16)

        _proj_inner("w1T", 4, xh, None, xcast, None)

        _proj_inner("w2T", 5, xcast, xh, xcast2, "s1h")

        _norm(p, g, G, src=g["s1h"], ncols=TK, precast=True, eps=0.0,
              pcsrc=xcast2)

        for j in range(4):
            c0, c1 = 2 * j, 2 * j + 1

            @p.step("sync", waits=("SV",))
            def _(s, c0=c0, c1=c1):
                p.dma(s, stg[:, 0:TK],
                      g["strmh"].ap()[c0 * 128:(c0 + 1) * 128, :], "SD")
                p.dma(s, stg[:, TKH:TKH + TK],
                      g["strmh"].ap()[c1 * 128:(c1 + 1) * 128, :], "SD")

            @p.step("vector", waits=("SD",))
            def _(v, c0=c0, c1=c1):
                v.tensor_add(stg[:, 0:TK], stg[:, 0:TK],
                             xh[:, c0 * TKH: c0 * TKH + TK])
                v.tensor_add(stg[:, TKH:TKH + TK], stg[:, TKH:TKH + TK],
                             xh[:, c1 * TKH: c1 * TKH + TK])
                v.tensor_copy(xh[:, c0 * TKH: c0 * TKH + TK], stg[:, 0:TK])
                inst = v.tensor_copy(xh[:, c1 * TKH: c1 * TKH + TK],
                                     stg[:, TKH:TKH + TK])
                p.fin(inst, "SV")

            @p.step("sync", waits=("SV",))
            def _(s, c0=c0, c1=c1):
                p.dma(s, g["strmh"].ap()[c0 * 128:(c0 + 1) * 128, :],
                      stg[:, 0:TK], "SD")
                p.dma(s, g["strmh"].ap()[c1 * 128:(c1 + 1) * 128, :],
                      stg[:, TKH:TKH + TK], "SD")
    p.tap("x3", None, (D, TK), "hbm:strmh")





# version 16
# speedup vs baseline: 1.0114x; 1.0114x over previous
import math
from contextlib import ExitStack

import numpy as np
import ml_dtypes

import concourse.bass as bass
import concourse.mybir as mybir
from concourse.bass import ds
from concourse.bass_utils import run_bass_kernel_spmd
from concourse.bacc import Bacc

F32 = mybir.dt.float32
BF16 = mybir.dt.bfloat16
I32 = mybir.dt.int32
AF = mybir.ActivationFunctionType
ALU = mybir.AluOpType
AX = mybir.AxisListType

B, T, D = 4, 4096, 1024
TK = 2048
HALO = 256
TKH = TK + HALO
CA = 10.0
CB = 80.0
EPS = 1e-5
WIN = 256

DEBUG_TAPS = []
PHASES = 99
SUBPH = 99
LAST_RESULTS = None

BF = ml_dtypes.bfloat16


def _bucket(rel):
    n = -np.asarray(rel)
    nb = 16
    ret = (n < 0).astype(np.int64) * nb
    n = np.abs(n)
    mx = nb // 2
    small = n < mx
    with np.errstate(divide="ignore"):
        large = mx + (np.log(np.maximum(n, 1).astype(np.float64) / mx)
                      / math.log(128.0 / mx) * (nb - mx)).astype(np.int64)
    return ret + np.where(small, n, np.minimum(large, nb - 1))


_CCACHE = {}


def _host_consts(half):
    if half in _CCACHE:
        return _CCACHE[half]
    t0 = half * TK
    o = {}
    inv_freq = 1.0 / 10000.0 ** (np.arange(0, 64, 2, dtype=np.float64) / 64)
    tpos = np.arange(t0 - HALO, t0 + TK, dtype=np.float64)
    fr = np.outer(inv_freq, tpos)
    c64 = np.tile(np.cos(fr), (8, 1))
    s64 = np.tile(np.sin(fr), (8, 1))
    o["cosT"] = c64.astype(np.float32)
    o["cosTl"] = (c64 - o["cosT"].astype(np.float64)).astype(np.float32)
    o["sinT"] = s64.astype(np.float32)
    o["sinTl"] = (s64 - o["sinT"].astype(np.float64)).astype(np.float32)
    masks = np.zeros((8, 128, 512), np.float32)
    for j in range(6):
        dd = (HALO - j * 128) + np.arange(512)[None, :] - np.arange(128)[:, None]
        masks[j] = ((dd >= 0) & (dd <= WIN - 1)).astype(np.float32)
    for sl, j in ((6, 0), (7, 1)):
        kg = t0 - HALO + j * 128 + np.arange(128)[:, None]
        dd = t0 + np.arange(512)[None, :] - kg
        masks[sl] = ((dd >= 0) & (dd <= WIN - 1) & (kg >= 0)).astype(np.float32)
    o["amask"] = masks.reshape(8 * 128, 512).astype(BF)
    wbase = (t0 - 92) // 128
    indw = np.zeros((32, 20), np.float32)
    for w in range(19):
        gch = wbase + w
        if 0 <= gch < 32:
            indw[:gch, w] = 1.0
    indw[:, 19] = 1.0
    o["indw"] = indw.astype(BF)
    o["indwB"] = np.ascontiguousarray(
        np.concatenate([np.repeat(indw[:, w:w + 1], 128, 1) for w in range(19)], 1)
    ).astype(np.float32)
    ohsel = np.zeros((128, 32 * 32), np.float32)
    for gg in range(32):
        ohsel[:, gg * 32 + gg] = 1.0
    o["ohsel"] = ohsel.astype(BF)
    o["wbase"] = wbase
    e2 = np.zeros((32, TK), np.float32)
    dd = t0 + np.arange(TK) - 4096
    bk = _bucket(dd)
    for n in np.nonzero(dd > -91)[0]:
        e2[bk[n], n] += 1.0
        e2[15, n] -= 1.0
    o["e2"] = e2.astype(BF)
    edg = np.zeros((32, 185), np.float32)
    for di in range(185):
        d = di - 92
        edg[_bucket(d), di] += 1.0
        edg[_bucket(d - 1), di] -= 1.0
    o["edg"] = edg.astype(BF)
    o["trirev"] = (np.arange(128)[:, None] <= (127 - np.arange(128))[None, :]).astype(BF)
    e2n = np.zeros((32, 512), np.float32)
    dn = np.arange(512) - 512
    bkn = _bucket(dn)
    far = int(_bucket(np.asarray([-3000]))[0])
    for n in range(512):
        if bkn[n] != far:
            e2n[bkn[n], n] += 1.0
            e2n[far, n] -= 1.0
    o["e2n"] = e2n.astype(BF)
    fi = np.zeros((32, 1), np.float32)
    fi[far, 0] = 1.0
    o["farind"] = fi.astype(BF)
    iw = np.zeros((32, 32 * 128), np.float32)
    for cc in range(32):
        iw[:cc, cc * 128:(cc + 1) * 128] = 1.0
    o["indwb"] = iw.astype(BF)
    cf = np.zeros((512, 8 * 128), np.float32)
    for pr in range(8):
        for m in range(128):
            cf[15 * 16 + 2 * pr + (m >= 64), pr * 128 + m] = 1.0
    o["cfsel"] = cf.astype(BF)
    _CCACHE[half] = o
    return o


def _perm_wqkv(wqkv):
    cols = []
    for part in (0, 1):
        base = part * 512
        lo, hi = [], []
        for h in range(8):
            lo.extend(range(base + h * 64, base + h * 64 + 32))
            hi.extend(range(base + h * 64 + 32, base + h * 64 + 64))
        cols.extend(lo + hi)
    cols.extend(range(1024, 1536))
    return np.ascontiguousarray(wqkv[:, cols])


class P:
    def __init__(self, nc, ctx, blk):
        self.nc, self.ctx, self.blk = nc, ctx, blk
        self.sem = {}
        self.cnt = {}
        for s in ("SD", "SG", "SA", "ST", "SV", "SS", "CC", "GC"):
            self.sem[s] = ctx.enter_context(nc.semaphore(s))
            self.cnt[s] = 0
        self.taps = []

    def sb(self, name, shape, dt=F32, ctx=None):
        return (ctx or self.ctx).enter_context(self.nc.sbuf_tensor(name, shape, dt))

    def psum(self, name, shape, ctx=None):
        return (ctx or self.ctx).enter_context(self.nc.psum_tensor(name, shape, F32))

    def step(self, engine, waits=(), after=()):
        sec = getattr(self.blk, engine)
        pp = self

        def deco(body):
            wv = [(s, pp.cnt[s]) for s in waits] + list(after)

            def run(eng):
                for s, v in wv:
                    if v > 0:
                        eng.wait_ge(pp.sem[s], v)
                body(eng)
            sec(run)
        return deco

    def fin(self, inst, sig, n=1):
        inst.then_inc(self.sem[sig], n)
        self.cnt[sig] += n

    def dma(self, eng, out, in_, sig, **kw):
        self.fin(eng.dma_start(out=out, in_=in_, **kw), sig, 16)

    def tap(self, name, tensor, shape, dt):
        if name in DEBUG_TAPS:
            if tensor is None:
                self.taps.append((name, tensor, shape, dt))
                return
            h = self.nc.dram_tensor("tap_" + name, list(shape), dt,
                                    kind="ExternalOutput")
            pp = self

            @self.step("sync", waits=("SV", "SS", "ST", "GC"))
            def _(s):
                pp.dma(s, h.ap(), tensor[0:shape[0], 0:shape[1]], "SD")

    def barrier(self):
        allsems = ("SD", "SG", "SA", "ST", "SV", "SS", "GC")
        for e in ("sync", "gpsimd", "scalar", "tensor", "vector"):
            @self.step(e, waits=allsems)
            def _(eng):
                pass


def build_nc():
    nc = Bacc("TRN2", target_bir_lowering=False, detect_race_conditions=False)
    g = {}

    def din(name, shape, dt=F32):
        g[name] = nc.dram_tensor(name, shape, dt, kind="ExternalInput")

    din("xf", [D, TKH])
    din("wqkv", [D, 1536])
    din("wout", [512, D])
    for n in ("wqT", "wkT", "wvT", "woT", "w1T", "w2T", "frhs"):
        din(n, [D, D])
    din("ball", [128, 48])
    din("relb", [32, 16])
    din("cosT", [256, TKH])
    din("cosTl", [256, TKH])
    din("sinT", [256, TKH])
    din("sinTl", [256, TKH])
    din("amask", [8 * 128, 512], BF16)
    din("edg", [32, 185], BF16)
    din("trirev", [128, 128], BF16)
    din("trirevf", [128, 128])
    din("ident", [128, 128])
    din("indwb", [32, 32 * 128], BF16)
    din("ohsel", [128, 32 * 32], BF16)
    din("e2n", [32, 512], BF16)
    din("farind", [32, 1], BF16)
    din("relbh", [32, 8])
    din("bvrow", [1, D])
    g["y"] = nc.dram_tensor("y", [D, TK], F32, kind="ExternalOutput")
    g["strmh"] = nc.dram_tensor("strmh", [D, TK], F32)
    g["h3h"] = nc.dram_tensor("h3h", [D, TK], BF16)
    g["qth"] = nc.dram_tensor("qth", [D, TK], BF16)
    g["atth"] = nc.dram_tensor("atth", [D, TK], BF16)
    g["growh"] = nc.dram_tensor("growh", [8, 1280], BF16)
    g["kth"] = nc.dram_tensor("kth", [D, TK], BF16)
    g["vth"] = nc.dram_tensor("vth", [TK, D], BF16)
    g["vtf"] = nc.dram_tensor("vtf", [TK, D], F32)
    g["whrh"] = nc.dram_tensor("whrh", [8, 512], BF16)
    g["rowh"] = nc.dram_tensor("rowh", [16, TK], F32)
    g["s1h"] = nc.dram_tensor("s1h", [D, TK], F32)

    with ExitStack() as ctx:
        blk = ctx.enter_context(nc.Block())
        p = P(nc, ctx, blk)
        _emit(p, g, ctx)
    return nc


def _emit(p, g, ctx):
    G = {}
    G["bsb"] = p.sb("bsb", [128, 48])
    G["onesb"] = p.sb("onesb", [128, 1], BF16)
    G["onesf"] = p.sb("onesf", [1, 128])
    G["onesm"] = p.sb("onesm", [128, 64])
    G["csh"] = p.sb("csh", [128, 4])
    G["relbb"] = p.sb("relbb", [32, 16], BF16)
    G["relbhb"] = p.sb("relbhb", [32, 8], BF16)
    G["stg"] = p.sb("stg", [128, 2 * TKH])
    G["xtm"] = p.sb("xtm", [128, 8 * D], BF16)
    G["stt"] = p.sb("stt", [128, 128])
    G["stt2"] = p.sb("stt2", [128, 40])
    G["rowb"] = p.sb("rowb", [128, 128])
    G["rowf"] = p.sb("rowf", [1, TKH])
    G["lssav"] = p.sb("lssav", [18, 128])
    G["ident"] = p.sb("identsb", [128, 128])
    G["reps"] = p.sb("reps", [128, 2 * TKH])
    G["scr"] = p.sb("scr", [128, TKH])
    G["xh"] = p.sb("xh", [128, 8 * TKH], BF16)
    G["bps"] = p.psum("bps", [128, 512])

    @p.step("gpsimd")
    def _(gp):
        gp.memset(G["onesb"][:], 1.0)
        gp.memset(G["onesf"][:], 1.0)
        gp.memset(G["onesm"][:], 1.0)
        gp.memset(G["csh"][:, 0:1], -CA)
        gp.memset(G["csh"][:, 2:3], EPS)
        gp.memset(G["csh"][:, 3:4], 0.0)
        gp.memset(G["stt"][:], 0.0)
        gp.memset(G["stt2"][:], 0.0)
        gp.memset(G["rowb"][:], 0.0)
        gp.memset(G["rowf"][:], 0.0)
        p.fin(gp.memset(G["csh"][:, 1:2], -CB), "GC")
        p.dma(gp, G["relbb"][:], g["relb"].ap(), "SG")
        p.dma(gp, G["relbhb"][:], g["relbh"].ap(), "SG")

    @p.step("sync")
    def _(s):
        p.dma(s, G["bsb"][:], g["ball"].ap(), "SD")
        p.dma(s, G["ident"][:], g["ident"].ap(), "SD")

    _norm(p, g, G, src=g["xf"], ncols=TKH, precast=False)
    p.tap("h1", G["xh"], (128, 8 * TKH), BF16)
    p.tap("dbg_stt", G["stt"], (128, 128), F32)
    p.tap("dbg_stt2", G["stt2"], (128, 40), F32)
    p.tap("dbg_xtm", G["xtm"], (128, 4 * D), BF16)
    p.tap("dbg_reps", G["reps"], (128, 2 * TKH), F32)
    if PHASES < 2:
        return _fin(p, g)

    with ExitStack() as c_aa:
        atta = p.sb("atta", [128, 4 * TK], BF16, ctx=c_aa)
        qlo = p.sb("qlo", [128, 2 * TKH], BF16, ctx=c_aa)
        qhi = p.sb("qhi", [128, 2 * TKH], BF16, ctx=c_aa)
        klo = p.sb("klo", [128, 2 * TKH], BF16, ctx=c_aa)
        khi = p.sb("khi", [128, 2 * TKH], BF16, ctx=c_aa)
        vta = p.sb("vta", [128, 18 * 512], BF16, ctx=c_aa)

        with ExitStack() as c_qkv:
            p.barrier()
            wqm = p.sb("wqm", [128, 2 * 1024], BF16, ctx=c_qkv)
            wqv = p.sb("wqv", [128, 8 * 512], BF16, ctx=c_qkv)
            HTK = TKH // 2
            csb = p.sb("csb", [128, HTK], ctx=c_qkv)
            snb = p.sb("snb", [128, HTK], ctx=c_qkv)
            rs1 = G["scr"]
            rs2 = G["stg"]
            pA = p.psum("pA", [128, 512], ctx=c_qkv)
            pB = p.psum("pB", [128, 512], ctx=c_qkv)

            @p.step("gpsimd")
            def _(gp):
                for kc in range(8):
                    p.dma(gp, wqv[:, kc * 512:(kc + 1) * 512],
                          g["wqkv"].ap()[kc * 128:(kc + 1) * 128, 1024:1536], "SG")

            NT5 = [(0, 512), (512, 512), (1024, 512), (1536, 512), (2048, 256)]
            sv0 = p.cnt["SV"]
            st_after = {}
            si = 0
            for m in range(8):
                dst = (qlo, qhi, klo, khi)[m // 2]

                @p.step("gpsimd", after=(("ST", st_after.get(m - 2, 0)),))
                def _(gp, m=m):
                    for kc in range(8):
                        p.dma(gp, wqm[:, (m % 2) * 1024 + kc * 128: (m % 2) * 1024 + kc * 128 + 128],
                              g["wqkv"].ap()[kc * 128:(kc + 1) * 128, m * 128:(m + 1) * 128],
                              "SG")

                sg_m = p.cnt["SG"]
                for (tc0, tn) in NT5:
                    ps = pA if si % 2 == 0 else pB

                    @p.step("tensor", after=(("SV", max(sv0, sv0 + si - 1)), ("SG", sg_m)))
                    def _(t, m=m, tc0=tc0, tn=tn, ps=ps):
                        for kc in range(8):
                            inst = t.matmul(ps[:, 0:tn],
                                            wqm[:, (m % 2) * 1024 + kc * 128: (m % 2) * 1024 + kc * 128 + 128],
                                            G["xh"][:, kc * TKH + tc0: kc * TKH + tc0 + tn],
                                            start=(kc == 0), stop=(kc == 7))
                        p.fin(inst, "ST")

                    @p.step("vector", waits=("ST",))
                    def _(v, dst=dst, m=m, tc0=tc0, tn=tn, ps=ps):
                        inst = v.tensor_copy(dst[:, (m % 2) * TKH + tc0: (m % 2) * TKH + tc0 + tn],
                                             ps[:, 0:tn])
                        p.fin(inst, "SV")
                    si += 1
                st_after[m] = p.cnt["ST"]

            sg_v = p.cnt["SG"]
            for tc in range(18):
                ps = pA if si % 2 == 0 else pB

                @p.step("tensor", after=(("SV", sv0 + si - 1), ("SG", sg_v)))
                def _(t, tc=tc, ps=ps):
                    for kc in range(8):
                        inst = t.matmul(ps[:, 0:512],
                                        G["xh"][:, kc * TKH + tc * 128: kc * TKH + tc * 128 + 128],
                                        wqv[:, kc * 512:(kc + 1) * 512],
                                        start=(kc == 0), stop=(kc == 7))
                    p.fin(inst, "ST")

                @p.step("vector", waits=("ST",))
                def _(v, tc=tc, ps=ps):
                    inst = v.tensor_copy(vta[:, tc * 512:(tc + 1) * 512], ps[:, 0:512])
                    p.fin(inst, "SV")
                si += 1

            for c in range(2):
                for hf in range(2):
                    @p.step("sync", waits=("SV",))
                    def _(s, c=c, hf=hf):
                        h0 = hf * HTK
                        p.dma(s, csb[:],
                              g["cosT"].ap()[c * 128:(c + 1) * 128, h0:h0 + HTK], "SD")
                        p.dma(s, snb[:],
                              g["sinT"].ap()[c * 128:(c + 1) * 128, h0:h0 + HTK], "SD")
                        p.dma(s, rs1[:, 0:HTK],
                              g["cosTl"].ap()[c * 128:(c + 1) * 128, h0:h0 + HTK], "SD")
                        p.dma(s, rs2[:, 0:HTK],
                              g["sinTl"].ap()[c * 128:(c + 1) * 128, h0:h0 + HTK], "SD")

                    @p.step("vector", waits=("SD",))
                    def _(v, c=c, hf=hf):
                        v.tensor_add(snb[:], snb[:], rs2[:, 0:HTK])
                        v.tensor_add(csb[:], csb[:], rs1[:, 0:HTK])
                        sl = slice(c * TKH + hf * HTK, c * TKH + (hf + 1) * HTK)
                        for lo, hi in ((qlo, qhi), (klo, khi)):
                            v.tensor_mul(rs1[:, 0:HTK], lo[:, sl], snb[:])
                            v.tensor_mul(rs2[:, 0:HTK], hi[:, sl], snb[:])
                            v.tensor_mul(lo[:, sl], lo[:, sl], csb[:])
                            v.tensor_mul(hi[:, sl], hi[:, sl], csb[:])
                            v.tensor_sub(lo[:, sl], lo[:, sl], rs2[:, 0:HTK])
                            inst = v.tensor_add(hi[:, sl], hi[:, sl], rs1[:, 0:HTK])
                        p.fin(inst, "SV")

        p.tap("qlo", qlo, (128, 2 * TKH), BF16)
        p.tap("khi", khi, (128, 2 * TKH), BF16)
        p.tap("vta", vta, (128, 18 * 512), BF16)
        if PHASES < 3:
            return _fin(p, g)

        with ExitStack() as c_att:
            p.barrier()
            msk = p.sb("msk", [128, 8 * 512], BF16, ctx=c_att)
            ptl = p.sb("ptl", [128, 4 * 512], BF16, ctx=c_att)
            dsb = p.sb("dsb", [128, 2 * 512], ctx=c_att)
            saps = [p.psum(f"saps{i}", [128, 512], ctx=c_att) for i in range(4)]
            avps = [p.psum(f"avps{i}", [128, 512], ctx=c_att) for i in range(2)]
            dnps = p.psum("dnps", [128, 512], ctx=c_att)
            rpps = G["bps"]

            @p.step("sync")
            def _(s):
                for sl in range(8):
                    p.dma(s, msk[:, sl * 512:(sl + 1) * 512],
                          g["amask"].ap()[sl * 128:(sl + 1) * 128, :], "SD")

            @p.step("gpsimd")
            def _(gp):
                p.fin(gp.memset(dsb[:], 1.0), "GC")

            for hg in range(2):
                for qt in range(4):
                    _attn_a_tile(p, G, hg, qt, qlo, qhi, klo, khi, vta, msk,
                                 ptl, saps, avps, dnps, rpps, dsb, atta)

        p.tap("atta", atta, (128, 4 * TK), BF16)
        if PHASES < 4:
            return _fin(p, g)

        with ExitStack() as c_wo:
            p.barrier()
            wos = p.sb("wos", [128, 4 * D], BF16, ctx=c_wo)
            ppa = p.psum("ppra", [128, 512], ctx=c_wo)
            ppb = p.psum("pprb", [128, 512], ctx=c_wo)

            @p.step("gpsimd")
            def _(gp):
                for kc in range(4):
                    p.dma(gp, wos[:, kc * D:(kc + 1) * D],
                          g["wout"].ap()[kc * 128:(kc + 1) * 128, :], "SG")

            def wout_mm(t, m, n0, nn, ps):
                for kc in range(4):
                    inst = t.matmul(ps[:, 0:nn],
                                    wos[:, kc * D + m * 128: kc * D + m * 128 + 128],
                                    atta[:, kc * TK + n0: kc * TK + n0 + nn],
                                    start=(kc == 0), stop=(kc == 3))
                return inst

            _proj_residual(p, g, G, mm=wout_mm, src_res=g["xf"], src_off=HALO,
                           ppa=ppa, ppb=ppb, fscale=None, wait_w="SG")

    if PHASES < 5:
        return _fin(p, g)

    G["qrhs"] = p.sb("qrhs", [128, 8 * D], BF16)
    G["rsv"] = p.sb("rsv", [128, 8])
    with ExitStack() as c_f1:
        p.barrier()
        h2q = p.sb("h2q", [128, 8 * TK], BF16, ctx=c_f1)
        qrhs, rsv = G["qrhs"], G["rsv"]
        G["scrq"] = p.sb("scrq_f1", [128, 512], ctx=c_f1)
        G["scri"] = p.sb("scri_f1", [128, TK], I32, ctx=c_f1)
        G["scri2"] = p.sb("scri2_f1", [128, TK], I32, ctx=c_f1)

        _norm(p, g, G, src=g["strmh"], ncols=TK, precast=True, quant_out=h2q)
        p.tap("h2q", h2q, (128, 8 * TK), BF16)
        p.tap("dbg_2reps", G["reps"], (128, 2 * TKH), F32)
        p.tap("dbg_2stt", G["stt"], (128, 128), F32)
        _quant_rhs(p, g, G, c_f1)
        p.tap("qrhs", qrhs, (128, 8 * D), BF16)
        p.tap("rsv", rsv, (128, 8), F32)
        if PHASES >= 6:
            with ExitStack() as c_mm:
                p.barrier()
                ppa = p.psum("pfa", [128, 512], ctx=c_mm)
                ppb = p.psum("pfb", [128, 512], ctx=c_mm)

                def ffn_mm(t, m, n0, nn, ps):
                    for kc in range(8):
                        inst = t.matmul(ps[:, 0:nn],
                                        qrhs[:, kc * D + m * 128: kc * D + m * 128 + 128],
                                        h2q[:, kc * TK + n0: kc * TK + n0 + nn],
                                        start=(kc == 0), stop=(kc == 7))
                    return inst

                _proj_residual(p, g, G, mm=ffn_mm, src_res=g["strmh"], src_off=0,
                               ppa=ppa, ppb=ppb, fscale="ffn", wait_w=None)
    if PHASES < 7:
        return _fin(p, g)

    _attn_b(p, g, G)
    if PHASES < 8:
        return _fin(p, g)
    with ExitStack() as c_f2:
        p.barrier()
        h4q = p.sb("h4q", [128, 8 * TK], BF16, ctx=c_f2)
        G["scrq"] = p.sb("scrq_f2", [128, 512], ctx=c_f2)
        G["scri"] = p.sb("scri_f2", [128, TK], I32, ctx=c_f2)
        G["scri2"] = p.sb("scri2_f2", [128, TK], I32, ctx=c_f2)
        _norm(p, g, G, src=g["strmh"], ncols=TK, precast=True, quant_out=h4q)
        ppa = p.psum("pf2a", [128, 512], ctx=c_f2)
        ppb = p.psum("pf2b", [128, 512], ctx=c_f2)

        def ffn2_mm(t, m, n0, nn, ps):
            for kc in range(8):
                inst = t.matmul(ps[:, 0:nn],
                                G["qrhs"][:, kc * D + m * 128: kc * D + m * 128 + 128],
                                h4q[:, kc * TK + n0: kc * TK + n0 + nn],
                                start=(kc == 0), stop=(kc == 7))
            return inst

        _proj_residual(p, g, G, mm=ffn2_mm, src_res=g["strmh"], src_off=0,
                       ppa=ppa, ppb=ppb, fscale="ffn", wait_w=None, dst=g["y"])
    return _fin(p, g)


def _norm(p, g, G, src, ncols, precast, quant_out=None, eps=EPS, pcsrc=None):
    stg, xtm, stt = G["stg"], G["xtm"], G["stt"]
    reps, scr = G["reps"], G["scr"]
    xh = G["xh"]
    pc = pcsrc if pcsrc is not None else xh
    pcw = TK if pcsrc is not None else TKH
    nt = (ncols + 127) // 128

    if not precast:
        for c in range(8):
            @p.step("sync", waits=("SV",))
            def _(s, c=c):
                p.dma(s, stg[:, (c % 2) * TKH:(c % 2) * TKH + ncols],
                      src.ap()[c * 128:(c + 1) * 128, 0:ncols], "SD")

            @p.step("vector", waits=("SD",))
            def _(v, c=c):
                inst = v.tensor_copy(xh[:, c * TKH:c * TKH + ncols],
                                     stg[:, (c % 2) * TKH:(c % 2) * TKH + ncols])
                p.fin(inst, "SV")

    ngr = (nt + 3) // 4
    svq = {}
    for tg in range(ngr):
        span = min(512, ncols - tg * 512)
        nti = (span + 127) // 128
        xoff = (tg % 2) * 4 * D
        svgate = svq[tg - 2] if tg >= 2 else p.cnt["SV"]

        @p.step("scalar", after=(("SV", svgate),))
        def _(sc, tg=tg, span=span, xoff=xoff):
            nb = span // 128
            for c in range(8):
                p.dma(sc, bass.AP(xtm, xoff + c * 512, [[8 * D, 128], [128, nb], [1, 128]]),
                      bass.AP(pc, c * pcw + tg * 512, [[8 * pcw, 128], [128, nb], [1, 128]]),
                      "SA", transpose=True)

        if quant_out is not None:
            qo = quant_out
            for c in range(8):
                @p.step("sync", waits=("SV", "SA"))
                def _(s, c=c, tg=tg, span=span):
                    p.dma(s, stg[:, (c % 2) * TKH:(c % 2) * TKH + span],
                          src.ap()[c * 128:(c + 1) * 128, tg * 512:tg * 512 + span],
                          "SD")

                @p.step("vector", waits=("SD",))
                def _(v, c=c, tg=tg, span=span):
                    inst = v.tensor_sub(qo[:, 4 * D + c * 512:4 * D + c * 512 + span],
                                        stg[:, (c % 2) * TKH:(c % 2) * TKH + span],
                                        pc[:, c * pcw + tg * 512:c * pcw + tg * 512 + span])
                    p.fin(inst, "SV")

            @p.step("scalar", waits=("SV",))
            def _(sc, tg=tg, span=span):
                nb = span // 128
                for c in range(8):
                    p.dma(sc, bass.AP(qo, c * 512, [[8 * TK, 128], [128, nb], [1, 128]]),
                          bass.AP(qo, 4 * D + c * 512, [[8 * TK, 128], [128, nb], [1, 128]]),
                          "SA", transpose=True)

            @p.step("vector", waits=("SA",))
            def _(v, tg=tg, nti=nti, qo=qo, xoff=xoff):
                stt2 = G["stt2"]
                for pp in range(nti // 2):
                    ta = tg * 4 + 2 * pp
                    tb = ta + 1
                    hiA = bass.AP(xtm, xoff + (2 * pp) * 128,
                                  [[8 * D, 128], [512, 8], [1, 128]])
                    hiB = bass.AP(xtm, xoff + (2 * pp + 1) * 128,
                                  [[8 * D, 128], [512, 8], [1, 128]])
                    loA = bass.AP(qo, (2 * pp) * 128, [[8 * TK, 128], [512, 8], [1, 128]])
                    loB = bass.AP(qo, (2 * pp + 1) * 128, [[8 * TK, 128], [512, 8], [1, 128]])
                    tmpA = scr[:, 0:1024]
                    tmpB = scr[:, 1024:2048]
                    dmpA = qo[:, 4 * D:4 * D + 1024]
                    dmpB = qo[:, 4 * D + 1024:4 * D + 2048]
                    v.tensor_tensor(tmpA, hiA, loA, ALU.add)
                    v.tensor_tensor(tmpB, hiB, loB, ALU.add)
                    v.tensor_reduce(stt[:, ta:ta + 1], tmpA, AX.X, ALU.add)
                    v.tensor_reduce(stt[:, tb:tb + 1], tmpB, AX.X, ALU.add)
                    v.tensor_scalar(stt[:, 32 + ta:33 + ta], stt[:, ta:ta + 1],
                                    1.0 / D, None, ALU.mult)
                    v.tensor_scalar(stt[:, 32 + tb:33 + tb], stt[:, tb:tb + 1],
                                    1.0 / D, None, ALU.mult)
                    v.scalar_tensor_tensor(dmpA, tmpA, 1.0, tmpA, ALU.mult, ALU.mult,
                                           accum_out=stt2[:, ta:ta + 1])
                    v.scalar_tensor_tensor(dmpB, tmpB, 1.0, tmpB, ALU.mult, ALU.mult,
                                           accum_out=stt2[:, tb:tb + 1])
                    v.tensor_scalar(tmpA, tmpA, stt[:, 32 + ta:33 + ta], None,
                                    ALU.subtract)
                    v.tensor_scalar(tmpB, tmpB, stt[:, 32 + tb:33 + tb], None,
                                    ALU.subtract)
                    v.tensor_reduce(stt[:, 96 + ta:97 + ta], tmpA, AX.X,
                                    ALU.max, apply_absolute_value=True)
                    inst = v.tensor_reduce(stt[:, 96 + tb:97 + tb], tmpB, AX.X,
                                           ALU.max, apply_absolute_value=True)
                    p.fin(inst, "SV")
        else:
            @p.step("vector", waits=("SA",))
            def _(v, tg=tg, nti=nti, xoff=xoff):
                for tj in range(nti):
                    ti = tg * 4 + tj
                    tm = bass.AP(xtm, xoff + tj * 128, [[8 * D, 128], [512, 8], [1, 128]])
                    sc3 = bass.AP(scr, 0, [[TKH, 128], [128, 8], [1, 128]])
                    stt2 = G["stt2"]
                    v.tensor_reduce(stt[:, ti:ti + 1], tm, AX.XY, ALU.add)
                    inst = v.scalar_tensor_tensor(sc3, tm, 1.0, tm, ALU.mult,
                                                  ALU.mult, accum_out=stt2[:, ti:ti + 1])
                    p.fin(inst, "SV")
        svq[tg] = p.cnt["SV"]

    stt2 = G["stt2"]

    @p.step("vector", waits=("SV",))
    def _(v):
        inst = v.tensor_scalar(stt[:, 32:32 + nt], stt[:, 0:nt], 1.0 / D,
                               None, ALU.mult)
        p.fin(inst, "SV")

    @p.step("gpsimd", waits=("SV",))
    def _(gp):
        inst = gp.tensor_mul(scr[:, 0:nt], stt[:, 32:32 + nt], stt[:, 32:32 + nt])
        p.fin(inst, "GC")

    @p.step("vector", waits=("GC",))
    def _(v):
        inst = v.scalar_tensor_tensor(stt2[:, 18:18 + nt], stt2[:, 0:nt], 1.0 / D,
                                      scr[:, 0:nt], ALU.mult, ALU.subtract)
        p.fin(inst, "SV")

    @p.step("scalar", waits=("SV",))
    def _(sc):
        ecol = G["csh"][:, 2:3] if eps else G["csh"][:, 3:4]
        inst = sc.activation(stt[:, 64:64 + nt], stt2[:, 18:18 + nt], AF.Sqrt,
                             bias=ecol, scale=1.0)
        p.fin(inst, "SS")

    @p.step("vector", waits=("SS",))
    def _(v):
        h = (nt + 1) // 2
        v.reciprocal(stt[:, 64:64 + h], stt[:, 64:64 + h])
        inst = v.reciprocal(stt[:, 64 + h:64 + nt], stt[:, 64 + h:64 + nt])
        p.fin(inst, "SV")

    if quant_out is None:
        _rows_transpose(p, G)
        _bcast_rows(p, g, G, [(32, 0), (64, 1)], ncols)
        for j in range(4):
            c0, c1 = 2 * j, 2 * j + 1

            @p.step("sync", waits=("SV",))
            def _(s, c0=c0, c1=c1):
                p.dma(s, stg[:, 0:ncols],
                      src.ap()[c0 * 128:(c0 + 1) * 128, 0:ncols], "SD")
                p.dma(s, stg[:, TKH:TKH + ncols],
                      src.ap()[c1 * 128:(c1 + 1) * 128, 0:ncols], "SD")

            @p.step("vector", waits=("SD",))
            def _(v, c0=c0, c1=c1, j=j):
                if j == 0:
                    v.tensor_copy(scr[:, 0:1], G["onesb"][:, 0:1])
                v.tensor_sub(stg[:, 0:ncols], stg[:, 0:ncols], reps[:, 0:ncols])
                v.tensor_sub(stg[:, TKH:TKH + ncols], stg[:, TKH:TKH + ncols],
                             reps[:, 0:ncols])
                v.tensor_mul(xh[:, c0 * TKH:c0 * TKH + ncols], stg[:, 0:ncols],
                             reps[:, TKH:TKH + ncols])
                inst = v.tensor_mul(xh[:, c1 * TKH:c1 * TKH + ncols],
                                    stg[:, TKH:TKH + ncols],
                                    reps[:, TKH:TKH + ncols])
                p.fin(inst, "SV")
    else:
        @p.step("vector", waits=("SV",))
        def _(v):
            h = nt // 2
            a = slice(0, h)
            b = slice(h, nt)
            def S(base, sl):
                return stt[:, base + sl.start: base + sl.stop]
            v.tensor_mul(scr[:, 0:h], S(96, a), S(64, a))
            v.tensor_mul(scr[:, h:nt], S(96, b), S(64, b))
            v.tensor_scalar(scr[:, 0:h], scr[:, 0:h], 1e-12, 1.0 / 127.0,
                            ALU.max, ALU.mult)
            v.tensor_scalar(scr[:, h:nt], scr[:, h:nt], 1e-12, 1.0 / 127.0,
                            ALU.max, ALU.mult)
            v.reciprocal(scr[:, nt:nt + h], scr[:, 0:h])
            v.reciprocal(scr[:, nt + h:2 * nt], scr[:, h:nt])
            v.tensor_mul(S(96, a), S(64, a), scr[:, nt:nt + h])
            v.tensor_mul(S(96, b), S(64, b), scr[:, nt + h:2 * nt])
            v.scalar_tensor_tensor(S(64, a), S(32, a), -0.5, S(96, a),
                                   ALU.bypass, ALU.mult)
            v.scalar_tensor_tensor(S(64, b), S(32, b), -0.5, S(96, b),
                                   ALU.bypass, ALU.mult)
            v.tensor_scalar(S(64, a), S(64, a), -0.5, None, ALU.add)
            v.tensor_scalar(S(64, b), S(64, b), -0.5, None, ALU.add)
            v.tensor_copy(S(32, a), scr[:, 0:h])
            inst = v.tensor_copy(S(32, b), scr[:, h:nt])
            p.fin(inst, "SV")

        _rows_transpose(p, G)

        @p.step("vector", waits=("ST",))
        def _(v):
            inst = v.tensor_copy(G["lssav"][0:nt, :], G["bps"][32:32 + nt, 0:128])
            p.fin(inst, "SV")

        _bcast_rows(p, g, G, [(96, 0), (64, 1)], ncols)

        scri, scri2 = G["scri"], G["scri2"]
        for j in range(4):
            c0, c1 = 2 * j, 2 * j + 1

            @p.step("sync", waits=("SV",))
            def _(s, c0=c0, c1=c1):
                p.dma(s, stg[:, 0:ncols],
                      src.ap()[c0 * 128:(c0 + 1) * 128, 0:ncols], "SD")
                p.dma(s, stg[:, TKH:TKH + ncols],
                      src.ap()[c1 * 128:(c1 + 1) * 128, 0:ncols], "SD")

            @p.step("vector", waits=("SD",))
            def _(v, c0=c0, c1=c1, j=j):
                z0 = stg[:, 0:ncols]
                z1 = stg[:, TKH:TKH + ncols]
                q0 = quant_out[:, c0 * TK:c0 * TK + ncols]
                q1 = quant_out[:, c1 * TK:c1 * TK + ncols]
                if j == 0:
                    v.tensor_copy(scr[:, 0:1], G["onesb"][:, 0:1])
                v.tensor_mul(z0, z0, reps[:, 0:ncols])
                v.tensor_mul(z1, z1, reps[:, 0:ncols])
                v.tensor_sub(z0, z0, reps[:, TKH:TKH + ncols])
                v.tensor_sub(z1, z1, reps[:, TKH:TKH + ncols])
                v.tensor_copy(scri[:, 0:ncols], z0)
                v.tensor_copy(scri2[:, 0:ncols], z1)
                v.tensor_copy(q0, scri[:, 0:ncols])
                v.tensor_copy(q1, scri2[:, 0:ncols])
                v.tensor_tensor(scr[:, 0:ncols], z0, q0, ALU.is_lt)
                v.tensor_tensor(scri[:, 0:ncols], z1, q1, ALU.is_lt)
                v.tensor_sub(q0, q0, scr[:, 0:ncols])
                inst = v.tensor_sub(q1, q1, scri[:, 0:ncols])
                p.fin(inst, "SV")


def _rows_transpose(p, G, ncolT=128):
    @p.step("tensor", waits=("SV",))
    def _(t):
        inst = t.transpose(G["bps"][0:128, 0:128], G["stt"][:, 0:128], G["ident"][:])
        p.fin(inst, "ST")

    @p.step("vector", waits=("ST",))
    def _(v):
        inst = v.tensor_copy(G["rowb"][0:128, :], G["bps"][0:128, 0:128])
        p.fin(inst, "SV")


def _bcast_rows(p, g, G, pairs, ncols, srcbuf=None):
    bps, onesf, reps, rowf = G["bps"], G["onesf"], G["reps"], G["rowf"]
    rb = srcbuf if srcbuf is not None else G["rowb"]
    nt = (ncols + 127) // 128
    for rowbase, dstc in pairs:
        @p.step("scalar", waits=("SV",))
        def _(sc, rowbase=rowbase):
            p.dma(sc, rowf[0:1, 0:ncols], rb[rowbase:rowbase + nt, 0:128], "SA")

        for bi in range((ncols + 511) // 512):
            n0 = bi * 512
            nn = min(512, ncols - n0)

            @p.step("tensor", waits=("SA", "SV"))
            def _(t, n0=n0, nn=nn):
                inst = t.matmul(bps[:, 0:nn], onesf[:], rowf[0:1, n0:n0 + nn],
                                start=True, stop=True)
                p.fin(inst, "ST")

            @p.step("vector", waits=("ST",))
            def _(v, dstc=dstc, n0=n0, nn=nn):
                inst = v.tensor_copy(reps[:, dstc * TKH + n0: dstc * TKH + n0 + nn],
                                     bps[:, 0:nn])
                p.fin(inst, "SV")


def _attn_a_tile(p, G, hg, qt, qlo, qhi, klo, khi, vta, msk, ptl,
                 saps, avps, dnps, rpps, dsb, atta):
    heads = [4 * hg + i for i in range(4)]
    qof = hg * TKH + HALO + qt * 512

    for j in range(6):
        kc = qt * 4 + j
        msl = 6 + j if (qt == 0 and j < 2) else j

        @p.step("tensor", waits=("SS", "SV", "SD"))
        def _(t, kc=kc):
            for i, h in enumerate(heads):
                ro = (h % 4) * 32
                ksl = slice(hg * TKH + kc * 128, hg * TKH + kc * 128 + 128)
                qsl = slice(qof, qof + 512)
                t.matmul(saps[i][:, :], klo[ro:ro + 32, ksl], qlo[ro:ro + 32, qsl],
                         start=True, stop=False, tile_position=(ro, 0))
                inst = t.matmul(saps[i][:, :], khi[ro:ro + 32, ksl],
                                qhi[ro:ro + 32, qsl], start=False, stop=True,
                                tile_position=(ro, 0))
                p.fin(inst, "ST")

        @p.step("scalar", waits=("ST",))
        def _(sc):
            for i in range(4):
                inst = sc.activation(ptl[:, i * 512:(i + 1) * 512], saps[i][:, :],
                                     AF.Exp, bias=G["csh"][:, 0:1], scale=0.125)
                p.fin(inst, "SS")

        @p.step("vector", waits=("SS",))
        def _(v, msl=msl):
            for i in range(4):
                inst = v.tensor_mul(ptl[:, i * 512:(i + 1) * 512],
                                    ptl[:, i * 512:(i + 1) * 512],
                                    msk[:, msl * 512:(msl + 1) * 512])
            p.fin(inst, "SV")

        @p.step("tensor", waits=("SV",))
        def _(t, j=j, kc=kc):
            for pr in range(2):
                for ii in range(2):
                    i = pr * 2 + ii
                    h = heads[i]
                    vsl = slice(kc * 512 + h * 64, kc * 512 + h * 64 + 64)
                    t.matmul(avps[pr][ii * 64:(ii + 1) * 64, :], vta[:, vsl],
                             ptl[:, i * 512:(i + 1) * 512],
                             start=(j == 0), stop=(j == 5),
                             tile_position=(0, ii * 64), skip_group_check=True)
                    inst = t.matmul(dnps[i * 32:i * 32 + 1, :], G["onesb"][:],
                                    ptl[:, i * 512:(i + 1) * 512],
                                    start=(j == 0), stop=(j == 5),
                                    tile_position=(0, i * 32), skip_group_check=True)
                    p.fin(inst, "ST")

    @p.step("vector", waits=("ST", "GC"))
    def _(v):
        for i in range(4):
            v.tensor_copy(dsb[i * 32:i * 32 + 1, 0:512], dnps[i * 32:i * 32 + 1, :])
        for i in range(4):
            inst = v.reciprocal(dsb[i * 32:i * 32 + 1, 512:1024],
                                dsb[i * 32:i * 32 + 1, 0:512])
        p.fin(inst, "SV")

    for pr in range(2):
        @p.step("tensor", waits=("SV",))
        def _(t, pr=pr):
            for ii in range(2):
                i = pr * 2 + ii
                inst = t.matmul(rpps[ii * 64:(ii + 1) * 64, :],
                                G["onesm"][i * 32:i * 32 + 1, 0:64],
                                dsb[i * 32:i * 32 + 1, 512:1024],
                                start=True, stop=True,
                                tile_position=(i * 32, ii * 64),
                                skip_group_check=True)
                p.fin(inst, "ST")

        @p.step("vector", waits=("ST",))
        def _(v, pr=pr):
            pc = hg * 2 + pr
            ob = pc * TK + qt * 512
            v.tensor_copy(dsb[:, 0:256], rpps[:, 0:256])
            v.tensor_copy(dsb[:, 256:512], rpps[:, 256:512])
            v.tensor_mul(atta[:, ob:ob + 256], avps[pr][:, 0:256], dsb[:, 0:256])
            inst = v.tensor_mul(atta[:, ob + 256:ob + 512], avps[pr][:, 256:512],
                                dsb[:, 256:512])
            p.fin(inst, "SV")


def _proj_residual(p, g, G, mm, src_res, src_off, ppa, ppb, fscale, wait_w, dst=None):
    stg, xh, reps = G["stg"], G["xh"], G["reps"]
    if fscale == "ffn":
        _bcast_rows(p, g, G, [(0, 0)], TK, srcbuf=G["lssav"])
    items = [(m, n0) for m in range(8) for n0 in (0, 512, 1024, 1536)]
    sv0 = p.cnt["SV"]
    extra = (("SG", p.cnt["SG"]),) if wait_w else ()
    for si, (m, n0) in enumerate(items):
        ps = ppa if si % 2 == 0 else ppb

        @p.step("tensor", after=(("SV", max(sv0, sv0 + (si - 1) * 2)),) + extra)
        def _(t, m=m, n0=n0, ps=ps):
            inst = mm(t, m, n0, 512, ps)
            p.fin(inst, "ST")

        @p.step("sync", waits=("SV",))
        def _(s, m=m, n0=n0, si=si):
            p.dma(s, stg[:, (si % 2) * TKH:(si % 2) * TKH + 512],
                  src_res.ap()[m * 128:(m + 1) * 128, src_off + n0: src_off + n0 + 512],
                  "SD")

        @p.step("vector", waits=("ST", "SD"))
        def _(v, m=m, n0=n0, si=si, ps=ps):
            s0 = (si % 2) * TKH
            if fscale == "ffn":
                v.scalar_tensor_tensor(G["scrq"][:, 0:256], ps[:, 0:256],
                                       G["rsv"][:, m:m + 1], reps[:, n0:n0 + 256],
                                       ALU.mult, ALU.mult)
                v.scalar_tensor_tensor(G["scrq"][:, 256:512], ps[:, 256:512],
                                       G["rsv"][:, m:m + 1],
                                       reps[:, n0 + 256:n0 + 512],
                                       ALU.mult, ALU.mult)
                v.tensor_add(stg[:, s0:s0 + 256], stg[:, s0:s0 + 256],
                             G["scrq"][:, 0:256])
                v.tensor_add(stg[:, s0 + 256:s0 + 512], stg[:, s0 + 256:s0 + 512],
                             G["scrq"][:, 256:512])
            else:
                v.tensor_add(stg[:, s0:s0 + 256], stg[:, s0:s0 + 256], ps[:, 0:256])
                v.tensor_add(stg[:, s0 + 256:s0 + 512], stg[:, s0 + 256:s0 + 512],
                             ps[:, 256:512])
            v.tensor_copy(xh[:, m * TKH + n0: m * TKH + n0 + 256], stg[:, s0:s0 + 256])
            inst = v.tensor_copy(xh[:, m * TKH + n0 + 256: m * TKH + n0 + 512],
                                 stg[:, s0 + 256:s0 + 512])
            p.fin(inst, "SV", 2)

        @p.step("sync", waits=("SV",))
        def _(s, m=m, n0=n0, si=si):
            p.dma(s, (dst or g["strmh"]).ap()[m * 128:(m + 1) * 128, n0:n0 + 512],
                  stg[:, (si % 2) * TKH:(si % 2) * TKH + 512], "SD")


def _quant_rhs(p, g, G, cscope):
    stg, scr, scri = G["stg"], G["scr"], G["scri"]
    xtm, reps, stt = G["xtm"], G["reps"], G["stt"]
    qrhs, rsv = G["qrhs"], G["rsv"]
    hi8 = p.sb("hi8_qr", [128, 8 * D], BF16, ctx=cscope)
    lo8 = p.sb("lo8_qr", [128, 8 * D], BF16, ctx=cscope)

    for j in range(4):
        c0, c1 = 2 * j, 2 * j + 1

        @p.step("sync", waits=("SV",))
        def _(s, c0=c0, c1=c1):
            p.dma(s, stg[:, 0:D], g["frhs"].ap()[c0 * 128:(c0 + 1) * 128, :], "SD")
            p.dma(s, stg[:, TKH:TKH + D],
                  g["frhs"].ap()[c1 * 128:(c1 + 1) * 128, :], "SD")

        @p.step("vector", waits=("SD",))
        def _(v, c0=c0, c1=c1):
            v.tensor_copy(hi8[:, c0 * D:(c0 + 1) * D], stg[:, 0:D])
            v.tensor_copy(hi8[:, c1 * D:(c1 + 1) * D], stg[:, TKH:TKH + D])
            v.tensor_sub(lo8[:, c0 * D:(c0 + 1) * D], stg[:, 0:D],
                         hi8[:, c0 * D:(c0 + 1) * D])
            inst = v.tensor_sub(lo8[:, c1 * D:(c1 + 1) * D], stg[:, TKH:TKH + D],
                                hi8[:, c1 * D:(c1 + 1) * D])
            p.fin(inst, "SV")

    for ig in range(2):
        @p.step("scalar", waits=("SV",))
        def _(sc, ig=ig):
            for c in range(8):
                p.dma(sc, bass.AP(xtm, c * 512, [[8 * D, 128], [128, 4], [1, 128]]),
                      bass.AP(hi8, c * D + ig * 512, [[8 * D, 128], [128, 4], [1, 128]]),
                      "SA", transpose=True)
                p.dma(sc, bass.AP(G["xh"], c * 512, [[8 * TKH, 128], [128, 4], [1, 128]]),
                      bass.AP(lo8, c * D + ig * 512, [[8 * D, 128], [128, 4], [1, 128]]),
                      "SA", transpose=True)

        @p.step("vector", waits=("SA",))
        def _(v, ig=ig):
            for pp in range(2):
                ta = ig * 4 + 2 * pp
                tb = ta + 1
                hiA = bass.AP(xtm, (2 * pp) * 128, [[8 * D, 128], [512, 8], [1, 128]])
                hiB = bass.AP(xtm, (2 * pp + 1) * 128, [[8 * D, 128], [512, 8], [1, 128]])
                loA = bass.AP(G["xh"], (2 * pp) * 128, [[8 * TKH, 128], [512, 8], [1, 128]])
                loB = bass.AP(G["xh"], (2 * pp + 1) * 128, [[8 * TKH, 128], [512, 8], [1, 128]])
                scA = bass.AP(scr, 0, [[TKH, 128], [128, 8], [1, 128]])
                scB = bass.AP(scr, 1024, [[TKH, 128], [128, 8], [1, 128]])
                v.tensor_tensor(scA, hiA, loA, ALU.add)
                v.tensor_tensor(scB, hiB, loB, ALU.add)
                v.tensor_reduce(stt[:, 96 + ta:97 + ta], scA, AX.XY,
                                ALU.max, apply_absolute_value=True)
                inst = v.tensor_reduce(stt[:, 96 + tb:97 + tb], scB, AX.XY,
                                       ALU.max, apply_absolute_value=True)
                p.fin(inst, "SV")

    @p.step("vector", waits=("SV",))
    def _(v):
        v.tensor_scalar(rsv[:, 0:4], stt[:, 96:100], 1e-12, 1.0 / 127.0,
                        ALU.max, ALU.mult)
        v.tensor_scalar(rsv[:, 4:8], stt[:, 100:104], 1e-12, 1.0 / 127.0,
                        ALU.max, ALU.mult)
        v.reciprocal(stt[:, 96:100], rsv[:, 0:4])
        inst = v.reciprocal(stt[:, 100:104], rsv[:, 4:8])
        p.fin(inst, "SV")

    _rows_transpose(p, G)
    _bcast_rows(p, g, G, [(96, 1)], D)

    scri2 = G["scri2"]
    for j in range(4):
        c0, c1 = 2 * j, 2 * j + 1

        @p.step("sync", waits=("SV",))
        def _(s, c0=c0, c1=c1):
            p.dma(s, stg[:, 0:D], g["frhs"].ap()[c0 * 128:(c0 + 1) * 128, :], "SD")
            p.dma(s, stg[:, TKH:TKH + D],
                  g["frhs"].ap()[c1 * 128:(c1 + 1) * 128, :], "SD")

        @p.step("vector", waits=("SD",))
        def _(v, c0=c0, c1=c1, j=j):
            z0 = stg[:, 0:D]
            z1 = stg[:, TKH:TKH + D]
            q0 = qrhs[:, c0 * D:(c0 + 1) * D]
            q1 = qrhs[:, c1 * D:(c1 + 1) * D]
            if j == 0:
                v.tensor_copy(scr[:, 0:1], G["onesb"][:, 0:1])
            v.tensor_mul(z0, z0, reps[:, TKH:TKH + D])
            v.tensor_mul(z1, z1, reps[:, TKH:TKH + D])
            v.tensor_scalar(z0, z0, 0.5, None, ALU.add)
            v.tensor_scalar(z1, z1, 0.5, None, ALU.add)
            v.tensor_copy(scri[:, 0:D], z0)
            v.tensor_copy(scri2[:, 0:D], z1)
            v.tensor_copy(q0, scri[:, 0:D])
            v.tensor_copy(q1, scri2[:, 0:D])
            v.tensor_tensor(scr[:, 0:D], z0, q0, ALU.is_lt)
            v.tensor_tensor(scri[:, 0:D], z1, q1, ALU.is_lt)
            v.tensor_sub(q0, q0, scr[:, 0:D])
            inst = v.tensor_sub(q1, q1, scri[:, 0:D])
            p.fin(inst, "SV")


def _fin(p, g):
    nc = p.nc
    outs = []
    for name, tensor, shape, dt in p.taps:
        if tensor is None or isinstance(dt, str):
            continue
        h = nc.dram_tensor("tap_" + name, list(shape), dt, kind="ExternalOutput")
        outs.append((h, tensor, shape))

    @p.step("sync", waits=("SV", "SS", "ST", "SG", "SA", "GC"))
    def _(s):
        for h, tensor, shape in outs:
            p.dma(s, h.ap(), tensor[0:shape[0], 0:shape[1]], "SD")
        s.wait_ge(p.sem["SD"], p.cnt["SD"])


def kernel(**inputs):
    global LAST_RESULTS
    x = np.asarray(inputs["L0_attn_a_x"])
    wqkv_p = _perm_wqkv(np.asarray(inputs["L0_attn_a_Wqkv"], np.float32))
    wout = np.ascontiguousarray(np.asarray(inputs["L0_attn_a_Wout"], np.float32))
    wT = {n: np.ascontiguousarray(np.asarray(inputs[f"L0_attn_b_{m}"], np.float32).T)
          for n, m in (("wqT", "Wq"), ("wkT", "Wk"), ("wvT", "Wv"),
                       ("woT", "Wo"), ("w1T", "W1"), ("w2T", "W2"))}
    frhs = np.ascontiguousarray(np.asarray(inputs["L0_ffn_rhs"], np.float32))
    ball = np.zeros((128, 48), np.float32)
    for bi, nm in enumerate(("bq", "bk", "bv", "bo", "b1", "b2")):
        ball[:, bi * 8:(bi + 1) * 8] = np.asarray(
            inputs[f"L0_attn_b_{nm}"], np.float32).reshape(8, 128).T
    relb = np.asarray(inputs["L0_attn_b_rel_bias"], np.float32)

    in_maps = []
    for c in range(8):
        b, half = divmod(c, 2)
        t0 = half * TK
        xf = np.zeros((D, TKH), np.float32)
        if t0 >= HALO:
            xf[:, :] = x[b, t0 - HALO: t0 + TK, :].T
        else:
            xf[:, HALO:] = x[b, t0: t0 + TK, :].T
        cst = _host_consts(half)
        in_maps.append(dict(
            xf=np.ascontiguousarray(xf), wqkv=wqkv_p, wout=wout, frhs=frhs,
            ball=ball, relb=relb, cosT=cst["cosT"], sinT=cst["sinT"],
            cosTl=cst["cosTl"], sinTl=cst["sinTl"],
            amask=cst["amask"], edg=cst["edg"],
            trirev=cst["trirev"], trirevf=cst["trirev"].astype(np.float32),
            indwb=cst["indwb"], ohsel=cst["ohsel"],
            e2n=cst["e2n"], farind=cst["farind"],
            relbh=np.ascontiguousarray(relb[:, half * 8:(half + 1) * 8]),
            bvrow=np.asarray(inputs["L0_attn_b_bv"], np.float32).reshape(1, D),
            ident=np.eye(128, dtype=np.float32), **wT))

    nc = build_nc()
    nc.compile()
    res = run_bass_kernel_spmd(nc, in_maps, core_ids=list(range(8)))
    LAST_RESULTS = res
    out = np.empty((B, T, D), np.float32)
    for c in range(8):
        b, half = divmod(c, 2)
        out[b, half * TK:(half + 1) * TK, :] = np.asarray(res.results[c]["y"]).T
    return out


def _attn_b(p, g, G):
    stg, xh, reps, scr = G["stg"], G["xh"], G["reps"], G["scr"]

    _norm(p, g, G, src=g["strmh"], ncols=TK, precast=True)

    @p.step("sync", waits=("SV",))
    def _(s):
        for c in range(8):
            p.dma(s, g["h3h"].ap()[c * 128:(c + 1) * 128, :],
                  xh[:, c * TKH: c * TKH + TK], "SD")
    p.tap("h3", xh, (128, 8 * TKH), BF16)

    with ExitStack() as c_qkb:
        p.barrier()
        wqb = p.sb("wqb", [128, 2 * D], BF16, ctx=c_qkb)
        wvb = p.sb("wvb", [128, 8 * D], BF16, ctx=c_qkb)
        vcb = p.sb("vcb", [128, 2 * D], BF16, ctx=c_qkb)
        vcf = p.sb("vcf", [128, 2 * D], ctx=c_qkb)
        qkev = p.sb("qkev", [128, 2 * 512], BF16, ctx=c_qkb)
        bvr = p.sb("bvr", [1, D], ctx=c_qkb)
        pqa = p.psum("pqa", [128, 512], ctx=c_qkb)
        pqb = p.psum("pqb", [128, 512], ctx=c_qkb)

        @p.step("sync")
        def _(s):
            p.dma(s, bvr[:], g["bvrow"].ap(), "SD")

        st_after = {}
        for wi, (wname, bslot, dsth) in enumerate(
                (("wqT", 0, "qth"), ("wkT", 1, "kth"))):
            sv0q = p.cnt["SV"]
            si = 0
            for m in range(8):
                @p.step("gpsimd", after=(("ST", st_after.get(wi * 8 + m - 2, 0)),))
                def _(gp, m=m, wname=wname):
                    for kc in range(8):
                        p.dma(gp, wqb[:, (m % 2) * D + kc * 128:(m % 2) * D + kc * 128 + 128],
                              g[wname].ap()[kc * 128:(kc + 1) * 128, m * 128:(m + 1) * 128],
                              "SG")

                sg_m = p.cnt["SG"]
                for tt in range(4):
                    ps = pqa if si % 2 == 0 else pqb

                    @p.step("tensor", after=(("SV", max(sv0q, sv0q + si - 1)), ("SG", sg_m)))
                    def _(t, m=m, tt=tt, ps=ps):
                        for kc in range(8):
                            inst = t.matmul(
                                ps[:, 0:512],
                                wqb[:, (m % 2) * D + kc * 128:(m % 2) * D + kc * 128 + 128],
                                xh[:, kc * TKH + tt * 512: kc * TKH + tt * 512 + 512],
                                start=(kc == 0), stop=(kc == 7))
                        p.fin(inst, "ST")

                    @p.step("vector", waits=("ST", "SD"))
                    def _(v, m=m, ps=ps, bslot=bslot, si=si):
                        inst = v.tensor_scalar(qkev[:, (si % 2) * 512:(si % 2) * 512 + 512],
                                               ps[:, 0:512],
                                               G["bsb"][:, bslot * 8 + m: bslot * 8 + m + 1],
                                               None, ALU.add)
                        p.fin(inst, "SV")

                    @p.step("sync", waits=("SV",))
                    def _(s, m=m, tt=tt, dsth=dsth, si=si):
                        p.dma(s, g[dsth].ap()[m * 128:(m + 1) * 128, tt * 512:(tt + 1) * 512],
                              qkev[:, (si % 2) * 512:(si % 2) * 512 + 512], "SD")
                    si += 1
                st_after[wi * 8 + m] = p.cnt["ST"]

        @p.step("gpsimd", waits=("ST",))
        def _(gp):
            for kc in range(8):
                p.dma(gp, wvb[:, kc * D:(kc + 1) * D],
                      g["wvT"].ap()[kc * 128:(kc + 1) * 128, :], "SG")

        sv0v = p.cnt["SV"]
        sg_v = p.cnt["SG"]
        sd_v = p.cnt["SD"]
        for tc in range(16):
            for nsl in range(2):
                si = tc * 2 + nsl
                ps = pqa if si % 2 == 0 else pqb

                @p.step("tensor", after=(("SV", max(sv0v, sv0v + 2 * (si - 1))),
                                         ("SG", sg_v), ("SD", sd_v)))
                def _(t, tc=tc, nsl=nsl, ps=ps):
                    for kc in range(8):
                        t.matmul(ps[:, 0:512],
                                 xh[:, kc * TKH + tc * 128: kc * TKH + tc * 128 + 128],
                                 wvb[:, kc * D + nsl * 512: kc * D + nsl * 512 + 512],
                                 start=(kc == 0), stop=False)
                    inst = t.matmul(ps[:, 0:512], G["onesf"][0:1, 0:128],
                                    bvr[0:1, nsl * 512:(nsl + 1) * 512],
                                    start=False, stop=True)
                    p.fin(inst, "ST")

                @p.step("vector", waits=("ST",))
                def _(v, tc=tc, nsl=nsl, ps=ps):
                    v.tensor_copy(
                        vcf[:, (tc % 2) * D + nsl * 512:(tc % 2) * D + (nsl + 1) * 512],
                        ps[:, 0:512])
                    inst = v.tensor_copy(
                        vcb[:, (tc % 2) * D + nsl * 512:(tc % 2) * D + (nsl + 1) * 512],
                        ps[:, 0:512])
                    p.fin(inst, "SV", 2)

            @p.step("sync", waits=("SV",))
            def _(s, tc=tc):
                p.dma(s, g["vth"].ap()[tc * 128:(tc + 1) * 128, :],
                      vcb[:, (tc % 2) * D:(tc % 2) * D + D], "SD")
                p.dma(s, g["vtf"].ap()[tc * 128:(tc + 1) * 128, :],
                      vcf[:, (tc % 2) * D:(tc % 2) * D + D], "SD")

    if SUBPH < 2:
        return
    with ExitStack() as c_ab:
        p.barrier()
        trirev_s = p.sb("trirev_s", [128, 128], ctx=c_ab)
        indw_s = p.sb("indw_s", [32, 32 * 128], BF16, ctx=c_ab)
        csumsb = p.sb("csumsb", [32, 64], BF16, ctx=c_ab)
        ohsel_s = p.sb("ohsel_s", [128, 32 * 32], ctx=c_ab)
        e2s = p.sb("e2s", [32, 512], BF16, ctx=c_ab)
        edg_s = p.sb("edg_s", [32, 185], BF16, ctx=c_ab)
        farind_s = p.sb("farind_s", [32, 1], BF16, ctx=c_ab)
        grw = p.sb("grw", [8, 1280], BF16, ctx=c_ab)
        cfr = p.sb("cfr", [1, 8], ctx=c_ab)
        ones32 = p.sb("ones32", [32, 1], ctx=c_ab)
        wev = p.sb("wev", [8, 512], BF16, ctx=c_ab)
        qblk = p.sb("qblk", [128, 8 * 256], BF16, ctx=c_ab)
        kblk = p.sb("kblk", [128, 16 * 256], BF16, ctx=c_ab)
        vb65 = p.sb("vb65", [128, 32 * 65], BF16, ctx=c_ab)
        vload = p.sb("vload", [128, 32 * 64], ctx=c_ab)
        pwin = p.sb("pwin", [128, 32 * 64], BF16, ctx=c_ab)
        ttl = p.sb("ttl", [128, 6 * 512], BF16, ctx=c_ab)
        bsb64 = p.sb("bsb64", [64, 8 * 512], ctx=c_ab)
        csums = p.sb("csums", [32, 64], ctx=c_ab)
        ptrowS = p.sb("ptrowS", [1, 64], ctx=c_ab)
        ptrowB = p.sb("ptrowB", [1, 64], BF16, ctx=c_ab)
        whrs = p.sb("whrs", [1, 512], BF16, ctx=c_ab)
        cfpt = p.sb("cfpt", [64, 1], ctx=c_ab)
        ptl = p.sb("ptlb", [128, 2 * 512], BF16, ctx=c_ab)
        dsb = p.sb("dsb_ab", [1, 2 * 512], ctx=c_ab)
        mgf = p.sb("mgf", [64, 512], ctx=c_ab)
        pnbs = p.sb("pnbs", [64, 512], ctx=c_ab)
        mgb = p.sb("mgb", [64, 512], BF16, ctx=c_ab)
        spsE = p.psum("spsE", [128, 1024], ctx=c_ab)
        spsF = p.psum("spsF", [128, 1024], ctx=c_ab)
        avb = p.psum("avb", [128, 512], ctx=c_ab)
        avb2 = p.psum("avb2", [128, 512], ctx=c_ab)
        ppw = p.psum("ppw", [128, 512], ctx=c_ab)
        pps = ppw

        @p.step("sync")
        def _(s):
            p.dma(s, trirev_s[:], g["trirevf"].ap(), "SD")
            p.dma(s, indw_s[:], g["indwb"].ap(), "SD")
            p.dma(s, e2s[:], g["e2n"].ap(), "SD")
            p.dma(s, edg_s[:], g["edg"].ap(), "SD")
            p.dma(s, farind_s[:], g["farind"].ap(), "SD")

        @p.step("gpsimd")
        def _(gp):
            gp.memset(ohsel_s[:], 0.0)
            gp.memset(ones32[:], 1.0)
            gp.memset(bass.AP(ohsel_s, 0, [[32 * 32, 128], [33, 32]]), 1.0)
            p.fin(gp.memset(grw[:], 0.0), "GC")

        @p.step("tensor", waits=("SD", "SG", "GC"))
        def _(t):
            t.matmul(ppw[0:8, 0:185], G["relbhb"][:], edg_s[:],
                     start=True, stop=True, skip_group_check=True)
            inst = t.matmul(ppw[0:1, 192:200], farind_s[:], G["relbhb"][:],
                            start=True, stop=True, skip_group_check=True)
            p.fin(inst, "ST")

        @p.step("vector", waits=("ST",))
        def _(v):
            v.tensor_copy(grw[0:8, 548:733], ppw[0:8, 0:185])
            inst = v.tensor_copy(cfr[:], ppw[0:1, 192:200])
            p.fin(inst, "SV")

        @p.step("tensor", waits=("SV",))
        def _(t):
            inst = t.matmul(ppw[0:8, 0:512], G["relbhb"][:], e2s[:],
                            start=True, stop=True, skip_group_check=True)
            p.fin(inst, "ST")

        @p.step("vector", waits=("ST",))
        def _(v):
            inst = v.tensor_copy(wev[0:8, :], ppw[0:8, 0:512])
            p.fin(inst, "SV")

        @p.step("sync", waits=("SV",))
        def _(s):
            p.dma(s, g["growh"].ap(), grw[:], "SD")
            p.dma(s, g["whrh"].ap(), wev[:], "SD")

        MSV = {}
        for m in range(8 if SUBPH >= 3 else 0):
            @p.step("sync", waits=("SV", "SS"))
            def _(s, m=m):
                for c in range(8):
                    p.dma(s, qblk[:, c * 256:(c + 1) * 256],
                          g["qth"].ap()[c * 128:(c + 1) * 128, m * 256:(m + 1) * 256],
                          "SD")
                for jkf in range(16):
                    for hb in range(2):
                        p.dma(s, kblk[hb * 64:hb * 64 + 64, jkf * 256:(jkf + 1) * 256],
                              g["kth"].ap()[jkf * 64:(jkf + 1) * 64,
                                            m * 256:(m + 1) * 256], "SD")
                p.dma(s, bass.AP(vload, 0, [[32 * 64, 128], [64, 32], [1, 64]]),
                      bass.AP(g["vtf"], m * 256 * D, [[64, 128], [8192, 32], [1, 64]]),
                      "SD")
                p.dma(s, whrs[:], g["whrh"].ap()[m:m + 1, :], "SD")
                for rr in range(6):
                    p.dma(s, ttl[:, rr * 512:(rr + 1) * 512],
                          bass.AP(g["growh"], m * 1280 + 641 - rr * 128,
                                  [[1, 128], [1, 512]]), "SD")

            @p.step("gpsimd", waits=("SV", "SS"))
            def _(gp, m=m):
                p.fin(gp.memset(vb65[:], 1.0), "GC")

            @p.step("sync", waits=("GC",))
            def _(s, m=m):
                for t2 in range(2):
                    p.dma(s, bass.AP(vb65, t2 * 16 * 65, [[32 * 65, 128], [65, 16], [1, 64]]),
                          bass.AP(g["vth"], (m * 256 + t2 * 128) * D,
                                  [[D, 128], [64, 16], [1, 64]]), "SD")

            @p.step("tensor", waits=("SD", "SV"))
            def _(t, m=m):
                for cc in range(32):
                    inst = t.matmul(pps[0:32, 0:64], ohsel_s[:, cc * 32:(cc + 1) * 32],
                                    vload[:, cc * 64:(cc + 1) * 64],
                                    start=(cc == 0), stop=(cc == 31))
                p.fin(inst, "ST")

            @p.step("vector", waits=("ST",))
            def _(v):
                v.tensor_copy(csums[:], pps[0:32, 0:64])
                inst = v.tensor_copy(csumsb[:], pps[0:32, 0:64])
                p.fin(inst, "SV")

            @p.step("tensor", waits=("SV",))
            def _(t):
                inst = t.matmul(pps[0:1, 64:128], ones32[:], csums[:],
                                start=True, stop=True, skip_group_check=True)
                p.fin(inst, "ST")

            @p.step("vector", waits=("ST",))
            def _(v):
                v.tensor_copy(ptrowS[:], pps[0:1, 64:128])
                inst = v.tensor_copy(ptrowB[:], pps[0:1, 64:128])
                p.fin(inst, "SV")

            @p.step("tensor", waits=("SV",))
            def _(t, m=m):
                inst = t.matmul(pps[0:64, 128:129], ptrowS[:], cfr[0:1, m:m + 1],
                                start=True, stop=True, skip_group_check=True)
                p.fin(inst, "ST")

            @p.step("vector", waits=("ST",))
            def _(v):
                inst = v.tensor_copy(cfpt[:], pps[0:64, 128:129])
                p.fin(inst, "SV")

            for cg in range(4):
                @p.step("tensor", waits=("SV",))
                def _(t, cg=cg):
                    t.matmul(ppw[:, 0:512], trirev_s[:],
                             vload[:, cg * 512:(cg + 1) * 512],
                             start=True, stop=False, skip_group_check=True)
                    for i in range(8):
                        cc = cg * 8 + i
                        inst = t.matmul(ppw[:, i * 64:(i + 1) * 64],
                                        indw_s[:, cc * 128:(cc + 1) * 128],
                                        csumsb[:], start=False, stop=True,
                                        skip_group_check=True)
                    p.fin(inst, "ST")

                @p.step("vector", waits=("ST",))
                def _(v, cg=cg):
                    inst = v.tensor_copy(pwin[:, cg * 512:(cg + 1) * 512], ppw[:, 0:512])
                    p.fin(inst, "SV")

            for sub in range(8):
                rrs = [rr for rr in range(6) if 0 <= sub * 4 - 1 + rr <= 31]

                @p.step("tensor", waits=("SV",))
                def _(t, m=m, sub=sub, rrs=rrs):
                    for i, rr in enumerate(rrs):
                        cc = sub * 4 - 1 + rr
                        last = (i == len(rrs) - 1) and sub != 7
                        inst = t.matmul(G["bps"][0:64, 0:512],
                                        pwin[:, cc * 64:(cc + 1) * 64],
                                        ttl[:, rr * 512:(rr + 1) * 512],
                                        start=(i == 0), stop=last,
                                        skip_group_check=True)
                    if sub == 7:
                        inst = t.matmul(G["bps"][0:64, 0:512], ptrowB[:], whrs[:],
                                        start=False, stop=True,
                                        skip_group_check=True)
                    p.fin(inst, "ST")

                @p.step("vector", waits=("ST",))
                def _(v, sub=sub):
                    inst = v.tensor_copy(bsb64[:, sub * 512:(sub + 1) * 512],
                                         G["bps"][0:64, 0:512])
                    p.fin(inst, "SV")

            p.tap(f"pwin{m}", pwin, (128, 32 * 64), BF16)
            p.tap(f"bias{m}", bsb64, (64, 8 * 512), F32)

            blk_sv = p.cnt["SV"]
            for c in range(8 if SUBPH >= 4 else 0):
                ss0c = p.cnt["SS"]
                gp_idx = m * 8 + c
                sv_pass = max(blk_sv, MSV.get(gp_idx - 2, 0))
                avbc = avb if c % 2 == 0 else avb2
                def _emit_av(kc):
                    jkf, jkt2 = kc // 2, kc % 2

                    @p.step("tensor", after=(("SS", ss0c + kc + 1),)
                            + ((("SV", sv_pass),) if kc == 0 else ()))
                    def _(t, kc=kc, jkf=jkf, jkt2=jkt2):
                        inst = t.matmul(avbc[0:65, 0:512],
                                        vb65[:, (jkt2 * 16 + jkf) * 65:
                                             (jkt2 * 16 + jkf) * 65 + 65],
                                        ptl[:, (kc % 2) * 512:(kc % 2) * 512 + 512],
                                        start=(kc == 0), stop=(kc == 31),
                                        skip_group_check=True)
                        p.fin(inst, "ST")

                for kc in range(32):
                    jkf, jkt2 = kc // 2, kc % 2
                    sE = spsE if kc % 2 == 0 else spsF

                    @p.step("tensor", after=(("SS", max(ss0c, ss0c + kc - 1)),
                                             ("SV", sv_pass)))
                    def _(t, m=m, c=c, jkf=jkf, jkt2=jkt2, sE=sE):
                        col0 = jkf * 256 + jkt2 * 128
                        t.matmul(sE[:, 256:512], kblk[0:64, col0:col0 + 128],
                                 qblk[0:64, c * 256:(c + 1) * 256],
                                 start=True, stop=True, tile_position=(0, 0),
                                 skip_group_check=True)
                        inst = t.matmul(sE[:, 512:768], kblk[64:128, col0:col0 + 128],
                                        qblk[64:128, c * 256:(c + 1) * 256],
                                        start=True, stop=True, tile_position=(64, 0),
                                        skip_group_check=True)
                        p.fin(inst, "ST")

                    @p.step("scalar", waits=("ST",))
                    def _(sc, kc=kc, sE=sE):
                        inst = sc.activation(ptl[:, (kc % 2) * 512:(kc % 2) * 512 + 512],
                                             sE[:, 256:768], AF.Exp,
                                             bias=G["csh"][:, 1:2], scale=1.0)
                        p.fin(inst, "SS")

                    if SUBPH >= 5 and kc > 0:
                        _emit_av(kc - 1)
                if SUBPH >= 5:
                    _emit_av(31)

                if SUBPH < 6:
                    continue

                @p.step("vector", waits=("ST",))
                def _(v, avbc=avbc):
                    v.tensor_copy(dsb[0:1, 0:256], avbc[64:65, 0:256])
                    v.tensor_copy(dsb[0:1, 256:512], avbc[64:65, 256:512])
                    v.reciprocal(dsb[0:1, 512:768], dsb[0:1, 0:256])
                    inst = v.reciprocal(dsb[0:1, 768:1024], dsb[0:1, 256:512])
                    p.fin(inst, "SV")

                @p.step("tensor", waits=("SV",))
                def _(t):
                    inst = t.matmul(ppw[0:64, 0:512], G["onesf"][0:1, 0:64],
                                    dsb[0:1, 512:1024], start=True, stop=True,
                                    skip_group_check=True)
                    p.fin(inst, "ST")

                @p.step("vector", waits=("ST", "SD"))
                def _(v, m=m, c=c, avbc=avbc):
                    v.tensor_copy(pnbs[:, 0:256], ppw[0:64, 0:256])
                    v.tensor_copy(pnbs[:, 256:512], ppw[0:64, 256:512])
                    v.tensor_mul(mgf[:, 0:256], avbc[0:64, 0:256], pnbs[:, 0:256])
                    v.tensor_mul(mgf[:, 256:512], avbc[0:64, 256:512],
                                 pnbs[:, 256:512])
                    for h2 in range(2):
                        jqf = 2 * c + h2
                        inst = v.scalar_tensor_tensor(
                            mgb[:, h2 * 256:(h2 + 1) * 256],
                            mgf[:, h2 * 256:(h2 + 1) * 256],
                            cfpt[:],
                            bass.AP(bsb64, jqf, [[8 * 512, 64], [16, 256]]),
                            ALU.add, ALU.add)
                    p.fin(inst, "SV")

                @p.step("sync", waits=("SV",))
                def _(s, m=m, c=c):
                    for h2 in range(2):
                        jqf = 2 * c + h2
                        p.dma(s, g["atth"].ap()[jqf * 64:(jqf + 1) * 64,
                                                m * 256:(m + 1) * 256],
                              mgb[:, h2 * 256:(h2 + 1) * 256], "SD")
                MSV[m * 8 + c] = p.cnt["SV"]

    p.tap("attnraw", None, (D, TK), "hbm:atth")

    with ExitStack() as c_out:
        p.barrier()
        xcast = p.sb("xcast", [128, 8 * TK], BF16, ctx=c_out)
        xcast2 = p.sb("xcast2", [128, 8 * TK], BF16, ctx=c_out)
        wob = p.sb("wob", [128, 2 * D], BF16, ctx=c_out)
        h3st = p.sb("h3st", [128, 2 * 512], BF16, ctx=c_out)
        poa = p.psum("poa", [128, 512], ctx=c_out)
        pob = p.psum("pob", [128, 512], ctx=c_out)

        @p.step("sync")
        def _(s):
            for c in range(8):
                p.dma(s, xh[:, c * TKH: c * TKH + TK],
                      g["atth"].ap()[c * 128:(c + 1) * 128, :], "SD")

        def _proj_inner(wname, bslot, rhs_t, add_t, out_cast, out_hbm):
            st_after = {}
            sv0o = p.cnt["SV"]
            sd0o = p.cnt["SD"]
            si = 0
            for m in range(8):
                @p.step("gpsimd", after=(("ST", st_after.get(m - 2, 0)),))
                def _(gp, m=m, wname=wname):
                    for kc in range(8):
                        p.dma(gp, wob[:, (m % 2) * D + kc * 128:(m % 2) * D + kc * 128 + 128],
                              g[wname].ap()[kc * 128:(kc + 1) * 128, m * 128:(m + 1) * 128],
                              "SG")

                sg_m = p.cnt["SG"]
                for tt in range(4):
                    ps = poa if si % 2 == 0 else pob

                    @p.step("tensor", after=(("SV", max(sv0o, sv0o + 2 * (si - 1))),
                                             ("SG", sg_m), ("SD", sd0o)))
                    def _(t, m=m, tt=tt, ps=ps, rhs_t=rhs_t):
                        for kc in range(8):
                            inst = t.matmul(
                                ps[:, 0:512],
                                wob[:, (m % 2) * D + kc * 128:(m % 2) * D + kc * 128 + 128],
                                rhs_t[:, kc * (TKH if rhs_t is xh else TK) + tt * 512:
                                      kc * (TKH if rhs_t is xh else TK) + tt * 512 + 512],
                                start=(kc == 0), stop=(kc == 7))
                        p.fin(inst, "ST")

                    if add_t is not None and isinstance(add_t, str):
                        @p.step("sync", waits=("SV",))
                        def _(s, m=m, tt=tt, si=si, add_t=add_t):
                            p.dma(s, h3st[:, (si % 2) * 512:(si % 2) * 512 + 512],
                                  g[add_t].ap()[m * 128:(m + 1) * 128, tt * 512:(tt + 1) * 512],
                                  "SD")

                    @p.step("vector", waits=("ST", "SD"))
                    def _(v, m=m, tt=tt, si=si, ps=ps, bslot=bslot, add_t=add_t,
                          out_cast=out_cast):
                        bvec = G["bsb"][:, bslot * 8 + m: bslot * 8 + m + 1]
                        if add_t is None:
                            at, ab = None, 0
                        elif isinstance(add_t, str):
                            at, ab = h3st, (si % 2) * 512
                        elif add_t is G["xh"]:
                            at, ab = add_t, m * TKH + tt * 512
                        else:
                            at, ab = add_t, m * TK + tt * 512
                        for hh in range(2):
                            hs = slice(hh * 256, hh * 256 + 256)
                            if at is not None:
                                v.scalar_tensor_tensor(
                                    stg[:, hs], ps[:, hs], bvec,
                                    at[:, ab + hh * 256: ab + hh * 256 + 256],
                                    ALU.add, ALU.add)
                            else:
                                v.tensor_scalar(stg[:, hs], ps[:, hs],
                                                bvec, None, ALU.add)
                        ob = m * TK + tt * 512
                        v.tensor_copy(out_cast[:, ob:ob + 256], stg[:, 0:256])
                        inst = v.tensor_copy(out_cast[:, ob + 256:ob + 512],
                                             stg[:, 256:512])
                        p.fin(inst, "SV", 2)

                    @p.step("sync", waits=("SV",))
                    def _(s, m=m, tt=tt, out_hbm=out_hbm):
                        if out_hbm is not None:
                            p.dma(s, g[out_hbm].ap()[m * 128:(m + 1) * 128,
                                                     tt * 512:(tt + 1) * 512],
                                  stg[:, 0:512], "SD")
                    si += 1
                st_after[m] = p.cnt["ST"]

        _proj_inner("woT", 3, xh, "h3h", xcast, "s1h")
        if "s1" in DEBUG_TAPS:
            hs1 = p.nc.dram_tensor("tap_s1", [D, TK], F32, kind="ExternalOutput")

            @p.step("sync", waits=("SV", "SS", "ST", "GC"))
            def _(s):
                p.dma(s, hs1.ap(), g["s1h"].ap(), "SD")

        _norm(p, g, G, src=g["s1h"], ncols=TK, precast=True, eps=0.0,
              pcsrc=xcast)
        p.tap("attblk", xh, (128, 8 * TKH), BF16)

        _proj_inner("w1T", 4, xh, None, xcast, None)

        _proj_inner("w2T", 5, xcast, xh, xcast2, "s1h")

        _norm(p, g, G, src=g["s1h"], ncols=TK, precast=True, eps=0.0,
              pcsrc=xcast2)

        for j in range(4):
            c0, c1 = 2 * j, 2 * j + 1

            @p.step("sync", waits=("SV",))
            def _(s, c0=c0, c1=c1):
                p.dma(s, stg[:, 0:TK],
                      g["strmh"].ap()[c0 * 128:(c0 + 1) * 128, :], "SD")
                p.dma(s, stg[:, TKH:TKH + TK],
                      g["strmh"].ap()[c1 * 128:(c1 + 1) * 128, :], "SD")

            @p.step("vector", waits=("SD",))
            def _(v, c0=c0, c1=c1):
                v.tensor_add(stg[:, 0:TK], stg[:, 0:TK],
                             xh[:, c0 * TKH: c0 * TKH + TK])
                v.tensor_add(stg[:, TKH:TKH + TK], stg[:, TKH:TKH + TK],
                             xh[:, c1 * TKH: c1 * TKH + TK])
                v.tensor_copy(xh[:, c0 * TKH: c0 * TKH + TK], stg[:, 0:TK])
                inst = v.tensor_copy(xh[:, c1 * TKH: c1 * TKH + TK],
                                     stg[:, TKH:TKH + TK])
                p.fin(inst, "SV")

            @p.step("sync", waits=("SV",))
            def _(s, c0=c0, c1=c1):
                p.dma(s, g["strmh"].ap()[c0 * 128:(c0 + 1) * 128, :],
                      stg[:, 0:TK], "SD")
                p.dma(s, g["strmh"].ap()[c1 * 128:(c1 + 1) * 128, :],
                      stg[:, TKH:TKH + TK], "SD")
    p.tap("x3", None, (D, TK), "hbm:strmh")





# version 17
# speedup vs baseline: 1.0293x; 1.0178x over previous
import math
from contextlib import ExitStack

import numpy as np
import ml_dtypes

import concourse.bass as bass
import concourse.mybir as mybir
from concourse.bass import ds
from concourse.bass_utils import run_bass_kernel_spmd
from concourse.bacc import Bacc

F32 = mybir.dt.float32
BF16 = mybir.dt.bfloat16
I32 = mybir.dt.int32
AF = mybir.ActivationFunctionType
ALU = mybir.AluOpType
AX = mybir.AxisListType

B, T, D = 4, 4096, 1024
TK = 2048
HALO = 256
TKH = TK + HALO
CA = 10.0
CB = 80.0
EPS = 1e-5
WIN = 256

DEBUG_TAPS = []
PHASES = 99
SUBPH = 99
LAST_RESULTS = None

BF = ml_dtypes.bfloat16


def _bucket(rel):
    n = -np.asarray(rel)
    nb = 16
    ret = (n < 0).astype(np.int64) * nb
    n = np.abs(n)
    mx = nb // 2
    small = n < mx
    with np.errstate(divide="ignore"):
        large = mx + (np.log(np.maximum(n, 1).astype(np.float64) / mx)
                      / math.log(128.0 / mx) * (nb - mx)).astype(np.int64)
    return ret + np.where(small, n, np.minimum(large, nb - 1))


_CCACHE = {}


def _host_consts(half):
    if half in _CCACHE:
        return _CCACHE[half]
    t0 = half * TK
    o = {}
    inv_freq = 1.0 / 10000.0 ** (np.arange(0, 64, 2, dtype=np.float64) / 64)
    tpos = np.arange(t0 - HALO, t0 + TK, dtype=np.float64)
    fr = np.outer(inv_freq, tpos)
    c64 = np.tile(np.cos(fr), (8, 1))
    s64 = np.tile(np.sin(fr), (8, 1))
    o["cosT"] = c64.astype(np.float32)
    o["cosTl"] = (c64 - o["cosT"].astype(np.float64)).astype(np.float32)
    o["sinT"] = s64.astype(np.float32)
    o["sinTl"] = (s64 - o["sinT"].astype(np.float64)).astype(np.float32)
    masks = np.zeros((8, 128, 512), np.float32)
    for j in range(6):
        dd = (HALO - j * 128) + np.arange(512)[None, :] - np.arange(128)[:, None]
        masks[j] = ((dd >= 0) & (dd <= WIN - 1)).astype(np.float32)
    for sl, j in ((6, 0), (7, 1)):
        kg = t0 - HALO + j * 128 + np.arange(128)[:, None]
        dd = t0 + np.arange(512)[None, :] - kg
        masks[sl] = ((dd >= 0) & (dd <= WIN - 1) & (kg >= 0)).astype(np.float32)
    o["amask"] = masks.reshape(8 * 128, 512).astype(BF)
    wbase = (t0 - 92) // 128
    indw = np.zeros((32, 20), np.float32)
    for w in range(19):
        gch = wbase + w
        if 0 <= gch < 32:
            indw[:gch, w] = 1.0
    indw[:, 19] = 1.0
    o["indw"] = indw.astype(BF)
    o["indwB"] = np.ascontiguousarray(
        np.concatenate([np.repeat(indw[:, w:w + 1], 128, 1) for w in range(19)], 1)
    ).astype(np.float32)
    ohsel = np.zeros((128, 32 * 32), np.float32)
    for gg in range(32):
        ohsel[:, gg * 32 + gg] = 1.0
    o["ohsel"] = ohsel.astype(BF)
    o["wbase"] = wbase
    e2 = np.zeros((32, TK), np.float32)
    dd = t0 + np.arange(TK) - 4096
    bk = _bucket(dd)
    for n in np.nonzero(dd > -91)[0]:
        e2[bk[n], n] += 1.0
        e2[15, n] -= 1.0
    o["e2"] = e2.astype(BF)
    edg = np.zeros((32, 185), np.float32)
    for di in range(185):
        d = di - 92
        edg[_bucket(d), di] += 1.0
        edg[_bucket(d - 1), di] -= 1.0
    o["edg"] = edg.astype(BF)
    o["trirev"] = (np.arange(128)[:, None] <= (127 - np.arange(128))[None, :]).astype(BF)
    e2n = np.zeros((32, 512), np.float32)
    dn = np.arange(512) - 512
    bkn = _bucket(dn)
    far = int(_bucket(np.asarray([-3000]))[0])
    for n in range(512):
        if bkn[n] != far:
            e2n[bkn[n], n] += 1.0
            e2n[far, n] -= 1.0
    o["e2n"] = e2n.astype(BF)
    fi = np.zeros((32, 1), np.float32)
    fi[far, 0] = 1.0
    o["farind"] = fi.astype(BF)
    iw = np.zeros((32, 32 * 128), np.float32)
    for cc in range(32):
        iw[:cc, cc * 128:(cc + 1) * 128] = 1.0
    o["indwb"] = iw.astype(BF)
    cf = np.zeros((512, 8 * 128), np.float32)
    for pr in range(8):
        for m in range(128):
            cf[15 * 16 + 2 * pr + (m >= 64), pr * 128 + m] = 1.0
    o["cfsel"] = cf.astype(BF)
    _CCACHE[half] = o
    return o


def _perm_wqkv(wqkv):
    cols = []
    for part in (0, 1):
        base = part * 512
        lo, hi = [], []
        for h in range(8):
            lo.extend(range(base + h * 64, base + h * 64 + 32))
            hi.extend(range(base + h * 64 + 32, base + h * 64 + 64))
        cols.extend(lo + hi)
    cols.extend(range(1024, 1536))
    return np.ascontiguousarray(wqkv[:, cols])


class P:
    def __init__(self, nc, ctx, blk):
        self.nc, self.ctx, self.blk = nc, ctx, blk
        self.sem = {}
        self.cnt = {}
        for s in ("SD", "SG", "SA", "ST", "SV", "SS", "CC", "GC"):
            self.sem[s] = ctx.enter_context(nc.semaphore(s))
            self.cnt[s] = 0
        self.taps = []

    def sb(self, name, shape, dt=F32, ctx=None):
        return (ctx or self.ctx).enter_context(self.nc.sbuf_tensor(name, shape, dt))

    def psum(self, name, shape, ctx=None):
        return (ctx or self.ctx).enter_context(self.nc.psum_tensor(name, shape, F32))

    def step(self, engine, waits=(), after=()):
        sec = getattr(self.blk, engine)
        pp = self

        def deco(body):
            wv = [(s, pp.cnt[s]) for s in waits] + list(after)

            def run(eng):
                for s, v in wv:
                    if v > 0:
                        eng.wait_ge(pp.sem[s], v)
                body(eng)
            sec(run)
        return deco

    def fin(self, inst, sig, n=1):
        inst.then_inc(self.sem[sig], n)
        self.cnt[sig] += n

    def dma(self, eng, out, in_, sig, **kw):
        self.fin(eng.dma_start(out=out, in_=in_, **kw), sig, 16)

    def tap(self, name, tensor, shape, dt):
        if name in DEBUG_TAPS:
            if tensor is None:
                self.taps.append((name, tensor, shape, dt))
                return
            h = self.nc.dram_tensor("tap_" + name, list(shape), dt,
                                    kind="ExternalOutput")
            pp = self

            @self.step("sync", waits=("SV", "SS", "ST", "GC"))
            def _(s):
                pp.dma(s, h.ap(), tensor[0:shape[0], 0:shape[1]], "SD")

    def barrier(self):
        allsems = ("SD", "SG", "SA", "ST", "SV", "SS", "GC")
        for e in ("sync", "gpsimd", "scalar", "tensor", "vector"):
            @self.step(e, waits=allsems)
            def _(eng):
                pass


def build_nc():
    nc = Bacc("TRN2", target_bir_lowering=False, detect_race_conditions=False)
    g = {}

    def din(name, shape, dt=F32):
        g[name] = nc.dram_tensor(name, shape, dt, kind="ExternalInput")

    din("xf", [D, TKH])
    din("wqkv", [D, 1536])
    din("wout", [512, D])
    for n in ("wqT", "wkT", "wvT", "woT", "w1T", "w2T", "frhs"):
        din(n, [D, D])
    din("ball", [128, 48])
    din("relb", [32, 16])
    din("cosT", [256, TKH])
    din("cosTl", [256, TKH])
    din("sinT", [256, TKH])
    din("sinTl", [256, TKH])
    din("amask", [8 * 128, 512], BF16)
    din("edg", [32, 185], BF16)
    din("trirev", [128, 128], BF16)
    din("trirevf", [128, 128])
    din("ident", [128, 128])
    din("indwb", [32, 32 * 128], BF16)
    din("ohsel", [128, 32 * 32], BF16)
    din("e2n", [32, 512], BF16)
    din("farind", [32, 1], BF16)
    din("relbh", [32, 8])
    din("bvrow", [1, D])
    g["y"] = nc.dram_tensor("y", [D, TK], F32, kind="ExternalOutput")
    g["strmh"] = nc.dram_tensor("strmh", [D, TK], F32)
    g["h3h"] = nc.dram_tensor("h3h", [D, TK], BF16)
    g["qth"] = nc.dram_tensor("qth", [D, TK], BF16)
    g["atth"] = nc.dram_tensor("atth", [D, TK], BF16)
    g["growh"] = nc.dram_tensor("growh", [8, 1280], BF16)
    g["kth"] = nc.dram_tensor("kth", [D, TK], BF16)
    g["vth"] = nc.dram_tensor("vth", [TK, D], BF16)
    g["vtf"] = nc.dram_tensor("vtf", [TK, D], F32)
    g["whrh"] = nc.dram_tensor("whrh", [8, 512], BF16)
    g["rowh"] = nc.dram_tensor("rowh", [16, TK], F32)
    g["s1h"] = nc.dram_tensor("s1h", [D, TK], F32)

    with ExitStack() as ctx:
        blk = ctx.enter_context(nc.Block())
        p = P(nc, ctx, blk)
        _emit(p, g, ctx)
    return nc


def _emit(p, g, ctx):
    G = {}
    G["bsb"] = p.sb("bsb", [128, 48])
    G["onesb"] = p.sb("onesb", [128, 1], BF16)
    G["onesf"] = p.sb("onesf", [1, 128])
    G["onesm"] = p.sb("onesm", [128, 64])
    G["csh"] = p.sb("csh", [128, 4])
    G["relbb"] = p.sb("relbb", [32, 16], BF16)
    G["relbhb"] = p.sb("relbhb", [32, 8], BF16)
    G["stg"] = p.sb("stg", [128, 2 * TKH])
    G["xtm"] = p.sb("xtm", [128, 8 * D], BF16)
    G["stt"] = p.sb("stt", [128, 128])
    G["stt2"] = p.sb("stt2", [128, 40])
    G["rowb"] = p.sb("rowb", [128, 128])
    G["rowf"] = p.sb("rowf", [1, TKH])
    G["lssav"] = p.sb("lssav", [18, 128])
    G["ident"] = p.sb("identsb", [128, 128])
    G["reps"] = p.sb("reps", [128, 2 * TKH])
    G["scr"] = p.sb("scr", [128, TKH])
    G["xh"] = p.sb("xh", [128, 8 * TKH], BF16)
    G["bps"] = p.psum("bps", [128, 512])

    @p.step("gpsimd")
    def _(gp):
        gp.memset(G["onesb"][:], 1.0)
        gp.memset(G["onesf"][:], 1.0)
        gp.memset(G["onesm"][:], 1.0)
        gp.memset(G["csh"][:, 0:1], -CA)
        gp.memset(G["csh"][:, 2:3], EPS)
        gp.memset(G["csh"][:, 3:4], 0.0)
        gp.memset(G["stt"][:], 0.0)
        gp.memset(G["stt2"][:], 0.0)
        gp.memset(G["rowb"][:], 0.0)
        gp.memset(G["rowf"][:], 0.0)
        p.fin(gp.memset(G["csh"][:, 1:2], -CB), "GC")
        p.dma(gp, G["relbb"][:], g["relb"].ap(), "SG")
        p.dma(gp, G["relbhb"][:], g["relbh"].ap(), "SG")

    @p.step("sync")
    def _(s):
        p.dma(s, G["bsb"][:], g["ball"].ap(), "SD")
        p.dma(s, G["ident"][:], g["ident"].ap(), "SD")

    _norm(p, g, G, src=g["xf"], ncols=TKH, precast=False)
    p.tap("h1", G["xh"], (128, 8 * TKH), BF16)
    p.tap("dbg_stt", G["stt"], (128, 128), F32)
    p.tap("dbg_stt2", G["stt2"], (128, 40), F32)
    p.tap("dbg_xtm", G["xtm"], (128, 4 * D), BF16)
    p.tap("dbg_reps", G["reps"], (128, 2 * TKH), F32)
    if PHASES < 2:
        return _fin(p, g)

    with ExitStack() as c_aa:
        atta = p.sb("atta", [128, 4 * TK], BF16, ctx=c_aa)
        qlo = p.sb("qlo", [128, 2 * TKH], BF16, ctx=c_aa)
        qhi = p.sb("qhi", [128, 2 * TKH], BF16, ctx=c_aa)
        klo = p.sb("klo", [128, 2 * TKH], BF16, ctx=c_aa)
        khi = p.sb("khi", [128, 2 * TKH], BF16, ctx=c_aa)
        vta = p.sb("vta", [128, 18 * 512], BF16, ctx=c_aa)

        with ExitStack() as c_qkv:
            p.barrier()
            wqm = p.sb("wqm", [128, 2 * 1024], BF16, ctx=c_qkv)
            wqv = p.sb("wqv", [128, 8 * 512], BF16, ctx=c_qkv)
            HTK = TKH // 2
            csb = p.sb("csb", [128, HTK], ctx=c_qkv)
            snb = p.sb("snb", [128, HTK], ctx=c_qkv)
            rs1 = G["scr"]
            rs2 = G["stg"]
            pA = p.psum("pA", [128, 512], ctx=c_qkv)
            pB = p.psum("pB", [128, 512], ctx=c_qkv)

            @p.step("gpsimd")
            def _(gp):
                for kc in range(8):
                    p.dma(gp, wqv[:, kc * 512:(kc + 1) * 512],
                          g["wqkv"].ap()[kc * 128:(kc + 1) * 128, 1024:1536], "SG")

            NT5 = [(0, 512), (512, 512), (1024, 512), (1536, 512), (2048, 256)]
            sv0 = p.cnt["SV"]
            st_after = {}
            si = 0
            for m in range(8):
                dst = (qlo, qhi, klo, khi)[m // 2]

                @p.step("gpsimd", after=(("ST", st_after.get(m - 2, 0)),))
                def _(gp, m=m):
                    for kc in range(8):
                        p.dma(gp, wqm[:, (m % 2) * 1024 + kc * 128: (m % 2) * 1024 + kc * 128 + 128],
                              g["wqkv"].ap()[kc * 128:(kc + 1) * 128, m * 128:(m + 1) * 128],
                              "SG")

                sg_m = p.cnt["SG"]
                for (tc0, tn) in NT5:
                    ps = pA if si % 2 == 0 else pB

                    @p.step("tensor", after=(("SV", max(sv0, sv0 + si - 1)), ("SG", sg_m)))
                    def _(t, m=m, tc0=tc0, tn=tn, ps=ps):
                        for kc in range(8):
                            inst = t.matmul(ps[:, 0:tn],
                                            wqm[:, (m % 2) * 1024 + kc * 128: (m % 2) * 1024 + kc * 128 + 128],
                                            G["xh"][:, kc * TKH + tc0: kc * TKH + tc0 + tn],
                                            start=(kc == 0), stop=(kc == 7))
                        p.fin(inst, "ST")

                    @p.step("vector", waits=("ST",))
                    def _(v, dst=dst, m=m, tc0=tc0, tn=tn, ps=ps):
                        inst = v.tensor_copy(dst[:, (m % 2) * TKH + tc0: (m % 2) * TKH + tc0 + tn],
                                             ps[:, 0:tn])
                        p.fin(inst, "SV")
                    si += 1
                st_after[m] = p.cnt["ST"]

            sg_v = p.cnt["SG"]
            for tc in range(18):
                ps = pA if si % 2 == 0 else pB

                @p.step("tensor", after=(("SV", sv0 + si - 1), ("SG", sg_v)))
                def _(t, tc=tc, ps=ps):
                    for kc in range(8):
                        inst = t.matmul(ps[:, 0:512],
                                        G["xh"][:, kc * TKH + tc * 128: kc * TKH + tc * 128 + 128],
                                        wqv[:, kc * 512:(kc + 1) * 512],
                                        start=(kc == 0), stop=(kc == 7))
                    p.fin(inst, "ST")

                @p.step("vector", waits=("ST",))
                def _(v, tc=tc, ps=ps):
                    inst = v.tensor_copy(vta[:, tc * 512:(tc + 1) * 512], ps[:, 0:512])
                    p.fin(inst, "SV")
                si += 1

            for c in range(2):
                for hf in range(2):
                    @p.step("sync", waits=("SV",))
                    def _(s, c=c, hf=hf):
                        h0 = hf * HTK
                        p.dma(s, csb[:],
                              g["cosT"].ap()[c * 128:(c + 1) * 128, h0:h0 + HTK], "SD")
                        p.dma(s, snb[:],
                              g["sinT"].ap()[c * 128:(c + 1) * 128, h0:h0 + HTK], "SD")
                        p.dma(s, rs1[:, 0:HTK],
                              g["cosTl"].ap()[c * 128:(c + 1) * 128, h0:h0 + HTK], "SD")
                        p.dma(s, rs2[:, 0:HTK],
                              g["sinTl"].ap()[c * 128:(c + 1) * 128, h0:h0 + HTK], "SD")

                    @p.step("vector", waits=("SD",))
                    def _(v, c=c, hf=hf):
                        v.tensor_add(snb[:], snb[:], rs2[:, 0:HTK])
                        v.tensor_add(csb[:], csb[:], rs1[:, 0:HTK])
                        sl = slice(c * TKH + hf * HTK, c * TKH + (hf + 1) * HTK)
                        for lo, hi in ((qlo, qhi), (klo, khi)):
                            v.tensor_mul(rs1[:, 0:HTK], lo[:, sl], snb[:])
                            v.tensor_mul(rs2[:, 0:HTK], hi[:, sl], snb[:])
                            v.tensor_mul(lo[:, sl], lo[:, sl], csb[:])
                            v.tensor_mul(hi[:, sl], hi[:, sl], csb[:])
                            v.tensor_sub(lo[:, sl], lo[:, sl], rs2[:, 0:HTK])
                            inst = v.tensor_add(hi[:, sl], hi[:, sl], rs1[:, 0:HTK])
                        p.fin(inst, "SV")

        p.tap("qlo", qlo, (128, 2 * TKH), BF16)
        p.tap("khi", khi, (128, 2 * TKH), BF16)
        p.tap("vta", vta, (128, 18 * 512), BF16)
        if PHASES < 3:
            return _fin(p, g)

        with ExitStack() as c_att:
            p.barrier()
            msk = p.sb("msk", [128, 8 * 512], BF16, ctx=c_att)
            ptl = p.sb("ptl", [128, 4 * 512], BF16, ctx=c_att)
            dsb = p.sb("dsb", [128, 2 * 512], ctx=c_att)
            saps = [p.psum(f"saps{i}", [128, 512], ctx=c_att) for i in range(4)]
            avps = [p.psum(f"avps{i}", [128, 512], ctx=c_att) for i in range(2)]
            dnps = p.psum("dnps", [128, 512], ctx=c_att)
            rpps = G["bps"]

            @p.step("sync")
            def _(s):
                for sl in range(8):
                    p.dma(s, msk[:, sl * 512:(sl + 1) * 512],
                          g["amask"].ap()[sl * 128:(sl + 1) * 128, :], "SD")

            @p.step("gpsimd")
            def _(gp):
                p.fin(gp.memset(dsb[:], 1.0), "GC")

            for hg in range(2):
                for qt in range(4):
                    _attn_a_tile(p, G, hg, qt, qlo, qhi, klo, khi, vta, msk,
                                 ptl, saps, avps, dnps, rpps, dsb, atta)

        p.tap("atta", atta, (128, 4 * TK), BF16)
        if PHASES < 4:
            return _fin(p, g)

        with ExitStack() as c_wo:
            p.barrier()
            wos = p.sb("wos", [128, 4 * D], BF16, ctx=c_wo)
            ppa = p.psum("ppra", [128, 512], ctx=c_wo)
            ppb = p.psum("pprb", [128, 512], ctx=c_wo)

            @p.step("gpsimd")
            def _(gp):
                for kc in range(4):
                    p.dma(gp, wos[:, kc * D:(kc + 1) * D],
                          g["wout"].ap()[kc * 128:(kc + 1) * 128, :], "SG")

            def wout_mm(t, m, n0, nn, ps):
                for kc in range(4):
                    inst = t.matmul(ps[:, 0:nn],
                                    wos[:, kc * D + m * 128: kc * D + m * 128 + 128],
                                    atta[:, kc * TK + n0: kc * TK + n0 + nn],
                                    start=(kc == 0), stop=(kc == 3))
                return inst

            _proj_residual(p, g, G, mm=wout_mm, src_res=g["xf"], src_off=HALO,
                           ppa=ppa, ppb=ppb, fscale=None, wait_w="SG")

    if PHASES < 5:
        return _fin(p, g)

    G["qrhs"] = p.sb("qrhs", [128, 8 * D], BF16)
    G["rsv"] = p.sb("rsv", [128, 8])
    with ExitStack() as c_f1:
        p.barrier()
        h2q = p.sb("h2q", [128, 8 * TK], BF16, ctx=c_f1)
        qrhs, rsv = G["qrhs"], G["rsv"]
        G["scrq"] = p.sb("scrq_f1", [128, 512], ctx=c_f1)
        G["scri"] = p.sb("scri_f1", [128, TK], I32, ctx=c_f1)
        G["scri2"] = p.sb("scri2_f1", [128, TK], I32, ctx=c_f1)

        _norm(p, g, G, src=g["strmh"], ncols=TK, precast=True, quant_out=h2q)
        p.tap("h2q", h2q, (128, 8 * TK), BF16)
        p.tap("dbg_2reps", G["reps"], (128, 2 * TKH), F32)
        p.tap("dbg_2stt", G["stt"], (128, 128), F32)
        _quant_rhs(p, g, G, c_f1)
        p.tap("qrhs", qrhs, (128, 8 * D), BF16)
        p.tap("rsv", rsv, (128, 8), F32)
        if PHASES >= 6:
            with ExitStack() as c_mm:
                p.barrier()
                ppa = p.psum("pfa", [128, 512], ctx=c_mm)
                ppb = p.psum("pfb", [128, 512], ctx=c_mm)

                def ffn_mm(t, m, n0, nn, ps):
                    for kc in range(8):
                        inst = t.matmul(ps[:, 0:nn],
                                        qrhs[:, kc * D + m * 128: kc * D + m * 128 + 128],
                                        h2q[:, kc * TK + n0: kc * TK + n0 + nn],
                                        start=(kc == 0), stop=(kc == 7))
                    return inst

                _proj_residual(p, g, G, mm=ffn_mm, src_res=g["strmh"], src_off=0,
                               ppa=ppa, ppb=ppb, fscale="ffn", wait_w=None)
    if PHASES < 7:
        return _fin(p, g)

    _attn_b(p, g, G)
    if PHASES < 8:
        return _fin(p, g)
    with ExitStack() as c_f2:
        p.barrier()
        h4q = p.sb("h4q", [128, 8 * TK], BF16, ctx=c_f2)
        G["scrq"] = p.sb("scrq_f2", [128, 512], ctx=c_f2)
        G["scri"] = p.sb("scri_f2", [128, TK], I32, ctx=c_f2)
        G["scri2"] = p.sb("scri2_f2", [128, TK], I32, ctx=c_f2)
        _norm(p, g, G, src=g["strmh"], ncols=TK, precast=True, quant_out=h4q)
        ppa = p.psum("pf2a", [128, 512], ctx=c_f2)
        ppb = p.psum("pf2b", [128, 512], ctx=c_f2)

        def ffn2_mm(t, m, n0, nn, ps):
            for kc in range(8):
                inst = t.matmul(ps[:, 0:nn],
                                G["qrhs"][:, kc * D + m * 128: kc * D + m * 128 + 128],
                                h4q[:, kc * TK + n0: kc * TK + n0 + nn],
                                start=(kc == 0), stop=(kc == 7))
            return inst

        _proj_residual(p, g, G, mm=ffn2_mm, src_res=g["strmh"], src_off=0,
                       ppa=ppa, ppb=ppb, fscale="ffn", wait_w=None, dst=g["y"])
    return _fin(p, g)


def _norm(p, g, G, src, ncols, precast, quant_out=None, eps=EPS, pcsrc=None):
    stg, xtm, stt = G["stg"], G["xtm"], G["stt"]
    reps, scr = G["reps"], G["scr"]
    xh = G["xh"]
    pc = pcsrc if pcsrc is not None else xh
    pcw = TK if pcsrc is not None else TKH
    nt = (ncols + 127) // 128

    if not precast:
        for c in range(8):
            @p.step("sync", waits=("SV",))
            def _(s, c=c):
                p.dma(s, stg[:, (c % 2) * TKH:(c % 2) * TKH + ncols],
                      src.ap()[c * 128:(c + 1) * 128, 0:ncols], "SD")

            @p.step("vector", waits=("SD",))
            def _(v, c=c):
                inst = v.tensor_copy(xh[:, c * TKH:c * TKH + ncols],
                                     stg[:, (c % 2) * TKH:(c % 2) * TKH + ncols])
                p.fin(inst, "SV")

    ngr = (nt + 3) // 4
    svq = {}
    for tg in range(ngr):
        span = min(512, ncols - tg * 512)
        nti = (span + 127) // 128
        xoff = (tg % 2) * 4 * D
        svgate = svq[tg - 2] if tg >= 2 else p.cnt["SV"]

        @p.step("scalar", after=(("SV", svgate),))
        def _(sc, tg=tg, span=span, xoff=xoff):
            nb = span // 128
            for c in range(8):
                p.dma(sc, bass.AP(xtm, xoff + c * 512, [[8 * D, 128], [128, nb], [1, 128]]),
                      bass.AP(pc, c * pcw + tg * 512, [[8 * pcw, 128], [128, nb], [1, 128]]),
                      "SA", transpose=True)

        if quant_out is not None:
            qo = quant_out
            for c in range(8):
                @p.step("sync", waits=("SV", "SA"))
                def _(s, c=c, tg=tg, span=span):
                    p.dma(s, stg[:, (c % 2) * TKH:(c % 2) * TKH + span],
                          src.ap()[c * 128:(c + 1) * 128, tg * 512:tg * 512 + span],
                          "SD")

                @p.step("vector", waits=("SD",))
                def _(v, c=c, tg=tg, span=span):
                    inst = v.tensor_sub(qo[:, 4 * D + c * 512:4 * D + c * 512 + span],
                                        stg[:, (c % 2) * TKH:(c % 2) * TKH + span],
                                        pc[:, c * pcw + tg * 512:c * pcw + tg * 512 + span])
                    p.fin(inst, "SV")

            @p.step("scalar", waits=("SV",))
            def _(sc, tg=tg, span=span):
                nb = span // 128
                for c in range(8):
                    p.dma(sc, bass.AP(qo, c * 512, [[8 * TK, 128], [128, nb], [1, 128]]),
                          bass.AP(qo, 4 * D + c * 512, [[8 * TK, 128], [128, nb], [1, 128]]),
                          "SA", transpose=True)

            @p.step("vector", waits=("SA",))
            def _(v, tg=tg, nti=nti, qo=qo, xoff=xoff):
                stt2 = G["stt2"]
                for pp in range(nti // 2):
                    ta = tg * 4 + 2 * pp
                    tb = ta + 1
                    hiA = bass.AP(xtm, xoff + (2 * pp) * 128,
                                  [[8 * D, 128], [512, 8], [1, 128]])
                    hiB = bass.AP(xtm, xoff + (2 * pp + 1) * 128,
                                  [[8 * D, 128], [512, 8], [1, 128]])
                    loA = bass.AP(qo, (2 * pp) * 128, [[8 * TK, 128], [512, 8], [1, 128]])
                    loB = bass.AP(qo, (2 * pp + 1) * 128, [[8 * TK, 128], [512, 8], [1, 128]])
                    tmpA = scr[:, 0:1024]
                    tmpB = scr[:, 1024:2048]
                    dmpA = qo[:, 4 * D:4 * D + 1024]
                    dmpB = qo[:, 4 * D + 1024:4 * D + 2048]
                    v.tensor_tensor(tmpA, hiA, loA, ALU.add)
                    v.tensor_tensor(tmpB, hiB, loB, ALU.add)
                    v.tensor_reduce(stt[:, ta:ta + 1], tmpA, AX.X, ALU.add)
                    v.tensor_reduce(stt[:, tb:tb + 1], tmpB, AX.X, ALU.add)
                    v.tensor_scalar(stt[:, 32 + ta:33 + ta], stt[:, ta:ta + 1],
                                    1.0 / D, None, ALU.mult)
                    v.tensor_scalar(stt[:, 32 + tb:33 + tb], stt[:, tb:tb + 1],
                                    1.0 / D, None, ALU.mult)
                    v.scalar_tensor_tensor(dmpA, tmpA, 1.0, tmpA, ALU.mult, ALU.mult,
                                           accum_out=stt2[:, ta:ta + 1])
                    v.scalar_tensor_tensor(dmpB, tmpB, 1.0, tmpB, ALU.mult, ALU.mult,
                                           accum_out=stt2[:, tb:tb + 1])
                    v.tensor_scalar(tmpA, tmpA, stt[:, 32 + ta:33 + ta], None,
                                    ALU.subtract)
                    v.tensor_scalar(tmpB, tmpB, stt[:, 32 + tb:33 + tb], None,
                                    ALU.subtract)
                    v.tensor_reduce(stt[:, 96 + ta:97 + ta], tmpA, AX.X,
                                    ALU.max, apply_absolute_value=True)
                    inst = v.tensor_reduce(stt[:, 96 + tb:97 + tb], tmpB, AX.X,
                                           ALU.max, apply_absolute_value=True)
                    p.fin(inst, "SV")
        else:
            @p.step("vector", waits=("SA",))
            def _(v, tg=tg, nti=nti, xoff=xoff):
                for tj in range(nti):
                    ti = tg * 4 + tj
                    tm = bass.AP(xtm, xoff + tj * 128, [[8 * D, 128], [512, 8], [1, 128]])
                    sc3 = bass.AP(scr, 0, [[TKH, 128], [128, 8], [1, 128]])
                    stt2 = G["stt2"]
                    v.tensor_reduce(stt[:, ti:ti + 1], tm, AX.XY, ALU.add)
                    inst = v.scalar_tensor_tensor(sc3, tm, 1.0, tm, ALU.mult,
                                                  ALU.mult, accum_out=stt2[:, ti:ti + 1])
                    p.fin(inst, "SV")
        svq[tg] = p.cnt["SV"]

    stt2 = G["stt2"]

    @p.step("vector", waits=("SV",))
    def _(v):
        inst = v.tensor_scalar(stt[:, 32:32 + nt], stt[:, 0:nt], 1.0 / D,
                               None, ALU.mult)
        p.fin(inst, "SV")

    @p.step("gpsimd", waits=("SV",))
    def _(gp):
        inst = gp.tensor_mul(scr[:, 0:nt], stt[:, 32:32 + nt], stt[:, 32:32 + nt])
        p.fin(inst, "GC")

    @p.step("vector", waits=("GC",))
    def _(v):
        inst = v.scalar_tensor_tensor(stt2[:, 18:18 + nt], stt2[:, 0:nt], 1.0 / D,
                                      scr[:, 0:nt], ALU.mult, ALU.subtract)
        p.fin(inst, "SV")

    @p.step("scalar", waits=("SV",))
    def _(sc):
        ecol = G["csh"][:, 2:3] if eps else G["csh"][:, 3:4]
        inst = sc.activation(stt[:, 64:64 + nt], stt2[:, 18:18 + nt], AF.Sqrt,
                             bias=ecol, scale=1.0)
        p.fin(inst, "SS")

    @p.step("vector", waits=("SS",))
    def _(v):
        h = (nt + 1) // 2
        v.reciprocal(stt[:, 64:64 + h], stt[:, 64:64 + h])
        inst = v.reciprocal(stt[:, 64 + h:64 + nt], stt[:, 64 + h:64 + nt])
        p.fin(inst, "SV")

    if quant_out is None:
        _rows_transpose(p, G)
        _bcast_rows(p, g, G, [(32, 0), (64, 1)], ncols)
        for j in range(4):
            c0, c1 = 2 * j, 2 * j + 1

            @p.step("sync", waits=("SV",))
            def _(s, c0=c0, c1=c1):
                p.dma(s, stg[:, 0:ncols],
                      src.ap()[c0 * 128:(c0 + 1) * 128, 0:ncols], "SD")
                p.dma(s, stg[:, TKH:TKH + ncols],
                      src.ap()[c1 * 128:(c1 + 1) * 128, 0:ncols], "SD")

            @p.step("vector", waits=("SD",))
            def _(v, c0=c0, c1=c1, j=j):
                if j == 0:
                    v.tensor_copy(scr[:, 0:1], G["onesb"][:, 0:1])
                v.tensor_sub(stg[:, 0:ncols], stg[:, 0:ncols], reps[:, 0:ncols])
                v.tensor_sub(stg[:, TKH:TKH + ncols], stg[:, TKH:TKH + ncols],
                             reps[:, 0:ncols])
                v.tensor_mul(xh[:, c0 * TKH:c0 * TKH + ncols], stg[:, 0:ncols],
                             reps[:, TKH:TKH + ncols])
                inst = v.tensor_mul(xh[:, c1 * TKH:c1 * TKH + ncols],
                                    stg[:, TKH:TKH + ncols],
                                    reps[:, TKH:TKH + ncols])
                p.fin(inst, "SV")
    else:
        @p.step("vector", waits=("SV",))
        def _(v):
            h = nt // 2
            a = slice(0, h)
            b = slice(h, nt)
            def S(base, sl):
                return stt[:, base + sl.start: base + sl.stop]
            v.tensor_mul(scr[:, 0:h], S(96, a), S(64, a))
            v.tensor_mul(scr[:, h:nt], S(96, b), S(64, b))
            v.tensor_scalar(scr[:, 0:h], scr[:, 0:h], 1e-12, 1.0 / 127.0,
                            ALU.max, ALU.mult)
            v.tensor_scalar(scr[:, h:nt], scr[:, h:nt], 1e-12, 1.0 / 127.0,
                            ALU.max, ALU.mult)
            v.reciprocal(scr[:, nt:nt + h], scr[:, 0:h])
            v.reciprocal(scr[:, nt + h:2 * nt], scr[:, h:nt])
            v.tensor_mul(S(96, a), S(64, a), scr[:, nt:nt + h])
            v.tensor_mul(S(96, b), S(64, b), scr[:, nt + h:2 * nt])
            v.scalar_tensor_tensor(S(64, a), S(32, a), -0.5, S(96, a),
                                   ALU.bypass, ALU.mult)
            v.scalar_tensor_tensor(S(64, b), S(32, b), -0.5, S(96, b),
                                   ALU.bypass, ALU.mult)
            v.tensor_scalar(S(64, a), S(64, a), -0.5, None, ALU.add)
            v.tensor_scalar(S(64, b), S(64, b), -0.5, None, ALU.add)
            v.tensor_copy(S(32, a), scr[:, 0:h])
            inst = v.tensor_copy(S(32, b), scr[:, h:nt])
            p.fin(inst, "SV")

        _rows_transpose(p, G)

        @p.step("vector", waits=("ST",))
        def _(v):
            inst = v.tensor_copy(G["lssav"][0:nt, :], G["bps"][32:32 + nt, 0:128])
            p.fin(inst, "SV")

        _bcast_rows(p, g, G, [(96, 0), (64, 1)], ncols)

        scri, scri2 = G["scri"], G["scri2"]
        for j in range(4):
            c0, c1 = 2 * j, 2 * j + 1

            @p.step("sync", waits=("SV",))
            def _(s, c0=c0, c1=c1):
                p.dma(s, stg[:, 0:ncols],
                      src.ap()[c0 * 128:(c0 + 1) * 128, 0:ncols], "SD")
                p.dma(s, stg[:, TKH:TKH + ncols],
                      src.ap()[c1 * 128:(c1 + 1) * 128, 0:ncols], "SD")

            @p.step("vector", waits=("SD",))
            def _(v, c0=c0, c1=c1, j=j):
                z0 = stg[:, 0:ncols]
                z1 = stg[:, TKH:TKH + ncols]
                q0 = quant_out[:, c0 * TK:c0 * TK + ncols]
                q1 = quant_out[:, c1 * TK:c1 * TK + ncols]
                if j == 0:
                    v.tensor_copy(scr[:, 0:1], G["onesb"][:, 0:1])
                v.tensor_mul(z0, z0, reps[:, 0:ncols])
                v.tensor_mul(z1, z1, reps[:, 0:ncols])
                v.tensor_sub(z0, z0, reps[:, TKH:TKH + ncols])
                v.tensor_sub(z1, z1, reps[:, TKH:TKH + ncols])
                v.tensor_copy(scri[:, 0:ncols], z0)
                v.tensor_copy(scri2[:, 0:ncols], z1)
                v.tensor_copy(q0, scri[:, 0:ncols])
                v.tensor_copy(q1, scri2[:, 0:ncols])
                v.tensor_tensor(scr[:, 0:ncols], z0, q0, ALU.is_lt)
                v.tensor_tensor(scri[:, 0:ncols], z1, q1, ALU.is_lt)
                v.tensor_sub(q0, q0, scr[:, 0:ncols])
                inst = v.tensor_sub(q1, q1, scri[:, 0:ncols])
                p.fin(inst, "SV")


def _rows_transpose(p, G, ncolT=128):
    @p.step("tensor", waits=("SV",))
    def _(t):
        inst = t.transpose(G["bps"][0:128, 0:128], G["stt"][:, 0:128], G["ident"][:])
        p.fin(inst, "ST")

    @p.step("vector", waits=("ST",))
    def _(v):
        inst = v.tensor_copy(G["rowb"][0:128, :], G["bps"][0:128, 0:128])
        p.fin(inst, "SV")


def _bcast_rows(p, g, G, pairs, ncols, srcbuf=None):
    bps, onesf, reps, rowf = G["bps"], G["onesf"], G["reps"], G["rowf"]
    rb = srcbuf if srcbuf is not None else G["rowb"]
    nt = (ncols + 127) // 128
    for rowbase, dstc in pairs:
        @p.step("scalar", waits=("SV",))
        def _(sc, rowbase=rowbase):
            p.dma(sc, rowf[0:1, 0:ncols], rb[rowbase:rowbase + nt, 0:128], "SA")

        for bi in range((ncols + 511) // 512):
            n0 = bi * 512
            nn = min(512, ncols - n0)

            @p.step("tensor", waits=("SA", "SV"))
            def _(t, n0=n0, nn=nn):
                inst = t.matmul(bps[:, 0:nn], onesf[:], rowf[0:1, n0:n0 + nn],
                                start=True, stop=True)
                p.fin(inst, "ST")

            @p.step("vector", waits=("ST",))
            def _(v, dstc=dstc, n0=n0, nn=nn):
                inst = v.tensor_copy(reps[:, dstc * TKH + n0: dstc * TKH + n0 + nn],
                                     bps[:, 0:nn])
                p.fin(inst, "SV")


def _attn_a_tile(p, G, hg, qt, qlo, qhi, klo, khi, vta, msk, ptl,
                 saps, avps, dnps, rpps, dsb, atta):
    heads = [4 * hg + i for i in range(4)]
    qof = hg * TKH + HALO + qt * 512

    for j in range(6):
        kc = qt * 4 + j
        msl = 6 + j if (qt == 0 and j < 2) else j

        @p.step("tensor", waits=("SS", "SV", "SD"))
        def _(t, kc=kc):
            for i, h in enumerate(heads):
                ro = (h % 4) * 32
                ksl = slice(hg * TKH + kc * 128, hg * TKH + kc * 128 + 128)
                qsl = slice(qof, qof + 512)
                t.matmul(saps[i][:, :], klo[ro:ro + 32, ksl], qlo[ro:ro + 32, qsl],
                         start=True, stop=False, tile_position=(ro, 0))
                inst = t.matmul(saps[i][:, :], khi[ro:ro + 32, ksl],
                                qhi[ro:ro + 32, qsl], start=False, stop=True,
                                tile_position=(ro, 0))
                p.fin(inst, "ST")

        @p.step("scalar", waits=("ST",))
        def _(sc):
            for i in range(4):
                inst = sc.activation(ptl[:, i * 512:(i + 1) * 512], saps[i][:, :],
                                     AF.Exp, bias=G["csh"][:, 0:1], scale=0.125)
                p.fin(inst, "SS")

        @p.step("vector", waits=("SS",))
        def _(v, msl=msl):
            for i in range(4):
                inst = v.tensor_mul(ptl[:, i * 512:(i + 1) * 512],
                                    ptl[:, i * 512:(i + 1) * 512],
                                    msk[:, msl * 512:(msl + 1) * 512])
            p.fin(inst, "SV")

        @p.step("tensor", waits=("SV",))
        def _(t, j=j, kc=kc):
            for pr in range(2):
                for ii in range(2):
                    i = pr * 2 + ii
                    h = heads[i]
                    vsl = slice(kc * 512 + h * 64, kc * 512 + h * 64 + 64)
                    t.matmul(avps[pr][ii * 64:(ii + 1) * 64, :], vta[:, vsl],
                             ptl[:, i * 512:(i + 1) * 512],
                             start=(j == 0), stop=(j == 5),
                             tile_position=(0, ii * 64), skip_group_check=True)
                    inst = t.matmul(dnps[i * 32:i * 32 + 1, :], G["onesb"][:],
                                    ptl[:, i * 512:(i + 1) * 512],
                                    start=(j == 0), stop=(j == 5),
                                    tile_position=(0, i * 32), skip_group_check=True)
                    p.fin(inst, "ST")

    @p.step("vector", waits=("ST", "GC"))
    def _(v):
        for i in range(4):
            v.tensor_copy(dsb[i * 32:i * 32 + 1, 0:512], dnps[i * 32:i * 32 + 1, :])
        for i in range(4):
            inst = v.reciprocal(dsb[i * 32:i * 32 + 1, 512:1024],
                                dsb[i * 32:i * 32 + 1, 0:512])
        p.fin(inst, "SV")

    for pr in range(2):
        @p.step("tensor", waits=("SV",))
        def _(t, pr=pr):
            for ii in range(2):
                i = pr * 2 + ii
                inst = t.matmul(rpps[ii * 64:(ii + 1) * 64, :],
                                G["onesm"][i * 32:i * 32 + 1, 0:64],
                                dsb[i * 32:i * 32 + 1, 512:1024],
                                start=True, stop=True,
                                tile_position=(i * 32, ii * 64),
                                skip_group_check=True)
                p.fin(inst, "ST")

        @p.step("vector", waits=("ST",))
        def _(v, pr=pr):
            pc = hg * 2 + pr
            ob = pc * TK + qt * 512
            v.tensor_copy(dsb[:, 0:256], rpps[:, 0:256])
            v.tensor_copy(dsb[:, 256:512], rpps[:, 256:512])
            v.tensor_mul(atta[:, ob:ob + 256], avps[pr][:, 0:256], dsb[:, 0:256])
            inst = v.tensor_mul(atta[:, ob + 256:ob + 512], avps[pr][:, 256:512],
                                dsb[:, 256:512])
            p.fin(inst, "SV")


def _proj_residual(p, g, G, mm, src_res, src_off, ppa, ppb, fscale, wait_w, dst=None):
    stg, xh, reps = G["stg"], G["xh"], G["reps"]
    if fscale == "ffn":
        _bcast_rows(p, g, G, [(0, 0)], TK, srcbuf=G["lssav"])
    items = [(m, n0) for m in range(8) for n0 in (0, 512, 1024, 1536)]
    sv0 = p.cnt["SV"]
    extra = (("SG", p.cnt["SG"]),) if wait_w else ()
    for si, (m, n0) in enumerate(items):
        ps = ppa if si % 2 == 0 else ppb

        @p.step("tensor", after=(("SV", max(sv0, sv0 + (si - 1) * 2)),) + extra)
        def _(t, m=m, n0=n0, ps=ps):
            inst = mm(t, m, n0, 512, ps)
            p.fin(inst, "ST")

        @p.step("sync", waits=("SV",))
        def _(s, m=m, n0=n0, si=si):
            p.dma(s, stg[:, (si % 2) * TKH:(si % 2) * TKH + 512],
                  src_res.ap()[m * 128:(m + 1) * 128, src_off + n0: src_off + n0 + 512],
                  "SD")

        @p.step("vector", waits=("ST", "SD"))
        def _(v, m=m, n0=n0, si=si, ps=ps):
            s0 = (si % 2) * TKH
            if fscale == "ffn":
                v.scalar_tensor_tensor(G["scrq"][:, 0:256], ps[:, 0:256],
                                       G["rsv"][:, m:m + 1], reps[:, n0:n0 + 256],
                                       ALU.mult, ALU.mult)
                v.scalar_tensor_tensor(G["scrq"][:, 256:512], ps[:, 256:512],
                                       G["rsv"][:, m:m + 1],
                                       reps[:, n0 + 256:n0 + 512],
                                       ALU.mult, ALU.mult)
                v.tensor_add(stg[:, s0:s0 + 256], stg[:, s0:s0 + 256],
                             G["scrq"][:, 0:256])
                v.tensor_add(stg[:, s0 + 256:s0 + 512], stg[:, s0 + 256:s0 + 512],
                             G["scrq"][:, 256:512])
            else:
                v.tensor_add(stg[:, s0:s0 + 256], stg[:, s0:s0 + 256], ps[:, 0:256])
                v.tensor_add(stg[:, s0 + 256:s0 + 512], stg[:, s0 + 256:s0 + 512],
                             ps[:, 256:512])
            v.tensor_copy(xh[:, m * TKH + n0: m * TKH + n0 + 256], stg[:, s0:s0 + 256])
            inst = v.tensor_copy(xh[:, m * TKH + n0 + 256: m * TKH + n0 + 512],
                                 stg[:, s0 + 256:s0 + 512])
            p.fin(inst, "SV", 2)

        @p.step("sync", waits=("SV",))
        def _(s, m=m, n0=n0, si=si):
            p.dma(s, (dst or g["strmh"]).ap()[m * 128:(m + 1) * 128, n0:n0 + 512],
                  stg[:, (si % 2) * TKH:(si % 2) * TKH + 512], "SD")


def _quant_rhs(p, g, G, cscope):
    stg, scr, scri = G["stg"], G["scr"], G["scri"]
    xtm, reps, stt = G["xtm"], G["reps"], G["stt"]
    qrhs, rsv = G["qrhs"], G["rsv"]
    hi8 = p.sb("hi8_qr", [128, 8 * D], BF16, ctx=cscope)
    lo8 = p.sb("lo8_qr", [128, 8 * D], BF16, ctx=cscope)

    for j in range(4):
        c0, c1 = 2 * j, 2 * j + 1

        @p.step("sync", waits=("SV",))
        def _(s, c0=c0, c1=c1):
            p.dma(s, stg[:, 0:D], g["frhs"].ap()[c0 * 128:(c0 + 1) * 128, :], "SD")
            p.dma(s, stg[:, TKH:TKH + D],
                  g["frhs"].ap()[c1 * 128:(c1 + 1) * 128, :], "SD")

        @p.step("vector", waits=("SD",))
        def _(v, c0=c0, c1=c1):
            v.tensor_copy(hi8[:, c0 * D:(c0 + 1) * D], stg[:, 0:D])
            v.tensor_copy(hi8[:, c1 * D:(c1 + 1) * D], stg[:, TKH:TKH + D])
            v.tensor_sub(lo8[:, c0 * D:(c0 + 1) * D], stg[:, 0:D],
                         hi8[:, c0 * D:(c0 + 1) * D])
            inst = v.tensor_sub(lo8[:, c1 * D:(c1 + 1) * D], stg[:, TKH:TKH + D],
                                hi8[:, c1 * D:(c1 + 1) * D])
            p.fin(inst, "SV")

    for ig in range(2):
        @p.step("scalar", waits=("SV",))
        def _(sc, ig=ig):
            for c in range(8):
                p.dma(sc, bass.AP(xtm, c * 512, [[8 * D, 128], [128, 4], [1, 128]]),
                      bass.AP(hi8, c * D + ig * 512, [[8 * D, 128], [128, 4], [1, 128]]),
                      "SA", transpose=True)
                p.dma(sc, bass.AP(G["xh"], c * 512, [[8 * TKH, 128], [128, 4], [1, 128]]),
                      bass.AP(lo8, c * D + ig * 512, [[8 * D, 128], [128, 4], [1, 128]]),
                      "SA", transpose=True)

        @p.step("vector", waits=("SA",))
        def _(v, ig=ig):
            for pp in range(2):
                ta = ig * 4 + 2 * pp
                tb = ta + 1
                hiA = bass.AP(xtm, (2 * pp) * 128, [[8 * D, 128], [512, 8], [1, 128]])
                hiB = bass.AP(xtm, (2 * pp + 1) * 128, [[8 * D, 128], [512, 8], [1, 128]])
                loA = bass.AP(G["xh"], (2 * pp) * 128, [[8 * TKH, 128], [512, 8], [1, 128]])
                loB = bass.AP(G["xh"], (2 * pp + 1) * 128, [[8 * TKH, 128], [512, 8], [1, 128]])
                scA = bass.AP(scr, 0, [[TKH, 128], [128, 8], [1, 128]])
                scB = bass.AP(scr, 1024, [[TKH, 128], [128, 8], [1, 128]])
                v.tensor_tensor(scA, hiA, loA, ALU.add)
                v.tensor_tensor(scB, hiB, loB, ALU.add)
                v.tensor_reduce(stt[:, 96 + ta:97 + ta], scA, AX.XY,
                                ALU.max, apply_absolute_value=True)
                inst = v.tensor_reduce(stt[:, 96 + tb:97 + tb], scB, AX.XY,
                                       ALU.max, apply_absolute_value=True)
                p.fin(inst, "SV")

    @p.step("vector", waits=("SV",))
    def _(v):
        v.tensor_scalar(rsv[:, 0:4], stt[:, 96:100], 1e-12, 1.0 / 127.0,
                        ALU.max, ALU.mult)
        v.tensor_scalar(rsv[:, 4:8], stt[:, 100:104], 1e-12, 1.0 / 127.0,
                        ALU.max, ALU.mult)
        v.reciprocal(stt[:, 96:100], rsv[:, 0:4])
        inst = v.reciprocal(stt[:, 100:104], rsv[:, 4:8])
        p.fin(inst, "SV")

    _rows_transpose(p, G)
    _bcast_rows(p, g, G, [(96, 1)], D)

    scri2 = G["scri2"]
    for j in range(4):
        c0, c1 = 2 * j, 2 * j + 1

        @p.step("sync", waits=("SV",))
        def _(s, c0=c0, c1=c1):
            p.dma(s, stg[:, 0:D], g["frhs"].ap()[c0 * 128:(c0 + 1) * 128, :], "SD")
            p.dma(s, stg[:, TKH:TKH + D],
                  g["frhs"].ap()[c1 * 128:(c1 + 1) * 128, :], "SD")

        @p.step("vector", waits=("SD",))
        def _(v, c0=c0, c1=c1, j=j):
            z0 = stg[:, 0:D]
            z1 = stg[:, TKH:TKH + D]
            q0 = qrhs[:, c0 * D:(c0 + 1) * D]
            q1 = qrhs[:, c1 * D:(c1 + 1) * D]
            if j == 0:
                v.tensor_copy(scr[:, 0:1], G["onesb"][:, 0:1])
            v.tensor_mul(z0, z0, reps[:, TKH:TKH + D])
            v.tensor_mul(z1, z1, reps[:, TKH:TKH + D])
            v.tensor_scalar(z0, z0, 0.5, None, ALU.add)
            v.tensor_scalar(z1, z1, 0.5, None, ALU.add)
            v.tensor_copy(scri[:, 0:D], z0)
            v.tensor_copy(scri2[:, 0:D], z1)
            v.tensor_copy(q0, scri[:, 0:D])
            v.tensor_copy(q1, scri2[:, 0:D])
            v.tensor_tensor(scr[:, 0:D], z0, q0, ALU.is_lt)
            v.tensor_tensor(scri[:, 0:D], z1, q1, ALU.is_lt)
            v.tensor_sub(q0, q0, scr[:, 0:D])
            inst = v.tensor_sub(q1, q1, scri[:, 0:D])
            p.fin(inst, "SV")


def _fin(p, g):
    nc = p.nc
    outs = []
    for name, tensor, shape, dt in p.taps:
        if tensor is None or isinstance(dt, str):
            continue
        h = nc.dram_tensor("tap_" + name, list(shape), dt, kind="ExternalOutput")
        outs.append((h, tensor, shape))

    @p.step("sync", waits=("SV", "SS", "ST", "SG", "SA", "GC"))
    def _(s):
        for h, tensor, shape in outs:
            p.dma(s, h.ap(), tensor[0:shape[0], 0:shape[1]], "SD")
        s.wait_ge(p.sem["SD"], p.cnt["SD"])


def kernel(**inputs):
    global LAST_RESULTS
    x = np.asarray(inputs["L0_attn_a_x"])
    wqkv_p = _perm_wqkv(np.asarray(inputs["L0_attn_a_Wqkv"], np.float32))
    wout = np.ascontiguousarray(np.asarray(inputs["L0_attn_a_Wout"], np.float32))
    wT = {n: np.ascontiguousarray(np.asarray(inputs[f"L0_attn_b_{m}"], np.float32).T)
          for n, m in (("wqT", "Wq"), ("wkT", "Wk"), ("wvT", "Wv"),
                       ("woT", "Wo"), ("w1T", "W1"), ("w2T", "W2"))}
    frhs = np.ascontiguousarray(np.asarray(inputs["L0_ffn_rhs"], np.float32))
    ball = np.zeros((128, 48), np.float32)
    for bi, nm in enumerate(("bq", "bk", "bv", "bo", "b1", "b2")):
        ball[:, bi * 8:(bi + 1) * 8] = np.asarray(
            inputs[f"L0_attn_b_{nm}"], np.float32).reshape(8, 128).T
    relb = np.asarray(inputs["L0_attn_b_rel_bias"], np.float32)

    in_maps = []
    for c in range(8):
        b, half = divmod(c, 2)
        t0 = half * TK
        xf = np.zeros((D, TKH), np.float32)
        if t0 >= HALO:
            xf[:, :] = x[b, t0 - HALO: t0 + TK, :].T
        else:
            xf[:, HALO:] = x[b, t0: t0 + TK, :].T
        cst = _host_consts(half)
        in_maps.append(dict(
            xf=np.ascontiguousarray(xf), wqkv=wqkv_p, wout=wout, frhs=frhs,
            ball=ball, relb=relb, cosT=cst["cosT"], sinT=cst["sinT"],
            cosTl=cst["cosTl"], sinTl=cst["sinTl"],
            amask=cst["amask"], edg=cst["edg"],
            trirev=cst["trirev"], trirevf=cst["trirev"].astype(np.float32),
            indwb=cst["indwb"], ohsel=cst["ohsel"],
            e2n=cst["e2n"], farind=cst["farind"],
            relbh=np.ascontiguousarray(relb[:, half * 8:(half + 1) * 8]),
            bvrow=np.asarray(inputs["L0_attn_b_bv"], np.float32).reshape(1, D),
            ident=np.eye(128, dtype=np.float32), **wT))

    nc = build_nc()
    nc.compile()
    res = run_bass_kernel_spmd(nc, in_maps, core_ids=list(range(8)))
    LAST_RESULTS = res
    out = np.empty((B, T, D), np.float32)
    for c in range(8):
        b, half = divmod(c, 2)
        out[b, half * TK:(half + 1) * TK, :] = np.asarray(res.results[c]["y"]).T
    return out


def _attn_b(p, g, G):
    stg, xh, reps, scr = G["stg"], G["xh"], G["reps"], G["scr"]

    _norm(p, g, G, src=g["strmh"], ncols=TK, precast=True)

    @p.step("sync", waits=("SV",))
    def _(s):
        for c in range(8):
            p.dma(s, g["h3h"].ap()[c * 128:(c + 1) * 128, :],
                  xh[:, c * TKH: c * TKH + TK], "SD")
    p.tap("h3", xh, (128, 8 * TKH), BF16)

    with ExitStack() as c_qkb:
        p.barrier()
        wqb = p.sb("wqb", [128, 2 * D], BF16, ctx=c_qkb)
        wvb = p.sb("wvb", [128, 8 * D], BF16, ctx=c_qkb)
        vcb = p.sb("vcb", [128, 2 * D], BF16, ctx=c_qkb)
        vcf = p.sb("vcf", [128, 2 * D], ctx=c_qkb)
        qkev = p.sb("qkev", [128, 2 * 512], BF16, ctx=c_qkb)
        bvr = p.sb("bvr", [1, D], ctx=c_qkb)
        pqa = p.psum("pqa", [128, 512], ctx=c_qkb)
        pqb = p.psum("pqb", [128, 512], ctx=c_qkb)

        @p.step("sync")
        def _(s):
            p.dma(s, bvr[:], g["bvrow"].ap(), "SD")

        st_after = {}
        for wi, (wname, bslot, dsth) in enumerate(
                (("wqT", 0, "qth"), ("wkT", 1, "kth"))):
            sv0q = p.cnt["SV"]
            si = 0
            for m in range(8):
                @p.step("gpsimd", after=(("ST", st_after.get(wi * 8 + m - 2, 0)),))
                def _(gp, m=m, wname=wname):
                    for kc in range(8):
                        p.dma(gp, wqb[:, (m % 2) * D + kc * 128:(m % 2) * D + kc * 128 + 128],
                              g[wname].ap()[kc * 128:(kc + 1) * 128, m * 128:(m + 1) * 128],
                              "SG")

                sg_m = p.cnt["SG"]
                for tt in range(4):
                    ps = pqa if si % 2 == 0 else pqb

                    @p.step("tensor", after=(("SV", max(sv0q, sv0q + si - 1)), ("SG", sg_m)))
                    def _(t, m=m, tt=tt, ps=ps):
                        for kc in range(8):
                            inst = t.matmul(
                                ps[:, 0:512],
                                wqb[:, (m % 2) * D + kc * 128:(m % 2) * D + kc * 128 + 128],
                                xh[:, kc * TKH + tt * 512: kc * TKH + tt * 512 + 512],
                                start=(kc == 0), stop=(kc == 7))
                        p.fin(inst, "ST")

                    @p.step("vector", waits=("ST", "SD"))
                    def _(v, m=m, ps=ps, bslot=bslot, si=si):
                        inst = v.tensor_scalar(qkev[:, (si % 2) * 512:(si % 2) * 512 + 512],
                                               ps[:, 0:512],
                                               G["bsb"][:, bslot * 8 + m: bslot * 8 + m + 1],
                                               None, ALU.add)
                        p.fin(inst, "SV")

                    @p.step("sync", waits=("SV",))
                    def _(s, m=m, tt=tt, dsth=dsth, si=si):
                        p.dma(s, g[dsth].ap()[m * 128:(m + 1) * 128, tt * 512:(tt + 1) * 512],
                              qkev[:, (si % 2) * 512:(si % 2) * 512 + 512], "SD")
                    si += 1
                st_after[wi * 8 + m] = p.cnt["ST"]

        @p.step("gpsimd", waits=("ST",))
        def _(gp):
            for kc in range(8):
                p.dma(gp, wvb[:, kc * D:(kc + 1) * D],
                      g["wvT"].ap()[kc * 128:(kc + 1) * 128, :], "SG")

        sv0v = p.cnt["SV"]
        sg_v = p.cnt["SG"]
        sd_v = p.cnt["SD"]
        for tc in range(16):
            for nsl in range(2):
                si = tc * 2 + nsl
                ps = pqa if si % 2 == 0 else pqb

                @p.step("tensor", after=(("SV", max(sv0v, sv0v + 2 * (si - 1))),
                                         ("SG", sg_v), ("SD", sd_v)))
                def _(t, tc=tc, nsl=nsl, ps=ps):
                    for kc in range(8):
                        t.matmul(ps[:, 0:512],
                                 xh[:, kc * TKH + tc * 128: kc * TKH + tc * 128 + 128],
                                 wvb[:, kc * D + nsl * 512: kc * D + nsl * 512 + 512],
                                 start=(kc == 0), stop=False)
                    inst = t.matmul(ps[:, 0:512], G["onesf"][0:1, 0:128],
                                    bvr[0:1, nsl * 512:(nsl + 1) * 512],
                                    start=False, stop=True)
                    p.fin(inst, "ST")

                @p.step("vector", waits=("ST",))
                def _(v, tc=tc, nsl=nsl, ps=ps):
                    v.tensor_copy(
                        vcf[:, (tc % 2) * D + nsl * 512:(tc % 2) * D + (nsl + 1) * 512],
                        ps[:, 0:512])
                    inst = v.tensor_copy(
                        vcb[:, (tc % 2) * D + nsl * 512:(tc % 2) * D + (nsl + 1) * 512],
                        ps[:, 0:512])
                    p.fin(inst, "SV", 2)

            @p.step("sync", waits=("SV",))
            def _(s, tc=tc):
                p.dma(s, g["vth"].ap()[tc * 128:(tc + 1) * 128, :],
                      vcb[:, (tc % 2) * D:(tc % 2) * D + D], "SD")
                p.dma(s, g["vtf"].ap()[tc * 128:(tc + 1) * 128, :],
                      vcf[:, (tc % 2) * D:(tc % 2) * D + D], "SD")

    if SUBPH < 2:
        return
    with ExitStack() as c_ab:
        p.barrier()
        trirev_s = p.sb("trirev_s", [128, 128], ctx=c_ab)
        indw_s = p.sb("indw_s", [32, 32 * 128], BF16, ctx=c_ab)
        csumsb = p.sb("csumsb", [32, 64], BF16, ctx=c_ab)
        ohsel_s = p.sb("ohsel_s", [128, 32 * 32], ctx=c_ab)
        e2s = p.sb("e2s", [32, 512], BF16, ctx=c_ab)
        edg_s = p.sb("edg_s", [32, 185], BF16, ctx=c_ab)
        farind_s = p.sb("farind_s", [32, 1], BF16, ctx=c_ab)
        grw = p.sb("grw", [8, 1280], BF16, ctx=c_ab)
        cfr = p.sb("cfr", [1, 8], ctx=c_ab)
        ones32 = p.sb("ones32", [32, 1], ctx=c_ab)
        wev = p.sb("wev", [8, 512], BF16, ctx=c_ab)
        qblk = p.sb("qblk", [128, 8 * 256], BF16, ctx=c_ab)
        kblk = p.sb("kblk", [128, 16 * 256], BF16, ctx=c_ab)
        vb65 = p.sb("vb65", [128, 32 * 65], BF16, ctx=c_ab)
        vload = p.sb("vload", [128, 32 * 64], ctx=c_ab)
        pwin = p.sb("pwin", [128, 32 * 64], BF16, ctx=c_ab)
        ttl = p.sb("ttl", [128, 6 * 512], BF16, ctx=c_ab)
        bsb64 = p.sb("bsb64", [64, 8 * 512], ctx=c_ab)
        csums = p.sb("csums", [32, 64], ctx=c_ab)
        ptrowS = p.sb("ptrowS", [1, 64], ctx=c_ab)
        ptrowB = p.sb("ptrowB", [1, 64], BF16, ctx=c_ab)
        whrs = p.sb("whrs", [1, 512], BF16, ctx=c_ab)
        cfpt = p.sb("cfpt", [64, 1], ctx=c_ab)
        ptl = p.sb("ptlb", [128, 2 * 512], BF16, ctx=c_ab)
        dsb = p.sb("dsb_ab", [1, 2 * 512], ctx=c_ab)
        mgf = p.sb("mgf", [64, 512], ctx=c_ab)
        pnbs = p.sb("pnbs", [64, 512], ctx=c_ab)
        mgb = p.sb("mgb", [64, 512], BF16, ctx=c_ab)
        spsE = p.psum("spsE", [128, 1024], ctx=c_ab)
        spsF = p.psum("spsF", [128, 1024], ctx=c_ab)
        avb = p.psum("avb", [128, 512], ctx=c_ab)
        avb2 = p.psum("avb2", [128, 512], ctx=c_ab)
        ppw = p.psum("ppw", [128, 512], ctx=c_ab)
        pps = ppw

        @p.step("sync")
        def _(s):
            p.dma(s, trirev_s[:], g["trirevf"].ap(), "SD")
            p.dma(s, indw_s[:], g["indwb"].ap(), "SD")
            p.dma(s, e2s[:], g["e2n"].ap(), "SD")
            p.dma(s, edg_s[:], g["edg"].ap(), "SD")
            p.dma(s, farind_s[:], g["farind"].ap(), "SD")

        @p.step("gpsimd")
        def _(gp):
            gp.memset(ohsel_s[:], 0.0)
            gp.memset(ones32[:], 1.0)
            gp.memset(bass.AP(ohsel_s, 0, [[32 * 32, 128], [33, 32]]), 1.0)
            p.fin(gp.memset(grw[:], 0.0), "GC")

        @p.step("tensor", waits=("SD", "SG", "GC"))
        def _(t):
            t.matmul(ppw[0:8, 0:185], G["relbhb"][:], edg_s[:],
                     start=True, stop=True, skip_group_check=True)
            inst = t.matmul(ppw[0:1, 192:200], farind_s[:], G["relbhb"][:],
                            start=True, stop=True, skip_group_check=True)
            p.fin(inst, "ST")

        @p.step("vector", waits=("ST",))
        def _(v):
            v.tensor_copy(grw[0:8, 548:733], ppw[0:8, 0:185])
            inst = v.tensor_copy(cfr[:], ppw[0:1, 192:200])
            p.fin(inst, "SV")

        @p.step("tensor", waits=("SV",))
        def _(t):
            inst = t.matmul(ppw[0:8, 0:512], G["relbhb"][:], e2s[:],
                            start=True, stop=True, skip_group_check=True)
            p.fin(inst, "ST")

        @p.step("vector", waits=("ST",))
        def _(v):
            inst = v.tensor_copy(wev[0:8, :], ppw[0:8, 0:512])
            p.fin(inst, "SV")

        @p.step("sync", waits=("SV",))
        def _(s):
            p.dma(s, g["growh"].ap(), grw[:], "SD")
            p.dma(s, g["whrh"].ap(), wev[:], "SD")

        MSV = {}
        for m in range(8 if SUBPH >= 3 else 0):
            @p.step("sync", waits=("SV", "SS"))
            def _(s, m=m):
                for c in range(8):
                    p.dma(s, qblk[:, c * 256:(c + 1) * 256],
                          g["qth"].ap()[c * 128:(c + 1) * 128, m * 256:(m + 1) * 256],
                          "SD")
                for jkf in range(16):
                    for hb in range(2):
                        p.dma(s, kblk[hb * 64:hb * 64 + 64, jkf * 256:(jkf + 1) * 256],
                              g["kth"].ap()[jkf * 64:(jkf + 1) * 64,
                                            m * 256:(m + 1) * 256], "SD")
                p.dma(s, bass.AP(vload, 0, [[32 * 64, 128], [64, 32], [1, 64]]),
                      bass.AP(g["vtf"], m * 256 * D, [[64, 128], [8192, 32], [1, 64]]),
                      "SD")
                p.dma(s, whrs[:], g["whrh"].ap()[m:m + 1, :], "SD")
                for rr in range(6):
                    p.dma(s, ttl[:, rr * 512:(rr + 1) * 512],
                          bass.AP(g["growh"], m * 1280 + 641 - rr * 128,
                                  [[1, 128], [1, 512]]), "SD")

            @p.step("gpsimd", waits=("SV", "SS"))
            def _(gp, m=m):
                p.fin(gp.memset(vb65[:], 1.0), "GC")

            @p.step("sync", waits=("GC",))
            def _(s, m=m):
                for t2 in range(2):
                    p.dma(s, bass.AP(vb65, t2 * 16 * 65, [[32 * 65, 128], [65, 16], [1, 64]]),
                          bass.AP(g["vth"], (m * 256 + t2 * 128) * D,
                                  [[D, 128], [64, 16], [1, 64]]), "SD")

            @p.step("tensor", waits=("SD", "SV"))
            def _(t, m=m):
                for cc in range(32):
                    inst = t.matmul(pps[0:32, 0:64], ohsel_s[:, cc * 32:(cc + 1) * 32],
                                    vload[:, cc * 64:(cc + 1) * 64],
                                    start=(cc == 0), stop=(cc == 31))
                p.fin(inst, "ST")

            @p.step("vector", waits=("ST",))
            def _(v):
                v.tensor_copy(csums[:], pps[0:32, 0:64])
                inst = v.tensor_copy(csumsb[:], pps[0:32, 0:64])
                p.fin(inst, "SV")

            @p.step("tensor", waits=("SV",))
            def _(t):
                inst = t.matmul(pps[0:1, 64:128], ones32[:], csums[:],
                                start=True, stop=True, skip_group_check=True)
                p.fin(inst, "ST")

            @p.step("vector", waits=("ST",))
            def _(v):
                v.tensor_copy(ptrowS[:], pps[0:1, 64:128])
                inst = v.tensor_copy(ptrowB[:], pps[0:1, 64:128])
                p.fin(inst, "SV")

            @p.step("tensor", waits=("SV",))
            def _(t, m=m):
                inst = t.matmul(pps[0:64, 128:129], ptrowS[:], cfr[0:1, m:m + 1],
                                start=True, stop=True, skip_group_check=True)
                p.fin(inst, "ST")

            @p.step("vector", waits=("ST",))
            def _(v):
                inst = v.tensor_copy(cfpt[:], pps[0:64, 128:129])
                p.fin(inst, "SV")

            for cg in range(4):
                @p.step("tensor", waits=("SV",))
                def _(t, cg=cg):
                    t.matmul(ppw[:, 0:512], trirev_s[:],
                             vload[:, cg * 512:(cg + 1) * 512],
                             start=True, stop=False, skip_group_check=True)
                    for i in range(8):
                        cc = cg * 8 + i
                        inst = t.matmul(ppw[:, i * 64:(i + 1) * 64],
                                        indw_s[:, cc * 128:(cc + 1) * 128],
                                        csumsb[:], start=False, stop=True,
                                        skip_group_check=True)
                    p.fin(inst, "ST")

                @p.step("vector", waits=("ST",))
                def _(v, cg=cg):
                    inst = v.tensor_copy(pwin[:, cg * 512:(cg + 1) * 512], ppw[:, 0:512])
                    p.fin(inst, "SV")

            sv0b = p.cnt["SV"]
            for sub in range(8):
                rrs = [rr for rr in range(6) if 0 <= sub * 4 - 1 + rr <= 31]
                po = (sub % 2) * 64

                @p.step("tensor", after=(("SV", max(sv0b, sv0b + sub - 1)),))
                def _(t, m=m, sub=sub, rrs=rrs, po=po):
                    for i, rr in enumerate(rrs):
                        cc = sub * 4 - 1 + rr
                        last = (i == len(rrs) - 1) and sub != 7
                        inst = t.matmul(G["bps"][po:po + 64, 0:512],
                                        pwin[:, cc * 64:(cc + 1) * 64],
                                        ttl[:, rr * 512:(rr + 1) * 512],
                                        start=(i == 0), stop=last,
                                        tile_position=(0, po),
                                        skip_group_check=True)
                    if sub == 7:
                        inst = t.matmul(G["bps"][po:po + 64, 0:512], ptrowB[:],
                                        whrs[:], start=False, stop=True,
                                        tile_position=(0, po),
                                        skip_group_check=True)
                    p.fin(inst, "ST")

                @p.step("vector", waits=("ST",))
                def _(v, sub=sub, po=po):
                    inst = v.tensor_copy(bsb64[:, sub * 512:(sub + 1) * 512],
                                         G["bps"][po:po + 64, 0:512])
                    p.fin(inst, "SV")

            p.tap(f"pwin{m}", pwin, (128, 32 * 64), BF16)
            p.tap(f"bias{m}", bsb64, (64, 8 * 512), F32)

            blk_sv = p.cnt["SV"]
            for c in range(8 if SUBPH >= 4 else 0):
                ss0c = p.cnt["SS"]
                gp_idx = m * 8 + c
                sv_pass = max(blk_sv, MSV.get(gp_idx - 2, 0))
                avbc = avb if c % 2 == 0 else avb2
                def _emit_av(kc):
                    jkf, jkt2 = kc // 2, kc % 2

                    @p.step("tensor", after=(("SS", ss0c + kc + 1),)
                            + ((("SV", sv_pass),) if kc == 0 else ()))
                    def _(t, kc=kc, jkf=jkf, jkt2=jkt2):
                        inst = t.matmul(avbc[0:65, 0:512],
                                        vb65[:, (jkt2 * 16 + jkf) * 65:
                                             (jkt2 * 16 + jkf) * 65 + 65],
                                        ptl[:, (kc % 2) * 512:(kc % 2) * 512 + 512],
                                        start=(kc == 0), stop=(kc == 31),
                                        skip_group_check=True)
                        p.fin(inst, "ST")

                for kc in range(32):
                    jkf, jkt2 = kc // 2, kc % 2
                    sE = spsE if kc % 2 == 0 else spsF

                    @p.step("tensor", after=(("SS", max(ss0c, ss0c + kc - 1)),
                                             ("SV", sv_pass)))
                    def _(t, m=m, c=c, jkf=jkf, jkt2=jkt2, sE=sE):
                        col0 = jkf * 256 + jkt2 * 128
                        t.matmul(sE[:, 256:512], kblk[0:64, col0:col0 + 128],
                                 qblk[0:64, c * 256:(c + 1) * 256],
                                 start=True, stop=True, tile_position=(0, 0),
                                 skip_group_check=True)
                        inst = t.matmul(sE[:, 512:768], kblk[64:128, col0:col0 + 128],
                                        qblk[64:128, c * 256:(c + 1) * 256],
                                        start=True, stop=True, tile_position=(64, 0),
                                        skip_group_check=True)
                        p.fin(inst, "ST")

                    @p.step("scalar", waits=("ST",))
                    def _(sc, kc=kc, sE=sE):
                        inst = sc.activation(ptl[:, (kc % 2) * 512:(kc % 2) * 512 + 512],
                                             sE[:, 256:768], AF.Exp,
                                             bias=G["csh"][:, 1:2], scale=1.0)
                        p.fin(inst, "SS")

                    if SUBPH >= 5 and kc > 0:
                        _emit_av(kc - 1)
                if SUBPH >= 5:
                    _emit_av(31)

                if SUBPH < 6:
                    continue

                @p.step("vector", waits=("ST",))
                def _(v, avbc=avbc):
                    v.tensor_copy(dsb[0:1, 0:256], avbc[64:65, 0:256])
                    v.tensor_copy(dsb[0:1, 256:512], avbc[64:65, 256:512])
                    v.reciprocal(dsb[0:1, 512:768], dsb[0:1, 0:256])
                    inst = v.reciprocal(dsb[0:1, 768:1024], dsb[0:1, 256:512])
                    p.fin(inst, "SV")

                @p.step("tensor", waits=("SV",))
                def _(t):
                    inst = t.matmul(ppw[0:64, 0:512], G["onesf"][0:1, 0:64],
                                    dsb[0:1, 512:1024], start=True, stop=True,
                                    skip_group_check=True)
                    p.fin(inst, "ST")

                @p.step("vector", waits=("ST", "SD"))
                def _(v, m=m, c=c, avbc=avbc):
                    v.tensor_copy(pnbs[:, 0:256], ppw[0:64, 0:256])
                    v.tensor_copy(pnbs[:, 256:512], ppw[0:64, 256:512])
                    v.tensor_mul(mgf[:, 0:256], avbc[0:64, 0:256], pnbs[:, 0:256])
                    v.tensor_mul(mgf[:, 256:512], avbc[0:64, 256:512],
                                 pnbs[:, 256:512])
                    for h2 in range(2):
                        jqf = 2 * c + h2
                        inst = v.scalar_tensor_tensor(
                            mgb[:, h2 * 256:(h2 + 1) * 256],
                            mgf[:, h2 * 256:(h2 + 1) * 256],
                            cfpt[:],
                            bass.AP(bsb64, jqf, [[8 * 512, 64], [16, 256]]),
                            ALU.add, ALU.add)
                    p.fin(inst, "SV")

                @p.step("sync", waits=("SV",))
                def _(s, m=m, c=c):
                    for h2 in range(2):
                        jqf = 2 * c + h2
                        p.dma(s, g["atth"].ap()[jqf * 64:(jqf + 1) * 64,
                                                m * 256:(m + 1) * 256],
                              mgb[:, h2 * 256:(h2 + 1) * 256], "SD")
                MSV[m * 8 + c] = p.cnt["SV"]

    p.tap("attnraw", None, (D, TK), "hbm:atth")

    with ExitStack() as c_out:
        p.barrier()
        xcast = p.sb("xcast", [128, 8 * TK], BF16, ctx=c_out)
        xcast2 = p.sb("xcast2", [128, 8 * TK], BF16, ctx=c_out)
        wob = p.sb("wob", [128, 2 * D], BF16, ctx=c_out)
        h3st = p.sb("h3st", [128, 2 * 512], BF16, ctx=c_out)
        poa = p.psum("poa", [128, 512], ctx=c_out)
        pob = p.psum("pob", [128, 512], ctx=c_out)

        @p.step("sync")
        def _(s):
            for c in range(8):
                p.dma(s, xh[:, c * TKH: c * TKH + TK],
                      g["atth"].ap()[c * 128:(c + 1) * 128, :], "SD")

        def _proj_inner(wname, bslot, rhs_t, add_t, out_cast, out_hbm):
            st_after = {}
            sv0o = p.cnt["SV"]
            sd0o = p.cnt["SD"]
            si = 0
            for m in range(8):
                @p.step("gpsimd", after=(("ST", st_after.get(m - 2, 0)),))
                def _(gp, m=m, wname=wname):
                    for kc in range(8):
                        p.dma(gp, wob[:, (m % 2) * D + kc * 128:(m % 2) * D + kc * 128 + 128],
                              g[wname].ap()[kc * 128:(kc + 1) * 128, m * 128:(m + 1) * 128],
                              "SG")

                sg_m = p.cnt["SG"]
                for tt in range(4):
                    ps = poa if si % 2 == 0 else pob

                    @p.step("tensor", after=(("SV", max(sv0o, sv0o + 2 * (si - 1))),
                                             ("SG", sg_m), ("SD", sd0o)))
                    def _(t, m=m, tt=tt, ps=ps, rhs_t=rhs_t):
                        for kc in range(8):
                            inst = t.matmul(
                                ps[:, 0:512],
                                wob[:, (m % 2) * D + kc * 128:(m % 2) * D + kc * 128 + 128],
                                rhs_t[:, kc * (TKH if rhs_t is xh else TK) + tt * 512:
                                      kc * (TKH if rhs_t is xh else TK) + tt * 512 + 512],
                                start=(kc == 0), stop=(kc == 7))
                        p.fin(inst, "ST")

                    if add_t is not None and isinstance(add_t, str):
                        @p.step("sync", waits=("SV",))
                        def _(s, m=m, tt=tt, si=si, add_t=add_t):
                            p.dma(s, h3st[:, (si % 2) * 512:(si % 2) * 512 + 512],
                                  g[add_t].ap()[m * 128:(m + 1) * 128, tt * 512:(tt + 1) * 512],
                                  "SD")

                    @p.step("vector", waits=("ST", "SD"))
                    def _(v, m=m, tt=tt, si=si, ps=ps, bslot=bslot, add_t=add_t,
                          out_cast=out_cast):
                        bvec = G["bsb"][:, bslot * 8 + m: bslot * 8 + m + 1]
                        if add_t is None:
                            at, ab = None, 0
                        elif isinstance(add_t, str):
                            at, ab = h3st, (si % 2) * 512
                        elif add_t is G["xh"]:
                            at, ab = add_t, m * TKH + tt * 512
                        else:
                            at, ab = add_t, m * TK + tt * 512
                        for hh in range(2):
                            hs = slice(hh * 256, hh * 256 + 256)
                            if at is not None:
                                v.scalar_tensor_tensor(
                                    stg[:, hs], ps[:, hs], bvec,
                                    at[:, ab + hh * 256: ab + hh * 256 + 256],
                                    ALU.add, ALU.add)
                            else:
                                v.tensor_scalar(stg[:, hs], ps[:, hs],
                                                bvec, None, ALU.add)
                        ob = m * TK + tt * 512
                        v.tensor_copy(out_cast[:, ob:ob + 256], stg[:, 0:256])
                        inst = v.tensor_copy(out_cast[:, ob + 256:ob + 512],
                                             stg[:, 256:512])
                        p.fin(inst, "SV", 2)

                    @p.step("sync", waits=("SV",))
                    def _(s, m=m, tt=tt, out_hbm=out_hbm):
                        if out_hbm is not None:
                            p.dma(s, g[out_hbm].ap()[m * 128:(m + 1) * 128,
                                                     tt * 512:(tt + 1) * 512],
                                  stg[:, 0:512], "SD")
                    si += 1
                st_after[m] = p.cnt["ST"]

        _proj_inner("woT", 3, xh, "h3h", xcast, "s1h")
        if "s1" in DEBUG_TAPS:
            hs1 = p.nc.dram_tensor("tap_s1", [D, TK], F32, kind="ExternalOutput")

            @p.step("sync", waits=("SV", "SS", "ST", "GC"))
            def _(s):
                p.dma(s, hs1.ap(), g["s1h"].ap(), "SD")

        _norm(p, g, G, src=g["s1h"], ncols=TK, precast=True, eps=0.0,
              pcsrc=xcast)
        p.tap("attblk", xh, (128, 8 * TKH), BF16)

        _proj_inner("w1T", 4, xh, None, xcast, None)

        _proj_inner("w2T", 5, xcast, xh, xcast2, "s1h")

        _norm(p, g, G, src=g["s1h"], ncols=TK, precast=True, eps=0.0,
              pcsrc=xcast2)

        for j in range(4):
            c0, c1 = 2 * j, 2 * j + 1

            @p.step("sync", waits=("SV",))
            def _(s, c0=c0, c1=c1):
                p.dma(s, stg[:, 0:TK],
                      g["strmh"].ap()[c0 * 128:(c0 + 1) * 128, :], "SD")
                p.dma(s, stg[:, TKH:TKH + TK],
                      g["strmh"].ap()[c1 * 128:(c1 + 1) * 128, :], "SD")

            @p.step("vector", waits=("SD",))
            def _(v, c0=c0, c1=c1):
                v.tensor_add(stg[:, 0:TK], stg[:, 0:TK],
                             xh[:, c0 * TKH: c0 * TKH + TK])
                v.tensor_add(stg[:, TKH:TKH + TK], stg[:, TKH:TKH + TK],
                             xh[:, c1 * TKH: c1 * TKH + TK])
                v.tensor_copy(xh[:, c0 * TKH: c0 * TKH + TK], stg[:, 0:TK])
                inst = v.tensor_copy(xh[:, c1 * TKH: c1 * TKH + TK],
                                     stg[:, TKH:TKH + TK])
                p.fin(inst, "SV")

            @p.step("sync", waits=("SV",))
            def _(s, c0=c0, c1=c1):
                p.dma(s, g["strmh"].ap()[c0 * 128:(c0 + 1) * 128, :],
                      stg[:, 0:TK], "SD")
                p.dma(s, g["strmh"].ap()[c1 * 128:(c1 + 1) * 128, :],
                      stg[:, TKH:TKH + TK], "SD")
    p.tap("x3", None, (D, TK), "hbm:strmh")



